# version 17
# baseline (speedup 1.0000x reference)
import math
import contextlib
import numpy as np
import ml_dtypes
import concourse.bass as bass
import concourse.mybir as mybir
from concourse.bass_utils import run_bass_kernel_spmd

F32 = mybir.dt.float32
BF16 = mybir.dt.bfloat16
AF = mybir.ActivationFunctionType
ALU = mybir.AluOpType
AX = mybir.AxisListType

B, S, D = 2, 8192, 2048
H, DH = 16, 128
T = 2048
GROUPS = [[0, 1, 2, 3], [4, 5, 6, 7]]
SCALE = 1.0 / math.sqrt(128.0)
EPS = 1e-5
NTB, NDC, NQB, NPAIR = 4, 16, 4, 32
NG = H * NQB * NPAIR
NEP = H * NQB
ALPHA_SCALING = 2.0

TAPS = set()
PHASES = "ABC"

_NC_CACHE = {}


def _consts():
    Rm = np.zeros((128, 128), np.float32)
    for d in range(64):
        Rm[d, d + 64] = -1.0
    for d in range(64, 128):
        Rm[d, d - 64] = 1.0
    RT = np.ascontiguousarray(Rm.T).astype(ml_dtypes.bfloat16)
    ident = np.eye(128, dtype=ml_dtypes.bfloat16)
    ones = np.ones((128, 1), dtype=ml_dtypes.bfloat16)
    onesrow = np.ones((1, 128), dtype=ml_dtypes.bfloat16)
    E2 = np.zeros((8, 128), dtype=ml_dtypes.bfloat16)
    for e in range(8):
        E2[e, 16 * e:16 * e + 16] = ALPHA_SCALING
    return RT, ident, ones, onesrow, E2


def build_program():
    nc = bass.Bass("TRN2", target_bir_lowering=False, debug=False)

    xh = nc.dram_tensor("xh", [128, NDC, T], F32, kind="ExternalInput").ap()
    wq5 = nc.dram_tensor("wq5", [128, 48, NDC, 128], F32, kind="ExternalInput").ap()
    wo5 = nc.dram_tensor("wo5", [128, 16, 16, 128], F32, kind="ExternalInput").ap()
    wb5 = nc.dram_tensor("wb5", [128, 16, NDC, 128], F32, kind="ExternalInput").ap()
    wa5 = nc.dram_tensor("wa5", [128, NDC, 128], F32, kind="ExternalInput").ap()
    wbw = nc.dram_tensor("wbw", [128, 2048], F32, kind="ExternalInput").ap()
    rw5 = nc.dram_tensor("rw5", [128, NDC, 8], F32, kind="ExternalInput").ap()
    bbi = nc.dram_tensor("bbi", [128, 16], F32, kind="ExternalInput").ap()
    cosi = nc.dram_tensor("cosi", [128, T], F32, kind="ExternalInput").ap()
    sini = nc.dram_tensor("sini", [128, T], F32, kind="ExternalInput").ap()
    outT = nc.dram_tensor("outT", [D, T], F32, kind="ExternalOutput").ap()

    cc_inK = nc.dram_tensor("cc_inK", [D, T], BF16, kind="Internal").ap()
    cc_outK = nc.dram_tensor("cc_outK", [4 * D, T], BF16, kind="Internal").ap()
    cc_inV = nc.dram_tensor("cc_inV", [4 * T, 512], BF16, kind="Internal").ap()
    cc_outV = nc.dram_tensor("cc_outV", [16 * T, 512], BF16, kind="Internal").ap()
    attd = nc.dram_tensor("attd", [D, T], BF16, kind="Internal").ap()

    RTc, IDc, ONESc, ONESRc, E2c = _consts()

    taps = {}
    for name, shape, dt in [("xn", [128, NDC * T], BF16), ("q", [128, H * T], BF16),
                            ("cc", [2 * D, T], BF16), ("att", [D, T], BF16),
                            ("x1", [128, 16 * T], BF16), ("xn2", [128, NDC * T], BF16),
                            ("hm", [128, T], BF16)]:
        if name in TAPS:
            taps[name] = nc.dram_tensor("t_" + name, shape, dt, kind="ExternalOutput").ap()
    n_tap = len(taps)
    n_tap_ab = sum(1 for nm in ("xn", "q", "cc", "att") if nm in TAPS)

    g = contextlib.ExitStack()
    EG = g.enter_context

    bufB = EG(nc.sbuf_tensor("bufB", [128, H * T], BF16))
    rtC = EG(nc.sbuf_tensor("rtC", [128, 128], BF16))
    idC = EG(nc.sbuf_tensor("idC", [128, 128], BF16))
    onC = EG(nc.sbuf_tensor("onC", [128, 1], BF16))
    onR = EG(nc.sbuf_tensor("onR", [1, 128], BF16))
    e2C = EG(nc.sbuf_tensor("e2C", [8, 128], BF16))
    wab = EG(nc.sbuf_tensor("wab", [128, 2048], BF16))
    wbwb = EG(nc.sbuf_tensor("wbwb", [128, 2048], BF16))
    rwb = EG(nc.sbuf_tensor("rwb", [128, NDC * 8], BF16))
    bbb = EG(nc.sbuf_tensor("bbb", [128, 16], F32))
    epsC = EG(nc.sbuf_tensor("epsC", [1, 1], F32))
    wbuf = [EG(nc.sbuf_tensor(f"wbuf{i}", [128, 2048], BF16)) for i in range(3)]
    xbuf = [EG(nc.sbuf_tensor(f"xbuf{i}", [128, 2048], F32)) for i in range(2)]
    abR = EG(nc.sbuf_tensor("abR", [1, T], BF16))
    bR = EG(nc.sbuf_tensor("bR", [1, T], BF16))
    stm_ = EG(nc.sbuf_tensor("stm_", [1, 512], F32))
    stq_ = EG(nc.sbuf_tensor("stq_", [1, 512], F32))
    stv_ = EG(nc.sbuf_tensor("stv_", [1, 512], F32))

    SN = ["sCN", "sDV", "sSQ", "sCB", "sXC", "sXS", "sST", "sVAR", "sSQT",
          "sSC", "sBC", "sXN", "sQm", "sQf", "sKR", "sRm", "sRf", "sKV", "sCI",
          "sCC", "sPH", "sVT", "sVTf", "sVR", "sS", "sE", "sPV", "sOn", "sOf",
          "sPT", "sPTf", "sAS", "sAL", "sYm", "sYf", "sX1", "sC1", "sLG",
          "sMsk", "sMT", "sMTf", "sEm", "sM2", "sHm", "sHr", "sBm", "sBf",
          "sOS", "sOD", "sXN2", "sRt", "sOnb", "sBCe", "sVA", "sVM", "sVB"]
    sem = {nm: EG(nc.semaphore(nm)) for nm in SN}
    sXA = [EG(nc.semaphore(f"sXA{i}")) for i in range(4)]
    sWL = [EG(nc.semaphore(f"sWL{i}")) for i in range(3)]
    sWF = [EG(nc.semaphore(f"sWF{i}")) for i in range(2)]
    sKL = [EG(nc.semaphore(f"sKL{i}")) for i in range(2)]
    sKF = [EG(nc.semaphore(f"sKF{i}")) for i in range(2)]
    sVL = [EG(nc.semaphore(f"sVL{i}")) for i in range(2)]
    sVG = [EG(nc.semaphore(f"sVG{i}")) for i in range(2)]
    sVF = [EG(nc.semaphore(f"sVF{i}")) for i in range(2)]
    sCIK = [EG(nc.semaphore(f"sCIK{i}")) for i in range(2)]
    sCIV = [EG(nc.semaphore(f"sCIV{i}")) for i in range(2)]
    sXW = [EG(nc.semaphore(f"sXW{i}")) for i in range(2)]
    sWC = [EG(nc.semaphore(f"sWC{i}")) for i in range(3)]
    sASs = [EG(nc.semaphore(f"sASs{i}")) for i in range(2)]
    sODs = [EG(nc.semaphore(f"sODs{i}")) for i in range(2)]

    block = EG(nc.Block())

    chunk_order = []
    for cg in range(4):
        chunk_order += [16 + 4 * cg + i for i in range(4)]
        chunk_order += [32 + 4 * cg + i for i in range(4)]
    chunk_order += list(range(16))
    gp_info, rope_gps, contrib_gps = [], [], []
    for oci, oc in enumerate(chunk_order):
        kind = "K" if 16 <= oc < 32 else ("V" if oc >= 32 else "Q")
        for pr in range(2):
            gp_info.append((oc, pr, kind))
            gp = oci * 2 + pr
            if kind in ("K", "Q"):
                rope_gps.append(gp)
            if kind in ("K", "V"):
                contrib_gps.append(gp)
    rope_idx = {gp: i for i, gp in enumerate(rope_gps)}
    contrib_idx = {gp: i for i, gp in enumerate(contrib_gps)}
    _kd = [0, 0]
    _vc = 0
    vci_of = {}
    _rel = {}
    for _ci, _gp in enumerate(contrib_gps):
        if gp_info[_gp][2] == "K":
            _kd[_ci % 2] += 1
            _rel[_ci] = ("K", _ci % 2, 16 * _kd[_ci % 2])
        else:
            vci_of[_gp] = _vc
            _rel[_ci] = ("VD", 16 * (_vc + 1))
            _vc += 1
    kv_release = {ci: _rel[ci - 2] for ci in range(2, 64)}
    pairs = [(h, qb, lp) for h in range(H) for qb in range(NQB) for lp in range(NPAIR)]

    def bar(eng, which):
        eng.sem_inc(sem["sPH"], 1)
        eng.wait_ge(sem["sPH"], 5 * which)

    with g:
        actx = contextlib.ExitStack()
        EA = actx.enter_context
        xnA = EA(nc.sbuf_tensor("xnA", [128, NDC * T], BF16))
        xcb = [EA(nc.sbuf_tensor(f"xcb{i}", [128, 512], BF16)) for i in range(2)]
        xsq = [EA(nc.sbuf_tensor(f"xsq{i}", [128, 512], BF16)) for i in range(2)]
        krawB = EA(nc.sbuf_tensor("krawB", [128, 1024], BF16))
        s1b = EA(nc.sbuf_tensor("s1b", [128, 1024], BF16))
        s2b = EA(nc.sbuf_tensor("s2b", [128, 1024], BF16))
        xt1 = [EA(nc.sbuf_tensor(f"xt1_{i}", [128, 512], BF16)) for i in range(2)]
        kvst = [EA(nc.sbuf_tensor(f"kvst{i}", [128, 1024], BF16)) for i in range(2)]
        abc = EA(nc.sbuf_tensor("abc", [128, 512], BF16))
        bbc = EA(nc.sbuf_tensor("bbc", [128, 512], BF16))
        cosT = EA(nc.sbuf_tensor("cosT", [128, T], BF16))
        sinT = EA(nc.sbuf_tensor("sinT", [128, T], BF16))
        psMix = EA(nc.psum_tensor("psMix", [128, 1024], F32))
        psQ = [EA(nc.psum_tensor(f"psQ{i}", [128, 1024], F32)) for i in range(2)]
        _sctx = contextlib.ExitStack()
        psAB = _sctx.enter_context(nc.psum_tensor("psAB", [1, 1024], F32))
        _sctx.close()
        psVtA = [EA(nc.psum_tensor("psVtA0", [128, 1024], BF16))]

        NCN = 12 * 16
        marks = {"xn_tb": []}
        xa = [0, 0]
        xa_dma, xa_t1m = {}, {}
        for u in range(32):
            sl = u % 2
            xa[sl] += 16
            xa_dma[u] = xa[sl]
            if u >= 16:
                for j in range(4):
                    xa[sl] += 8
                    xa_t1m[(u, j)] = xa[sl]
        xa_end = list(xa)
        for mt in range(16):
            sl = mt % 2
            xa[sl] += 16
            xa_dma[("C", mt)] = xa[sl]
        marks["xa_dma"] = xa_dma
        marks["xa_t1m"] = xa_t1m
        marks["xa_end"] = xa_end
        dvn = [0]
        N0W = 14

        def _is_cast(_o):
            return N0W <= _o < 48 and _o % 2 == 1

        _wl = [0, 0, 0]
        _wc = [0, 0, 0]
        wq_wait = {}
        for _o in range(80):
            if _is_cast(_o):
                _wc[_o % 3] += 1
                wq_wait[_o] = ("C", _wc[_o % 3])
            else:
                _wl[_o % 3] += 16
                wq_wait[_o] = ("L", _wl[_o % 3])
        cast_chunks = [c for c in range(48) if _is_cast(c)]
        cast_pos = {c: i for i, c in enumerate(cast_chunks)}
        stage_xw = {}

        def mkDV(v):
            def DV(inst):
                inst.then_inc(sem["sDV"], 1)
                dvn[0] += 1
                return inst
            return DV

        @block.gpsimd
        def _(gq):
            for dst, src in [(cosT, cosi[:, :]), (sinT, sini[:, :]),
                             (rtC, nc.inline_tensor(RTc, "c_rt").ap()),
                             (idC, nc.inline_tensor(IDc, "c_id").ap()),
                             (onC, nc.inline_tensor(ONESc, "c_on").ap()),
                             (onR, nc.inline_tensor(ONESRc, "c_or").ap()),
                             (e2C, nc.inline_tensor(E2c, "c_e2").ap()),
                             (wab, wa5.rearrange("p dc er -> p (dc er)")),
                             (wbwb, wbw[:, :]),
                             (rwb, rw5.rearrange("p dc e -> p (dc e)")),
                             (bbb, bbi[:, :]),
                             (epsC, nc.inline_tensor(np.full((1, 1), EPS, np.float32), "c_eps").ap())]:
                gq.dma_start(out=dst[:], in_=src).then_inc(sem["sCN"], 16)
            if PHASES == "A0":
                return
            agk_after = {8: 0}
            agv_after = {14: 0}
            for oci in range(48):
                if _is_cast(oci):
                    continue
                oc = chunk_order[oci]
                slot = oci % 3
                if oci >= 3:
                    gq.wait_ge(sem["sQm"], 2 * (oci - 2))
                gq.dma_start(out=wbuf[slot][:],
                             in_=wq5[:, oc, :, :].rearrange("p dc o -> p (dc o)")
                             ).then_inc(sWL[slot], 16)
                if oci in agk_after:
                    cg = agk_after[oci]
                    gq.wait_ge(sCIK[0], 64 * (cg + 1))
                    gq.wait_ge(sCIK[1], 64 * (cg + 1))
                    gq.collective_compute(
                        "AllGather", ALU.bypass, replica_groups=GROUPS,
                        ins=[cc_inK[512 * cg:512 * cg + 512, :].opt()],
                        outs=[cc_outK[2048 * cg:2048 * cg + 2048, :].opt()]
                    ).then_inc(sem["sCC"], 1)
                if oci in agv_after:
                    cg = agv_after[oci]
                    gq.wait_ge(sCIV[0], 16 * 8 * (cg + 1))
                    gq.collective_compute(
                        "AllGather", ALU.bypass, replica_groups=GROUPS,
                        ins=[cc_inV[2048 * cg:2048 * cg + 2048, :].opt()],
                        outs=[cc_outV[8192 * cg:8192 * cg + 8192, :].opt()]
                    ).then_inc(sem["sCC"], 1)
            for cg in range(1, 4):
                gq.wait_ge(sCIK[0], 64 * (cg + 1))
                gq.wait_ge(sCIK[1], 64 * (cg + 1))
                gq.collective_compute(
                    "AllGather", ALU.bypass, replica_groups=GROUPS,
                    ins=[cc_inK[512 * cg:512 * cg + 512, :].opt()],
                    outs=[cc_outK[2048 * cg:2048 * cg + 2048, :].opt()]
                ).then_inc(sem["sCC"], 1)
                gq.wait_ge(sCIV[0], 16 * 8 * (cg + 1))
                gq.collective_compute(
                    "AllGather", ALU.bypass, replica_groups=GROUPS,
                    ins=[cc_inV[2048 * cg:2048 * cg + 2048, :].opt()],
                    outs=[cc_outV[8192 * cg:8192 * cg + 8192, :].opt()]
                ).then_inc(sem["sCC"], 1)


        @block.sync
        def _(sy):
            for p in range(2):
                for tb in range(NTB):
                    for qd in range(4):
                        u = p * 16 + tb * 4 + qd
                        slot = u % 2
                        if 2 <= u < 18:
                            sy.wait_ge(sem["sXC"], 4 * (u - 1))
                            sy.wait_ge(sem["sXS"], 4 * (u - 1))
                        elif u >= 18:
                            sy.wait_ge(sXA[slot], marks["xa_t1m"][(u - 2, 3)])
                        sy.dma_start(out=xbuf[slot][:].rearrange("p (a b) -> p a b", a=4, b=512),
                                     in_=xh[:, 4 * qd:4 * qd + 4, 512 * tb:512 * tb + 512]
                                     ).then_inc(sXA[slot], 16)

            if PHASES == "A0":
                return
            sxw = [0, 0]
            for j in range(48):
                c = j + 2
                if _is_cast(c):
                    ip = cast_pos[c]
                    xs = ip % 2
                    if ip < 2:
                        sy.wait_ge(sXA[xs], marks["xa_t1m"][(30 + xs, 3)])
                    else:
                        pc = cast_chunks[ip - 2]
                        sy.wait_ge(sWC[pc % 3], wq_wait[pc][1])
                    sy.dma_start(out=xbuf[xs][:],
                                 in_=wq5[:, chunk_order[c], :, :]
                                 .rearrange("p dc o -> p (dc o)")
                                 ).then_inc(sXW[xs], 16)
                    sxw[xs] += 16
                    stage_xw[c] = sxw[xs]
                if j > 31:
                    continue
                for ci in (2 * j, 2 * j + 1):
                    gp = contrib_gps[ci]
                    oc, pr, kind = gp_info[gp]
                    slot = ci % 2
                    if kind == "K":
                        sy.wait_ge(sem["sKV"], ci + 1)
                        hh = oc - 16
                        row = 512 * (hh // 4) + 128 * (hh % 4)
                        sy.dma_start(out=cc_inK[row:row + 128,
                                                1024 * pr:1024 * pr + 1024],
                                     in_=kvst[slot][:]).then_inc(sCIK[slot], 16)
                    else:
                        vci = vci_of[gp]
                        sy.wait_ge(sem["sVB"], vci + 1)
                        hh = oc - 32
                        dst = cc_inV[2048 * (hh // 4) + 1024 * pr:
                                     2048 * (hh // 4) + 1024 * pr + 1024,
                                     128 * (hh % 4):128 * (hh % 4) + 128]
                        sy.dma_start(out=dst.rearrange("(kt p) d -> p kt d", p=128),
                                     in_=kvst[slot][:]
                                     .rearrange("p (kt d) -> p kt d", d=128)
                                     ).then_inc(sCIV[0], 16)

        @block.scalar
        def _(sc):
            for tb in range(NTB):
                for dc in range(NDC):
                    idx = tb * NDC + dc
                    u = tb * 4 + dc // 4
                    slot = u % 2
                    sc.wait_ge(sXA[slot], marks["xa_dma"][u])
                    if idx >= 2:
                        sc.wait_ge(sem["sXS"], idx - 1)
                    sc.activation(out=xsq[idx % 2][:],
                                  in_=xbuf[slot][:, 512 * (dc % 4):512 * (dc % 4) + 512],
                                  func=AF.Square).then_inc(sem["sSQ"], 1)
                sc.wait_ge(sem["sVAR"], tb + 1)
                if tb >= 1:
                    sc.wait_ge(sem["sSC"], tb)
                sc.activation(out=stq_[:], in_=stv_[:], func=AF.Sqrt,
                              bias=epsC[:]).then_inc(sem["sSQT"], 1)

        @block.vector
        def _(v):
            DV = mkDV(v)
            for tb in range(NTB):
                for dc in range(NDC):
                    idx = tb * NDC + dc
                    u = tb * 4 + dc // 4
                    slot = u % 2
                    v.wait_ge(sXA[slot], marks["xa_dma"][u])
                    if idx >= 2:
                        v.wait_ge(sem["sXC"], idx - 1)
                    v.tensor_copy(xcb[idx % 2][:],
                                  xbuf[slot][:, 512 * (dc % 4):512 * (dc % 4) + 512]
                                  ).then_inc(sem["sCB"], 1)
                v.wait_ge(sem["sXS"], 16 * (tb + 1))
                DV(v.tensor_scalar(stm_[:], psAB[0:1, 0:512], 1.0 / 2048.0, None, ALU.mult))
                DV(v.tensor_scalar(stq_[:], psAB[0:1, 512:1024], 1.0 / 2048.0, None, ALU.mult))
                v.wait_ge(sem["sDV"], dvn[0])
                if tb >= 1:
                    v.wait_ge(sem["sSQT"], tb)
                DV(v.tensor_tensor(stv_[:], stm_[:], stm_[:], ALU.mult))
                v.wait_ge(sem["sDV"], dvn[0])
                v.tensor_tensor(stv_[:], stq_[:], stv_[:], ALU.subtract
                                ).then_inc(sem["sVAR"], 1)
                v.wait_ge(sem["sSQT"], tb + 1)
                DV(v.reciprocal(stv_[:], stq_[:]))
                v.wait_ge(sem["sDV"], dvn[0])
                DV(v.tensor_copy(abR[0:1, 512 * tb:512 * tb + 512], stv_[:]))
                v.wait_ge(sem["sDV"], dvn[0])
                v.scalar_tensor_tensor(bR[0:1, 512 * tb:512 * tb + 512], stm_[:], -1.0,
                                       abR[0:1, 512 * tb:512 * tb + 512],
                                       ALU.mult, ALU.mult).then_inc(sem["sSC"], 1)
            for tb in range(NTB):
                v.wait_ge(sem["sBC"], 2 * (tb + 1))
                v.wait_ge(sem["sDV"], dvn[0])
                DV(v.tensor_copy(abc[:], psMix[:, 0:512]))
                v.tensor_copy(bbc[:], psMix[:, 512:1024]).then_inc(sem["sBCe"], 1)
                v.wait_ge(sem["sDV"], dvn[0])
                v.wait_ge(sem["sBCe"], tb + 1)
                for dc in range(NDC):
                    idx = 64 + tb * NDC + dc
                    u = 16 + tb * 4 + dc // 4
                    slot = u % 2
                    v.wait_ge(sXA[slot], marks["xa_dma"][u])
                    t1 = xt1[dc % 2]
                    if dc >= 2 or tb >= 1:
                        v.wait_ge(sem["sDV"], dvn[0] - 1)
                    v.tensor_tensor(t1[:],
                                    xbuf[slot][:, 512 * (dc % 4):512 * (dc % 4) + 512],
                                    abc[:], ALU.mult).then_inc(sXA[slot], 8)
                    v.wait_ge(sXA[slot], marks["xa_t1m"][(u, dc % 4)])
                    DV(v.tensor_tensor(
                        xnA[:, 2048 * dc + 512 * tb:2048 * dc + 512 * tb + 512],
                        t1[:], bbc[:], ALU.add))
                marks["xn_tb"].append(dvn[0])
            for gp in (range(96) if PHASES != "A0" else []):
                if gp >= 2 and (gp - 2) in vci_of:
                    vci2 = vci_of[gp - 2]
                    v.wait_ge(sem["sVA"], vci2 + 1)
                    v.tensor_copy(kvst[contrib_idx[gp - 2] % 2][:], psVtA[0][:]
                                  ).then_inc(sem["sVB"], 1)
                if gp % 2 == 1:
                    c = gp // 2 + 2
                    if _is_cast(c):
                        ws = c % 3
                        xs = cast_pos[c] % 2
                        v.wait_ge(sem["sQm"], 2 * (c - 2))
                        v.wait_ge(sXW[xs], stage_xw[c])
                        v.tensor_copy(wbuf[ws][:], xbuf[xs][:]
                                      ).then_inc(sWC[ws], 1)
                oc, pr, kind = gp_info[gp]
                v.wait_ge(sem["sQm"], gp + 1)
                if gp >= 1:
                    v.wait_ge(sem["sKV"], gp)
                if kind == "V":
                    ci = contrib_idx[gp]
                    slot = ci % 2
                    if ci >= 2:
                        rel = kv_release[ci]
                        if rel[0] == "K":
                            v.wait_ge(sCIK[rel[1]], rel[2])
                        else:
                            v.wait_ge(sCIV[0], rel[1])
                    v.tensor_copy(kvst[slot][:], psQ[gp % 2][:]).then_inc(sem["sKV"], 1)
                else:
                    ri = rope_idx[gp]
                    if ri >= 1:
                        v.wait_ge(sem["sRm"], ri)
                    v.tensor_copy(krawB[:], psQ[gp % 2][:]).then_inc(sem["sKR"], 1)
                    DV(v.tensor_tensor(s1b[:], psQ[gp % 2][:],
                                       cosT[:, 1024 * pr:1024 * pr + 1024], ALU.mult))
                    v.wait_ge(sem["sRm"], ri + 1)
                    v.tensor_tensor(s2b[:], psMix[:],
                                    sinT[:, 1024 * pr:1024 * pr + 1024], ALU.mult
                                    ).then_inc(sem["sRf"], 1)
                    v.wait_ge(sem["sDV"], dvn[0])
                    v.wait_ge(sem["sRf"], ri + 1)
                    if kind == "K":
                        ci = contrib_idx[gp]
                        slot = ci % 2
                        if ci >= 2:
                            rel = kv_release[ci]
                            if rel[0] == "K":
                                v.wait_ge(sCIK[rel[1]], rel[2])
                            else:
                                v.wait_ge(sCIV[0], rel[1])
                        dst = kvst[slot][:]
                    else:
                        dst = bufB[:, 2048 * oc + 1024 * pr:2048 * oc + 1024 * pr + 1024]
                    v.tensor_tensor(dst, s1b[:], s2b[:], ALU.add).then_inc(sem["sKV"], 1)

        @block.tensor
        def _(t):
            t.wait_ge(sem["sCN"], NCN)

            def stats_tb(tb):
                if tb >= 1:
                    t.wait_ge(sem["sSC"], tb)
                for dc in range(NDC):
                    idx = tb * NDC + dc
                    t.wait_ge(sem["sCB"], idx + 1)
                    t.wait_ge(sem["sSQ"], idx + 1)
                    t.matmul(psAB[0:1, 0:512], onC[:], xcb[idx % 2][:],
                             start=(dc == 0), stop=(dc == 15),
                             skip_group_check=True).then_inc(sem["sXC"], 1)
                    t.matmul(psAB[0:1, 512:1024], onC[:], xsq[idx % 2][:],
                             start=(dc == 0), stop=(dc == 15),
                             skip_group_check=True).then_inc(sem["sXS"], 1)

            def bcast_tb(tb):
                t.wait_ge(sem["sSC"], tb + 1)
                if tb >= 1:
                    t.wait_ge(sem["sBCe"], tb)
                t.matmul(psMix[:, 0:512], onR[:], abR[0:1, 512 * tb:512 * tb + 512],
                         start=True, stop=True)
                t.matmul(psMix[:, 512:1024], onR[:], bR[0:1, 512 * tb:512 * tb + 512],
                         start=True, stop=True).then_inc(sem["sBC"], 2)

            for tb in range(NTB):
                stats_tb(tb)
            for tb in range(NTB):
                bcast_tb(tb)
            if PHASES == "A0":
                return
            pend_rope = []
            pend_v = []

            def emit_vtr(vgp):
                vci = vci_of[vgp]
                t.wait_ge(sem["sKV"], vgp + 1)
                if vci >= 1:
                    t.wait_ge(sem["sVB"], vci)
                kslot = contrib_idx[vgp] % 2
                for i8 in range(8):
                    tr = t.transpose(psVtA[0][:, 128 * i8:128 * i8 + 128],
                                     kvst[kslot][:, 128 * i8:128 * i8 + 128], idC[:])
                    if i8 == 7:
                        tr.then_inc(sem["sVA"], 1)

            def emit_rot(rgp):
                ri = rope_idx[rgp]
                t.wait_ge(sem["sKR"], ri + 1)
                if ri >= 1:
                    t.wait_ge(sem["sRf"], ri)
                t.matmul(psMix[:, 0:512], rtC[:], krawB[:, 0:512], start=True, stop=True)
                t.matmul(psMix[:, 512:1024], rtC[:], krawB[:, 512:1024],
                         start=True, stop=True).then_inc(sem["sRm"], 1)

            for oci, oc in enumerate(chunk_order):
                slot = oci % 3
                _tag, _val = wq_wait[oci]
                t.wait_ge(sWL[slot] if _tag == "L" else sWC[slot], _val)
                for pr in range(2):
                    gp = oci * 2 + pr
                    if oci == 0:
                        t.wait_ge(sem["sDV"], marks["xn_tb"][2 * pr + 1])
                    if gp >= 2:
                        t.wait_ge(sem["sKV"], gp - 1)
                    for tin in range(2):
                        tcol = 1024 * pr + 512 * tin
                        for dc in range(NDC):
                            mm = t.matmul(psQ[gp % 2][:, 512 * tin:512 * tin + 512],
                                          wbuf[slot][:, 128 * dc:128 * dc + 128],
                                          xnA[:, 2048 * dc + tcol:2048 * dc + tcol + 512],
                                          start=(dc == 0), stop=(dc == 15))
                            if tin == 1 and dc == NDC - 1:
                                mm.then_inc(sem["sQm"], 1)
                    if pend_rope:
                        emit_rot(pend_rope.pop(0))
                    if pend_v:
                        emit_vtr(pend_v.pop(0))
                    if gp in rope_idx:
                        pend_rope.append(gp)
                    if gp in vci_of:
                        pend_v.append(gp)
            while pend_rope:
                emit_rot(pend_rope.pop(0))
            while pend_v:
                emit_vtr(pend_v.pop(0))

        tapsA = (1 if "xn" in TAPS else 0)
        if PHASES != "A0":
            tapsA += (1 if "q" in TAPS else 0)

        @block.sync
        def _(sy):
            if "xn" in TAPS:
                sy.wait_ge(sem["sDV"], marks["xn_tb"][3])
                sy.dma_start(out=taps["xn"][:, :], in_=xnA[:]).then_inc(sem["sOD"], 16)
            if "q" in TAPS and PHASES != "A0":
                sy.wait_ge(sem["sKV"], 96)
                sy.dma_start(out=taps["q"][:, :], in_=bufB[:]).then_inc(sem["sOD"], 16)
            bar(sy, 1)
        for engname in ["gpsimd", "scalar", "vector", "tensor"]:
            getattr(block, engname)(lambda e: bar(e, 1))

        actx.close()

        if PHASES in ("A", "A0"):
            @block.sync
            def _(sy):
                sy.wait_ge(sem["sOD"], 16 * tapsA)
            return nc

        bctx = contextlib.ExitStack()
        EB = bctx.enter_context
        kbuf = [EB(nc.sbuf_tensor(f"kbuf{i}", [128, 8192], BF16)) for i in range(2)]
        vone = [EB(nc.sbuf_tensor(f"vone{i}", [128, 64 * 129], BF16)) for i in range(2)]
        ptb = [EB(nc.sbuf_tensor(f"ptb{i}", [128, 512], BF16)) for i in range(4)]
        onb = EB(nc.sbuf_tensor("onb", [128, 512], BF16))
        orawA = EB(nc.sbuf_tensor("orawA", [128, 258], F32))
        orawB = EB(nc.sbuf_tensor("orawB", [128, 258], F32))
        rlb = EB(nc.sbuf_tensor("rlb", [128, 4], F32))
        attst = [EB(nc.sbuf_tensor(f"attst{i}", [128, 2048], BF16)) for i in range(2)]
        psS = [EB(nc.psum_tensor(f"psS{i}", [128, 512], F32)) for i in range(4)]
        psOA = EB(nc.psum_tensor("psOA", [128, 258], F32))
        psOB = EB(nc.psum_tensor("psOB", [128, 258], F32))
        psT = EB(nc.psum_tensor("psT", [128, 512], BF16))

        @block.sync
        def _(sy):
            sy.wait_ge(sem["sKV"], 96)
            sy.wait_ge(sem["sRm"], 64)
            sy.wait_ge(sem["sSQT"], 4)

            def k_load(h):
                slot = h % 2
                sy.wait_ge(sem["sCC"], 2 * (h // 4) + 1)
                if h >= 2:
                    sy.wait_ge(sem["sS"], 256 * (h - 1))
                base = 2048 * (h // 4)
                for r in range(4):
                    sy.dma_start(out=kbuf[slot][:, 2048 * r:2048 * r + 2048],
                                 in_=cc_outK[base + 512 * r + 128 * (h % 4):
                                             base + 512 * r + 128 * (h % 4) + 128, :]
                                 ).then_inc(sKL[slot], 16)

            def v_load(ht):
                slot = ht % 2
                sy.wait_ge(sem["sCC"], 2 * (ht // 4) + 2)
                sy.wait_ge(sem["sVM"], slot + 1)
                if ht >= 2:
                    sy.wait_ge(sem["sPV"], 256 * (ht - 1))
                g2 = ht // 4
                dstv = vone[slot][:].rearrange("p (kt c) -> p kt c", kt=64, c=129)
                sy.dma_start(out=dstv[:, 0:32, 0:128],
                             in_=cc_outV[8192 * g2:8192 * g2 + 4096,
                                         128 * (ht % 4):128 * (ht % 4) + 128]
                             .rearrange("(kt p) d -> p kt d", p=128)
                             ).then_inc(sVL[slot], 16)

            k_load(0)
            k_load(1)
            v_load(0)
            for h in range(H):
                if h + 1 < H:
                    v_load(h + 1)
                if h + 2 < H:
                    k_load(h + 2)
                sy.wait_ge(sem["sPTf"], 4 * (h + 1))
                sy.dma_start(out=attd[128 * h:128 * h + 128, :],
                             in_=attst[h % 2][:]).then_inc(sASs[h % 2], 16)

        @block.scalar
        def _(sc):
            NK = H * NQB * 64
            for k in range(NK):
                sc.wait_ge(sem["sS"], k + 1)
                if k >= 4:
                    sc.wait_ge(sem["sPV"], k - 3)
                sc.activation(out=ptb[k % 4][:], in_=psS[k % 4][:],
                              func=AF.Exp, scale=SCALE).then_inc(sem["sE"], 1)

        @block.tensor
        def _(t):
            NK = H * NQB * 64
            LAG = 3
            kdesc = [(h, qb, lk) for h in range(H) for qb in range(NQB)
                     for lk in range(64)]
            ep_of_k = [h * NQB + qb for (h, qb, lk) in kdesc]
            for ki in range(NK + LAG):
                if ki >= LAG:
                    k = ki - LAG
                    e_h, e_qb, e_lk = kdesc[k]
                    ep = e_h * NQB + e_qb
                    if e_lk == 0 and e_qb == 0:
                        t.wait_ge(sVL[e_h % 2], 16 * (e_h // 2 + 1))
                        t.wait_ge(sVG[e_h % 2], 16 * (e_h // 2 + 1))
                    if e_lk == 0 and ep >= 1:
                        t.wait_ge(sem["sOn"], ep)
                    t.wait_ge(sem["sE"], k + 1)
                    for qt in range(4):
                        po = psOA if qt < 2 else psOB
                        col = 129 * (qt % 2)
                        mm = t.matmul(po[:, col:col + 129],
                                      ptb[k % 4][:, 128 * qt:128 * qt + 128],
                                      vone[e_h % 2][:, 129 * e_lk:129 * e_lk + 129],
                                      start=(e_lk == 0 and qt in (0, 2)),
                                      stop=(e_lk == 63), skip_group_check=True)
                        if qt == 3:
                            mm.then_inc(sem["sPV"], 1)
                    do_tr = None
                    if e_lk == 63 and e_qb == NQB - 1:
                        do_tr = ep
                    elif e_lk == 4 and k >= 64 and ep_of_k[k] % NQB != 0:
                        do_tr = ep_of_k[k] - 1
                    if do_tr is not None:
                        t.wait_ge(sem["sOnb"], do_tr + 1)
                        if do_tr >= 1:
                            t.wait_ge(sem["sPTf"], do_tr)
                        for qt in range(4):
                            tr = t.transpose(psT[:, 128 * qt:128 * qt + 128],
                                             onb[:, 128 * qt:128 * qt + 128], idC[:])
                            if qt == 3:
                                tr.then_inc(sem["sPT"], 1)
                if ki < NK:
                    h, qb, lk = kdesc[ki]
                    if qb == 0 and lk == 0:
                        t.wait_ge(sKL[h % 2], 64 * (h // 2 + 1))
                    mm = t.matmul(psS[ki % 4][:],
                                  kbuf[h % 2][:, 128 * lk:128 * lk + 128],
                                  bufB[:, 2048 * h + 512 * qb:2048 * h + 512 * qb + 512],
                                  start=True, stop=True).then_inc(sem["sS"], 1)

        @block.vector
        def _(v):
            DV = mkDV(v)
            v.wait_ge(sem["sKV"], 96)
            v.memset(vone[0][:], 1.0).then_inc(sem["sVM"], 1)
            v.memset(vone[1][:], 1.0).then_inc(sem["sVM"], 1)

            for ep in range(NEP):
                h, qb = ep // NQB, ep % NQB
                v.wait_ge(sem["sPV"], 64 * (ep + 1))
                if qb == 0 and h >= 2:
                    v.wait_ge(sASs[h % 2], 16 * (h // 2))
                DV(v.tensor_copy(orawA[:], psOA[:]))
                v.tensor_copy(orawB[:], psOB[:]).then_inc(sem["sOn"], 1)
                v.wait_ge(sem["sDV"], dvn[0])
                v.wait_ge(sem["sOn"], ep + 1)
                DV(v.reciprocal(rlb[:, 0:2],
                                orawA[:].rearrange("p (a c) -> p a c", a=2, c=129)[:, :, 128]))
                DV(v.reciprocal(rlb[:, 2:4],
                                orawB[:].rearrange("p (a c) -> p a c", a=2, c=129)[:, :, 128]))
                v.wait_ge(sem["sDV"], dvn[0])
                for qt in range(4):
                    po = orawA if qt < 2 else orawB
                    col = 129 * (qt % 2)
                    mu = v.tensor_scalar(onb[:, 128 * qt:128 * qt + 128],
                                         po[:, col:col + 128], rlb[:, qt:qt + 1],
                                         None, ALU.mult)
                    if qt == 3:
                        mu.then_inc(sem["sOnb"], 1)
                v.wait_ge(sem["sPT"], ep + 1)
                v.tensor_copy(attst[h % 2][:, 512 * qb:512 * qb + 512],
                              psT[:]).then_inc(sem["sPTf"], 1)

        @block.sync
        def _(sy):
            if "att" in TAPS:
                sy.wait_ge(sASs[0], 128)
                sy.wait_ge(sASs[1], 128)
                sy.dma_start(out=taps["att"][:, :], in_=attd[:, :]).then_inc(sem["sOD"], 16)
            bar(sy, 2)

        @block.gpsimd
        def _(gq):
            for ht in range(H):
                slot = ht % 2
                gq.wait_ge(sem["sCC"], 2 * (ht // 4) + 2)
                gq.wait_ge(sem["sVM"], slot + 1)
                if ht >= 2:
                    gq.wait_ge(sem["sPV"], 256 * (ht - 1))
                g2 = ht // 4
                dstv = vone[slot][:].rearrange("p (kt c) -> p kt c", kt=64, c=129)
                gq.dma_start(out=dstv[:, 32:64, 0:128],
                             in_=cc_outV[8192 * g2 + 4096:8192 * g2 + 8192,
                                         128 * (ht % 4):128 * (ht % 4) + 128]
                             .rearrange("(kt p) d -> p kt d", p=128)
                             ).then_inc(sVG[slot], 16)

        @block.gpsimd
        def _(e):
            e.sem_inc(sem["sPH"], 1)
        for engname in ["scalar", "vector", "tensor"]:
            getattr(block, engname)(lambda e: bar(e, 2))

        bctx.close()

        if PHASES == "AB":
            tapsAB = tapsA + (1 if "att" in TAPS else 0)
            @block.sync
            def _(sy):
                sy.wait_ge(sem["sOD"], 16 * tapsAB)
            return nc

        cctx = contextlib.ExitStack()
        EC = cctx.enter_context
        bufC = EC(nc.sbuf_tensor("bufC", [128, NDC * T], BF16))
        xsqC = [EC(nc.sbuf_tensor(f"xsqC{i}", [128, 512], BF16)) for i in range(2)]
        ostg = [EC(nc.sbuf_tensor(f"ostg{i}", [128, 2048], F32)) for i in range(2)]
        xt1C = [EC(nc.sbuf_tensor("xt1C0", [128, 512], BF16))]
        abc2 = EC(nc.sbuf_tensor("abc2", [128, 512], BF16))
        bbc2 = EC(nc.sbuf_tensor("bbc2", [128, 512], BF16))
        lgb = EC(nc.sbuf_tensor("lgb", [128, 32], F32))
        mxb = EC(nc.sbuf_tensor("mxb", [128, 4], F32))
        mskb = EC(nc.sbuf_tensor("mskb", [128, 32], BF16))
        mskT = EC(nc.sbuf_tensor("mskT", [8, 512], BF16))
        m2b = EC(nc.sbuf_tensor("m2b", [128, 512], BF16))
        hmb = EC(nc.sbuf_tensor("hmb", [128, T], BF16))

        psY = [EC(nc.psum_tensor(f"psY{i}", [128, 512], F32)) for i in range(2)]
        psA2 = EC(nc.psum_tensor("psA2", [128, 512], F32))
        psB2s = EC(nc.psum_tensor("psB2s", [128, 512], F32))
        psMix2 = EC(nc.psum_tensor("psMix2", [128, 1024], F32))
        psMT = EC(nc.psum_tensor("psMT", [8, 512], BF16))
        psEx = EC(nc.psum_tensor("psEx", [128, 512], F32))
        psR = psA2
        psHx = psB2s
        psB2 = psY

        @block.sync
        def _(sy):
            sy.wait_ge(sASs[0], 128)
            sy.wait_ge(sASs[1], 128)
            sy.wait_ge(sem["sPTf"], 64)
            sy.wait_ge(sem["sPT"], 64)
            sy.wait_ge(sem["sE"], 4096)
            for oc in range(16):
                sy.dma_start(out=bufC[:, 2048 * oc:2048 * oc + 2048],
                             in_=attd[128 * oc:128 * oc + 128, :]).then_inc(sem["sAL"], 16)
            for mt in range(16):
                slot = mt % 2
                if mt < 2:
                    sy.wait_ge(sXA[slot], marks["xa_end"][slot])
                else:
                    sy.wait_ge(sem["sYf"], 4 * (mt - 1))
                sy.dma_start(out=xbuf[slot][:], in_=xh[:, mt, :]
                             ).then_inc(sXA[slot], 16)
            for mt in range(16):
                sy.wait_ge(sem["sBf"], 4 * (mt + 1))
                sy.dma_start(out=outT[128 * mt:128 * mt + 128, :],
                             in_=ostg[mt % 2][:]).then_inc(sODs[mt % 2], 16)
            if "x1" in TAPS:
                sy.wait_ge(sem["sYf"], 64)
                sy.dma_start(out=taps["x1"][:, :], in_=bufB[:]).then_inc(sem["sOD"], 16)
            if "xn2" in TAPS:
                sy.wait_ge(sem["sXN2"], 64)
                sy.dma_start(out=taps["xn2"][:, :], in_=bufC[:]).then_inc(sem["sOD"], 16)
            if "hm" in TAPS:
                sy.wait_ge(sem["sHr"], 4)
                sy.dma_start(out=taps["hm"][:, :], in_=hmb[:]).then_inc(sem["sOD"], 16)
            sy.wait_ge(sODs[0], 128)
            sy.wait_ge(sODs[1], 128)
            if n_tap:
                sy.wait_ge(sem["sOD"], 16 * n_tap)

        @block.gpsimd
        def _(gq):
            for i, src in enumerate([wo5[:, mt, :, :] for mt in range(16)]
                                    + [wb5[:, mt, :, :] for mt in range(16)]):
                oci = 48 + i
                slot = oci % 3
                if i < 3:
                    gq.wait_ge(sem["sQm"], 2 * (oci - 2))
                elif i - 3 < 16:
                    gq.wait_ge(sem["sYm"], 4 * (i - 3 + 1))
                else:
                    gq.wait_ge(sem["sBm"], 4 * (i - 3 - 16 + 1))
                gq.dma_start(out=wbuf[slot][:],
                             in_=src.rearrange("p a b -> p (a b)")).then_inc(sWL[slot], 16)

        @block.tensor
        def _(t):
            t.wait_ge(sem["sAL"], 256)
            for mt in range(16):
                oci = 48 + mt
                slot = oci % 3
                _tag, _val = wq_wait[oci]
                t.wait_ge(sWL[slot] if _tag == "L" else sWC[slot], _val)
                for qb in range(NQB):
                    idx = mt * 4 + qb
                    if idx >= 2:
                        t.wait_ge(sem["sYf"], idx - 1)
                    for oc in range(16):
                        mm = t.matmul(psY[idx % 2][:],
                                      wbuf[slot][:, 128 * oc:128 * oc + 128],
                                      bufC[:, 2048 * oc + 512 * qb:2048 * oc + 512 * qb + 512],
                                      start=(oc == 0), stop=(oc == 15))
                        if oc == 15:
                            mm.then_inc(sem["sYm"], 1)
            t.wait_ge(sem["sYf"], 64)

            def stats2_tb(tb):
                if tb >= 1:
                    t.wait_ge(sem["sSC"], 4 + tb)
                for dc in range(NDC):
                    idx = 64 + tb * NDC + dc
                    t.matmul(psA2[0:1, :], onC[:],
                             bufB[:, 2048 * dc + 512 * tb:2048 * dc + 512 * tb + 512],
                             start=(dc == 0), stop=(dc == 15))
                    t.wait_ge(sem["sSQ"], idx + 1)
                    t.matmul(psB2s[0:1, :], onC[:], xsqC[idx % 2][:],
                             start=(dc == 0), stop=(dc == 15)).then_inc(sem["sXS"], 1)

            def bcast2_tb(tb):
                t.wait_ge(sem["sSC"], 4 + tb + 1)
                if tb >= 1:
                    t.wait_ge(sem["sBCe"], 4 + tb)
                t.matmul(psMix2[:, 0:512], onR[:],
                         abR[0:1, 512 * tb:512 * tb + 512], start=True, stop=True)
                t.matmul(psMix2[:, 512:1024], onR[:],
                         bR[0:1, 512 * tb:512 * tb + 512],
                         start=True, stop=True).then_inc(sem["sBC"], 2)

            for tb in range(NTB):
                stats2_tb(tb)
            for tb in range(NTB):
                bcast2_tb(tb)
            for qb in range(NQB):
                t.wait_ge(sem["sXN2"], 16 * (qb + 1))
                if qb == 0:
                    t.wait_ge(sem["sSC"], 8)
                if qb >= 1:
                    t.wait_ge(sem["sLG"], qb)
                for qt in range(4):
                    for dc in range(NDC):
                        mm = t.matmul(psR[:, 8 * qt:8 * qt + 8],
                                      bufC[:, 2048 * dc + 512 * qb + 128 * qt:
                                            2048 * dc + 512 * qb + 128 * qt + 128],
                                      rwb[:, 8 * dc:8 * dc + 8],
                                      start=(dc == 0), stop=(dc == 15))
                        if qt == 3 and dc == NDC - 1:
                            mm.then_inc(sem["sRt"], 1)
                t.wait_ge(sem["sMsk"], qb + 1)
                if qb >= 1:
                    t.wait_ge(sem["sMTf"], qb)
                for qt in range(4):
                    tr = t.transpose(psMT[:, 128 * qt:128 * qt + 128],
                                     mskb[:, 8 * qt:8 * qt + 8], idC[:])
                    if qt == 3:
                        tr.then_inc(sem["sMT"], 1)
                t.wait_ge(sem["sMTf"], qb + 1)
                if qb >= 1:
                    t.wait_ge(sem["sM2"], qb)
                t.matmul(psEx[:], e2C[:], mskT[:], start=True, stop=True
                         ).then_inc(sem["sEm"], 1)
                if qb >= 1:
                    t.wait_ge(sem["sHr"], qb)
                for dc in range(NDC):
                    mm = t.matmul(psHx[:], wab[:, 128 * dc:128 * dc + 128],
                                  bufC[:, 2048 * dc + 512 * qb:2048 * dc + 512 * qb + 512],
                                  start=(dc == 0), stop=(dc == 15))
                    if dc == NDC - 1:
                        mm.then_inc(sem["sHm"], 1)
            for mt in range(16):
                oci = 64 + mt
                slot = oci % 3
                _tag, _val = wq_wait[oci]
                t.wait_ge(sWL[slot] if _tag == "L" else sWC[slot], _val)
                for qb in range(NQB):
                    idx = mt * 4 + qb
                    if idx < 2:
                        t.wait_ge(sem["sYf"], 64)
                    if idx >= 2:
                        t.wait_ge(sem["sBf"], idx - 1)
                    if mt == 0:
                        t.wait_ge(sem["sHr"], qb + 1)
                    for dc in range(NDC):
                        t.matmul(psB2[idx % 2][:],
                                 wbuf[slot][:, 128 * dc:128 * dc + 128],
                                 bufC[:, 2048 * dc + 512 * qb:2048 * dc + 512 * qb + 512],
                                 start=(dc == 0), stop=False)
                    t.matmul(psB2[idx % 2][:], wbwb[:, 128 * mt:128 * mt + 128],
                             hmb[:, 512 * qb:512 * qb + 512],
                             start=False, stop=True).then_inc(sem["sBm"], 1)

        @block.scalar
        def _(sc):
            sc.wait_ge(sem["sYf"], 64)
            for tb in range(NTB):
                for dc in range(NDC):
                    idx = 64 + tb * NDC + dc
                    sc.wait_ge(sem["sXS"], idx - 1)
                    sc.activation(out=xsqC[idx % 2][:],
                                  in_=bufB[:, 2048 * dc + 512 * tb:2048 * dc + 512 * tb + 512],
                                  func=AF.Square).then_inc(sem["sSQ"], 1)
                sc.wait_ge(sem["sVAR"], 4 + tb + 1)
                if tb >= 1:
                    sc.wait_ge(sem["sSC"], 4 + tb)
                sc.activation(out=stq_[:], in_=stv_[:], func=AF.Sqrt,
                              bias=epsC[:]).then_inc(sem["sSQT"], 1)

        @block.vector
        def _(v):
            DV = mkDV(v)
            for mt in range(16):
                for qb in range(NQB):
                    idx = mt * 4 + qb
                    v.wait_ge(sem["sYm"], idx + 1)
                    slot = mt % 2
                    v.wait_ge(sXA[slot], marks["xa_dma"][("C", mt)])
                    v.tensor_tensor(bufB[:, 2048 * mt + 512 * qb:2048 * mt + 512 * qb + 512],
                                    psY[idx % 2][:],
                                    xbuf[slot][:, 512 * qb:512 * qb + 512], ALU.add
                                    ).then_inc(sem["sYf"], 1)
            for tb in range(NTB):
                v.wait_ge(sem["sXS"], 64 + 16 * (tb + 1))
                DV(v.tensor_scalar(stm_[:], psA2[0:1, :], 1.0 / 2048.0, None, ALU.mult))
                DV(v.tensor_scalar(stq_[:], psB2s[0:1, :], 1.0 / 2048.0, None, ALU.mult))
                v.wait_ge(sem["sDV"], dvn[0])
                if tb >= 1:
                    v.wait_ge(sem["sSQT"], 4 + tb)
                DV(v.tensor_tensor(stv_[:], stm_[:], stm_[:], ALU.mult))
                v.wait_ge(sem["sDV"], dvn[0])
                v.tensor_tensor(stv_[:], stq_[:], stv_[:], ALU.subtract
                                ).then_inc(sem["sVAR"], 1)
                v.wait_ge(sem["sSQT"], 4 + tb + 1)
                DV(v.reciprocal(stv_[:], stq_[:]))
                v.wait_ge(sem["sDV"], dvn[0])
                DV(v.tensor_copy(abR[0:1, 512 * tb:512 * tb + 512], stv_[:]))
                v.wait_ge(sem["sDV"], dvn[0])
                v.scalar_tensor_tensor(bR[0:1, 512 * tb:512 * tb + 512], stm_[:], -1.0,
                                       abR[0:1, 512 * tb:512 * tb + 512],
                                       ALU.mult, ALU.mult).then_inc(sem["sSC"], 1)
            v.wait_ge(sem["sYm"], 64)
            for tb in range(NTB):
                v.wait_ge(sem["sBC"], 8 + 2 * (tb + 1))
                v.wait_ge(sem["sDV"], dvn[0])
                v.wait_ge(sem["sXN2"], 16 * tb)
                DV(v.tensor_copy(abc2[:], psMix2[:, 0:512]))
                v.tensor_copy(bbc2[:], psMix2[:, 512:1024]).then_inc(sem["sBCe"], 1)
                v.wait_ge(sem["sBCe"], 4 + tb + 1)
                for dc in range(NDC):
                    cnt = tb * NDC + dc
                    t1 = xt1C[0]
                    if cnt >= 1:
                        v.wait_ge(sem["sXN2"], cnt)
                    DV(v.tensor_tensor(t1[:],
                                       bufB[:, 2048 * dc + 512 * tb:2048 * dc + 512 * tb + 512],
                                       abc2[:], ALU.mult))
                    v.wait_ge(sem["sDV"], dvn[0])
                    v.tensor_tensor(bufC[:, 2048 * dc + 512 * tb:2048 * dc + 512 * tb + 512],
                                    t1[:], bbc2[:], ALU.add
                                    ).then_inc(sem["sXN2"], 1)
            for qb in range(NQB):
                v.wait_ge(sem["sRt"], qb + 1)
                v.tensor_copy(lgb[:], psR[:, 0:32]).then_inc(sem["sLG"], 1)
                v.wait_ge(sem["sLG"], qb + 1)
                for qt in range(4):
                    DV(v.reduce_max(mxb[:, qt:qt + 1], lgb[:, 8 * qt:8 * qt + 8], axis=AX.X))
                v.wait_ge(sem["sDV"], dvn[0])
                for qt in range(4):
                    mk = v.tensor_scalar(mskb[:, 8 * qt:8 * qt + 8], lgb[:, 8 * qt:8 * qt + 8],
                                         mxb[:, qt:qt + 1], None, ALU.is_ge)
                    if qt == 3:
                        mk.then_inc(sem["sMsk"], 1)
                v.wait_ge(sem["sMT"], qb + 1)
                v.tensor_copy(mskT[:], psMT[:]).then_inc(sem["sMTf"], 1)
                v.wait_ge(sem["sEm"], qb + 1)
                v.tensor_copy(m2b[:], psEx[:]).then_inc(sem["sM2"], 1)
                v.wait_ge(sem["sHm"], qb + 1)
                v.wait_ge(sem["sM2"], qb + 1)
                v.tensor_tensor(hmb[:, 512 * qb:512 * qb + 512], psHx[:], m2b[:],
                                ALU.mult).then_inc(sem["sHr"], 1)
            for mt in range(16):
                for qb in range(NQB):
                    idx = mt * 4 + qb
                    v.wait_ge(sem["sBm"], idx + 1)
                    if qb == 0 and mt >= 2:
                        v.wait_ge(sODs[mt % 2], 16 * (mt // 2))
                    v.scalar_tensor_tensor(
                        ostg[mt % 2][:, 512 * qb:512 * qb + 512],
                        psB2[idx % 2][:], bbb[:, mt:mt + 1],
                        bufB[:, 2048 * mt + 512 * qb:2048 * mt + 512 * qb + 512],
                        ALU.add, ALU.add).then_inc(sem["sBf"], 1)

        cctx.close()
    return nc


def _host_layout(inputs):
    x = inputs["L0_attn_x"]
    w_qkv = inputs["L0_attn_w_qkv"]
    w_out = inputs["L0_attn_w_out"]
    W_base = inputs["L0_ffn_W_base"]
    b_base = inputs["L0_ffn_b_base"]
    router_W = inputs["L0_ffn_router_W"]
    A = inputs["L0_ffn_A"]
    Bw = inputs["L0_ffn_Bw"]

    wq5 = np.ascontiguousarray(
        np.asarray(w_qkv, np.float32).reshape(48, 128, 16, 128).transpose(3, 0, 2, 1))
    wo5 = np.ascontiguousarray(
        np.asarray(w_out, np.float32).reshape(16, 128, 16, 128).transpose(3, 0, 2, 1))
    wb5 = np.ascontiguousarray(
        np.asarray(W_base, np.float32).reshape(16, 128, 16, 128).transpose(3, 0, 2, 1))
    wa5 = np.ascontiguousarray(
        np.asarray(A, np.float32).reshape(128, 16, 128).transpose(2, 1, 0))
    wbwp = np.ascontiguousarray(
        np.transpose(np.asarray(Bw, np.float32), (0, 2, 1)).reshape(128, 2048))
    rw5 = np.ascontiguousarray(
        np.asarray(router_W, np.float32).T.reshape(16, 128, 8).transpose(1, 0, 2))
    bb = np.ascontiguousarray(np.asarray(b_base, np.float32).reshape(16, 128).T)

    inv_freq = (1.0 / np.float32(10000.0) **
                (np.arange(0, 128, 2, dtype=np.float32) / np.float32(128.0))).astype(np.float32)
    maps = []
    for c in range(8):
        b = c // 4
        j = c % 4
        ts0 = T * j
        xc = np.asarray(x[b, ts0:ts0 + T, :], np.float32)
        xhc = np.ascontiguousarray(xc.T.reshape(16, 128, T).transpose(1, 0, 2))
        tloc = np.arange(ts0, ts0 + T, dtype=np.float32)
        ang = np.concatenate([np.outer(tloc, inv_freq)] * 2, axis=1).astype(np.float32)
        cosTc = np.ascontiguousarray(np.cos(ang).T.astype(np.float32))
        sinTc = np.ascontiguousarray(np.sin(ang).T.astype(np.float32))
        maps.append({"xh": xhc, "wq5": wq5, "wo5": wo5, "wb5": wb5, "wa5": wa5,
                     "wbw": wbwp, "rw5": rw5, "bbi": bb, "cosi": cosTc, "sini": sinTc})
    return maps


def kernel(**inputs):
    maps = _host_layout(inputs)
    key = (frozenset(TAPS), PHASES)
    if key not in _NC_CACHE:
        _NC_CACHE[key] = build_program()
    nc = _NC_CACHE[key]
    res = run_bass_kernel_spmd(nc, maps, core_ids=list(range(8)))
    out = np.empty((B, S, D), np.float32)
    for c in range(8):
        b = c // 4
        j = c % 4
        out[b, T * j:T * j + T, :] = res.results[c]["outT"].T
    if TAPS:
        kernel.last_results = res.results
    return out


# version 18
# speedup vs baseline: 1.0022x; 1.0022x over previous
import math
import contextlib
import numpy as np
import ml_dtypes
import concourse.bass as bass
import concourse.mybir as mybir
from concourse.bass_utils import run_bass_kernel_spmd

F32 = mybir.dt.float32
BF16 = mybir.dt.bfloat16
AF = mybir.ActivationFunctionType
ALU = mybir.AluOpType
AX = mybir.AxisListType

B, S, D = 2, 8192, 2048
H, DH = 16, 128
T = 2048
GROUPS = [[0, 1, 2, 3], [4, 5, 6, 7]]
SCALE = 1.0 / math.sqrt(128.0)
EPS = 1e-5
NTB, NDC, NQB, NPAIR = 4, 16, 4, 32
NG = H * NQB * NPAIR
NEP = H * NQB
ALPHA_SCALING = 2.0

TAPS = set()
PHASES = "ABC"

_NC_CACHE = {}


def _consts():
    Rm = np.zeros((128, 128), np.float32)
    for d in range(64):
        Rm[d, d + 64] = -1.0
    for d in range(64, 128):
        Rm[d, d - 64] = 1.0
    RT = np.ascontiguousarray(Rm.T).astype(ml_dtypes.bfloat16)
    ident = np.eye(128, dtype=ml_dtypes.bfloat16)
    ones = np.ones((128, 1), dtype=ml_dtypes.bfloat16)
    onesrow = np.ones((1, 128), dtype=ml_dtypes.bfloat16)
    E2 = np.zeros((8, 128), dtype=ml_dtypes.bfloat16)
    for e in range(8):
        E2[e, 16 * e:16 * e + 16] = ALPHA_SCALING
    return RT, ident, ones, onesrow, E2


def build_program():
    nc = bass.Bass("TRN2", target_bir_lowering=False, debug=False)

    xh = nc.dram_tensor("xh", [128, NDC, T], F32, kind="ExternalInput").ap()
    wq5 = nc.dram_tensor("wq5", [128, 48, NDC, 128], F32, kind="ExternalInput").ap()
    wo5 = nc.dram_tensor("wo5", [128, 16, 16, 128], F32, kind="ExternalInput").ap()
    wb5 = nc.dram_tensor("wb5", [128, 16, NDC, 128], F32, kind="ExternalInput").ap()
    wa5 = nc.dram_tensor("wa5", [128, NDC, 128], F32, kind="ExternalInput").ap()
    wbw = nc.dram_tensor("wbw", [128, 2048], F32, kind="ExternalInput").ap()
    rw5 = nc.dram_tensor("rw5", [128, NDC, 8], F32, kind="ExternalInput").ap()
    bbi = nc.dram_tensor("bbi", [128, 16], F32, kind="ExternalInput").ap()
    cosi = nc.dram_tensor("cosi", [128, T], F32, kind="ExternalInput").ap()
    sini = nc.dram_tensor("sini", [128, T], F32, kind="ExternalInput").ap()
    outT = nc.dram_tensor("outT", [D, T], F32, kind="ExternalOutput").ap()

    cc_inK = nc.dram_tensor("cc_inK", [D, T], BF16, kind="Internal").ap()
    cc_outK = nc.dram_tensor("cc_outK", [4 * D, T], BF16, kind="Internal").ap()
    cc_inV = nc.dram_tensor("cc_inV", [4 * T, 512], BF16, kind="Internal").ap()
    cc_outV = nc.dram_tensor("cc_outV", [16 * T, 512], BF16, kind="Internal").ap()
    attd = nc.dram_tensor("attd", [D, T], BF16, kind="Internal").ap()

    RTc, IDc, ONESc, ONESRc, E2c = _consts()

    taps = {}
    for name, shape, dt in [("xn", [128, NDC * T], BF16), ("q", [128, H * T], BF16),
                            ("cc", [2 * D, T], BF16), ("att", [D, T], BF16),
                            ("x1", [128, 16 * T], BF16), ("xn2", [128, NDC * T], BF16),
                            ("hm", [128, T], BF16)]:
        if name in TAPS:
            taps[name] = nc.dram_tensor("t_" + name, shape, dt, kind="ExternalOutput").ap()
    n_tap = len(taps)
    n_tap_ab = sum(1 for nm in ("xn", "q", "cc", "att") if nm in TAPS)

    g = contextlib.ExitStack()
    EG = g.enter_context

    bufB = EG(nc.sbuf_tensor("bufB", [128, H * T], BF16))
    rtC = EG(nc.sbuf_tensor("rtC", [128, 128], BF16))
    idC = EG(nc.sbuf_tensor("idC", [128, 128], BF16))
    onC = EG(nc.sbuf_tensor("onC", [128, 1], BF16))
    onR = EG(nc.sbuf_tensor("onR", [1, 128], BF16))
    e2C = EG(nc.sbuf_tensor("e2C", [8, 128], BF16))
    wab = EG(nc.sbuf_tensor("wab", [128, 2048], BF16))
    wbwb = EG(nc.sbuf_tensor("wbwb", [128, 2048], BF16))
    rwb = EG(nc.sbuf_tensor("rwb", [128, NDC * 8], BF16))
    bbb = EG(nc.sbuf_tensor("bbb", [128, 16], F32))
    epsC = EG(nc.sbuf_tensor("epsC", [1, 1], F32))
    wbuf = [EG(nc.sbuf_tensor(f"wbuf{i}", [128, 2048], BF16)) for i in range(3)]
    xbuf = [EG(nc.sbuf_tensor(f"xbuf{i}", [128, 2048], F32)) for i in range(2)]
    abR = EG(nc.sbuf_tensor("abR", [1, T], BF16))
    bR = EG(nc.sbuf_tensor("bR", [1, T], BF16))
    stm_ = EG(nc.sbuf_tensor("stm_", [1, 512], F32))
    stq_ = EG(nc.sbuf_tensor("stq_", [1, 512], F32))
    stv_ = EG(nc.sbuf_tensor("stv_", [1, 512], F32))

    SN = ["sCN", "sDV", "sSQ", "sCB", "sXC", "sXS", "sST", "sVAR", "sSQT",
          "sSC", "sBC", "sXN", "sQm", "sQf", "sKR", "sRm", "sRf", "sKV", "sCI",
          "sCC", "sPH", "sVT", "sVTf", "sVR", "sS", "sE", "sPV", "sOn", "sOf",
          "sPT", "sPTf", "sAS", "sAL", "sYm", "sYf", "sX1", "sC1", "sLG",
          "sMsk", "sMT", "sMTf", "sEm", "sM2", "sHm", "sHr", "sBm", "sBf",
          "sOS", "sOD", "sXN2", "sRt", "sOnb", "sBCe", "sVA", "sVM", "sVB", "sP2F"]
    sem = {nm: EG(nc.semaphore(nm)) for nm in SN}
    sXA = [EG(nc.semaphore(f"sXA{i}")) for i in range(4)]
    sWL = [EG(nc.semaphore(f"sWL{i}")) for i in range(3)]
    sWF = [EG(nc.semaphore(f"sWF{i}")) for i in range(2)]
    sKL = [EG(nc.semaphore(f"sKL{i}")) for i in range(2)]
    sKF = [EG(nc.semaphore(f"sKF{i}")) for i in range(2)]
    sVL = [EG(nc.semaphore(f"sVL{i}")) for i in range(2)]
    sVG = [EG(nc.semaphore(f"sVG{i}")) for i in range(2)]
    sVF = [EG(nc.semaphore(f"sVF{i}")) for i in range(2)]
    sCIK = [EG(nc.semaphore(f"sCIK{i}")) for i in range(2)]
    sCIV = [EG(nc.semaphore(f"sCIV{i}")) for i in range(2)]
    sXW = [EG(nc.semaphore(f"sXW{i}")) for i in range(2)]
    sWC = [EG(nc.semaphore(f"sWC{i}")) for i in range(3)]
    sASs = [EG(nc.semaphore(f"sASs{i}")) for i in range(2)]
    sODs = [EG(nc.semaphore(f"sODs{i}")) for i in range(2)]
    sALs = [EG(nc.semaphore(f"sALs{i}")) for i in range(2)]

    block = EG(nc.Block())

    chunk_order = []
    for cg in range(4):
        chunk_order += [16 + 4 * cg + i for i in range(4)]
        chunk_order += [32 + 4 * cg + i for i in range(4)]
    chunk_order += list(range(16))
    gp_info, rope_gps, contrib_gps = [], [], []
    for oci, oc in enumerate(chunk_order):
        kind = "K" if 16 <= oc < 32 else ("V" if oc >= 32 else "Q")
        for pr in range(2):
            gp_info.append((oc, pr, kind))
            gp = oci * 2 + pr
            if kind in ("K", "Q"):
                rope_gps.append(gp)
            if kind in ("K", "V"):
                contrib_gps.append(gp)
    rope_idx = {gp: i for i, gp in enumerate(rope_gps)}
    contrib_idx = {gp: i for i, gp in enumerate(contrib_gps)}
    _kd = [0, 0]
    _vc = 0
    vci_of = {}
    _rel = {}
    for _ci, _gp in enumerate(contrib_gps):
        if gp_info[_gp][2] == "K":
            _kd[_ci % 2] += 1
            _rel[_ci] = ("K", _ci % 2, 16 * _kd[_ci % 2])
        else:
            vci_of[_gp] = _vc
            _rel[_ci] = ("VD", 16 * (_vc + 1))
            _vc += 1
    kv_release = {ci: _rel[ci - 2] for ci in range(2, 64)}
    pairs = [(h, qb, lp) for h in range(H) for qb in range(NQB) for lp in range(NPAIR)]

    def bar(eng, which):
        eng.sem_inc(sem["sPH"], 1)
        eng.wait_ge(sem["sPH"], 5 * which)

    with g:
        actx = contextlib.ExitStack()
        EA = actx.enter_context
        xnA = EA(nc.sbuf_tensor("xnA", [128, NDC * T], BF16))
        xcb = [EA(nc.sbuf_tensor(f"xcb{i}", [128, 512], BF16)) for i in range(2)]
        xsq = [EA(nc.sbuf_tensor(f"xsq{i}", [128, 512], BF16)) for i in range(2)]
        krawB = EA(nc.sbuf_tensor("krawB", [128, 1024], BF16))
        s1b = EA(nc.sbuf_tensor("s1b", [128, 1024], BF16))
        s2b = EA(nc.sbuf_tensor("s2b", [128, 1024], BF16))
        xt1 = [EA(nc.sbuf_tensor(f"xt1_{i}", [128, 512], BF16)) for i in range(2)]
        kvst = [EA(nc.sbuf_tensor(f"kvst{i}", [128, 1024], BF16)) for i in range(2)]
        abc = EA(nc.sbuf_tensor("abc", [128, 512], BF16))
        bbc = EA(nc.sbuf_tensor("bbc", [128, 512], BF16))
        cosT = EA(nc.sbuf_tensor("cosT", [128, T], BF16))
        sinT = EA(nc.sbuf_tensor("sinT", [128, T], BF16))
        psMix = EA(nc.psum_tensor("psMix", [128, 1024], F32))
        psQ = [EA(nc.psum_tensor(f"psQ{i}", [128, 1024], F32)) for i in range(2)]
        _sctx = contextlib.ExitStack()
        psAB = _sctx.enter_context(nc.psum_tensor("psAB", [1, 1024], F32))
        _sctx.close()
        psVtA = [EA(nc.psum_tensor("psVtA0", [128, 1024], BF16))]

        NCN = 12 * 16
        marks = {"xn_tb": []}
        xa = [0, 0]
        xa_dma, xa_t1m = {}, {}
        for u in range(32):
            sl = u % 2
            xa[sl] += 16
            xa_dma[u] = xa[sl]
            if u >= 16:
                for j in range(4):
                    xa[sl] += 8
                    xa_t1m[(u, j)] = xa[sl]
        xa_end = list(xa)
        for mt in range(16):
            sl = mt % 2
            xa[sl] += 16
            xa_dma[("C", mt)] = xa[sl]
        marks["xa_dma"] = xa_dma
        marks["xa_t1m"] = xa_t1m
        marks["xa_end"] = xa_end
        dvn = [0]
        N0W = 14

        def _is_cast(_o):
            return N0W <= _o < 48 and _o % 2 == 1

        _wl = [0, 0, 0]
        _wc = [0, 0, 0]
        wq_wait = {}
        for _o in range(80):
            if _is_cast(_o):
                _wc[_o % 3] += 1
                wq_wait[_o] = ("C", _wc[_o % 3])
            else:
                _wl[_o % 3] += 16
                wq_wait[_o] = ("L", _wl[_o % 3])
        cast_chunks = [c for c in range(48) if _is_cast(c)]
        cast_pos = {c: i for i, c in enumerate(cast_chunks)}
        stage_xw = {}

        def mkDV(v):
            def DV(inst):
                inst.then_inc(sem["sDV"], 1)
                dvn[0] += 1
                return inst
            return DV

        @block.gpsimd
        def _(gq):
            for dst, src in [(cosT, cosi[:, :]), (sinT, sini[:, :]),
                             (rtC, nc.inline_tensor(RTc, "c_rt").ap()),
                             (idC, nc.inline_tensor(IDc, "c_id").ap()),
                             (onC, nc.inline_tensor(ONESc, "c_on").ap()),
                             (onR, nc.inline_tensor(ONESRc, "c_or").ap()),
                             (e2C, nc.inline_tensor(E2c, "c_e2").ap()),
                             (wab, wa5.rearrange("p dc er -> p (dc er)")),
                             (wbwb, wbw[:, :]),
                             (rwb, rw5.rearrange("p dc e -> p (dc e)")),
                             (bbb, bbi[:, :]),
                             (epsC, nc.inline_tensor(np.full((1, 1), EPS, np.float32), "c_eps").ap())]:
                gq.dma_start(out=dst[:], in_=src).then_inc(sem["sCN"], 16)
            if PHASES == "A0":
                return
            agk_after = {8: 0}
            agv_after = {14: 0}
            for oci in range(48):
                if _is_cast(oci):
                    continue
                oc = chunk_order[oci]
                slot = oci % 3
                if oci >= 3:
                    gq.wait_ge(sem["sQm"], 2 * (oci - 2))
                gq.dma_start(out=wbuf[slot][:],
                             in_=wq5[:, oc, :, :].rearrange("p dc o -> p (dc o)")
                             ).then_inc(sWL[slot], 16)
                if oci in agk_after:
                    cg = agk_after[oci]
                    gq.wait_ge(sCIK[0], 64 * (cg + 1))
                    gq.wait_ge(sCIK[1], 64 * (cg + 1))
                    gq.collective_compute(
                        "AllGather", ALU.bypass, replica_groups=GROUPS,
                        ins=[cc_inK[512 * cg:512 * cg + 512, :].opt()],
                        outs=[cc_outK[2048 * cg:2048 * cg + 2048, :].opt()]
                    ).then_inc(sem["sCC"], 1)
                if oci in agv_after:
                    cg = agv_after[oci]
                    gq.wait_ge(sCIV[0], 16 * 8 * (cg + 1))
                    gq.collective_compute(
                        "AllGather", ALU.bypass, replica_groups=GROUPS,
                        ins=[cc_inV[2048 * cg:2048 * cg + 2048, :].opt()],
                        outs=[cc_outV[8192 * cg:8192 * cg + 8192, :].opt()]
                    ).then_inc(sem["sCC"], 1)
            for cg in range(1, 4):
                gq.wait_ge(sCIK[0], 64 * (cg + 1))
                gq.wait_ge(sCIK[1], 64 * (cg + 1))
                gq.collective_compute(
                    "AllGather", ALU.bypass, replica_groups=GROUPS,
                    ins=[cc_inK[512 * cg:512 * cg + 512, :].opt()],
                    outs=[cc_outK[2048 * cg:2048 * cg + 2048, :].opt()]
                ).then_inc(sem["sCC"], 1)
                gq.wait_ge(sCIV[0], 16 * 8 * (cg + 1))
                gq.collective_compute(
                    "AllGather", ALU.bypass, replica_groups=GROUPS,
                    ins=[cc_inV[2048 * cg:2048 * cg + 2048, :].opt()],
                    outs=[cc_outV[8192 * cg:8192 * cg + 8192, :].opt()]
                ).then_inc(sem["sCC"], 1)


        @block.sync
        def _(sy):
            for p in range(2):
                for tb in range(NTB):
                    for qd in range(4):
                        u = p * 16 + tb * 4 + qd
                        slot = u % 2
                        if 2 <= u < 18:
                            sy.wait_ge(sem["sXC"], 4 * (u - 1))
                            sy.wait_ge(sem["sXS"], 4 * (u - 1))
                        elif u >= 18:
                            sy.wait_ge(sXA[slot], marks["xa_t1m"][(u - 2, 3)])
                        sy.dma_start(out=xbuf[slot][:].rearrange("p (a b) -> p a b", a=4, b=512),
                                     in_=xh[:, 4 * qd:4 * qd + 4, 512 * tb:512 * tb + 512]
                                     ).then_inc(sXA[slot], 16)

            if PHASES == "A0":
                return
            sxw = [0, 0]
            for j in range(48):
                c = j + 2
                if _is_cast(c):
                    ip = cast_pos[c]
                    xs = ip % 2
                    if ip < 2:
                        sy.wait_ge(sXA[xs], marks["xa_t1m"][(30 + xs, 3)])
                    else:
                        pc = cast_chunks[ip - 2]
                        sy.wait_ge(sWC[pc % 3], wq_wait[pc][1])
                    sy.dma_start(out=xbuf[xs][:],
                                 in_=wq5[:, chunk_order[c], :, :]
                                 .rearrange("p dc o -> p (dc o)")
                                 ).then_inc(sXW[xs], 16)
                    sxw[xs] += 16
                    stage_xw[c] = sxw[xs]
                if j > 31:
                    continue
                for ci in (2 * j, 2 * j + 1):
                    gp = contrib_gps[ci]
                    oc, pr, kind = gp_info[gp]
                    slot = ci % 2
                    if kind == "K":
                        sy.wait_ge(sem["sKV"], ci + 1)
                        hh = oc - 16
                        row = 512 * (hh // 4) + 128 * (hh % 4)
                        sy.dma_start(out=cc_inK[row:row + 128,
                                                1024 * pr:1024 * pr + 1024],
                                     in_=kvst[slot][:]).then_inc(sCIK[slot], 16)
                    else:
                        vci = vci_of[gp]
                        sy.wait_ge(sem["sVB"], vci + 1)
                        hh = oc - 32
                        dst = cc_inV[2048 * (hh // 4) + 1024 * pr:
                                     2048 * (hh // 4) + 1024 * pr + 1024,
                                     128 * (hh % 4):128 * (hh % 4) + 128]
                        sy.dma_start(out=dst.rearrange("(kt p) d -> p kt d", p=128),
                                     in_=kvst[slot][:]
                                     .rearrange("p (kt d) -> p kt d", d=128)
                                     ).then_inc(sCIV[0], 16)

        @block.scalar
        def _(sc):
            for tb in range(NTB):
                for dc in range(NDC):
                    idx = tb * NDC + dc
                    u = tb * 4 + dc // 4
                    slot = u % 2
                    sc.wait_ge(sXA[slot], marks["xa_dma"][u])
                    if idx >= 2:
                        sc.wait_ge(sem["sXS"], idx - 1)
                    sc.activation(out=xsq[idx % 2][:],
                                  in_=xbuf[slot][:, 512 * (dc % 4):512 * (dc % 4) + 512],
                                  func=AF.Square).then_inc(sem["sSQ"], 1)
                sc.wait_ge(sem["sVAR"], tb + 1)
                if tb >= 1:
                    sc.wait_ge(sem["sSC"], tb)
                sc.activation(out=stq_[:], in_=stv_[:], func=AF.Sqrt,
                              bias=epsC[:]).then_inc(sem["sSQT"], 1)

        @block.vector
        def _(v):
            DV = mkDV(v)
            for tb in range(NTB):
                for dc in range(NDC):
                    idx = tb * NDC + dc
                    u = tb * 4 + dc // 4
                    slot = u % 2
                    v.wait_ge(sXA[slot], marks["xa_dma"][u])
                    if idx >= 2:
                        v.wait_ge(sem["sXC"], idx - 1)
                    v.tensor_copy(xcb[idx % 2][:],
                                  xbuf[slot][:, 512 * (dc % 4):512 * (dc % 4) + 512]
                                  ).then_inc(sem["sCB"], 1)
                v.wait_ge(sem["sXS"], 16 * (tb + 1))
                if tb >= 1:
                    v.wait_ge(sem["sSC"], tb)
                DV(v.tensor_scalar(stm_[:], psAB[0:1, 0:512], 1.0 / 2048.0, None, ALU.mult))
                v.wait_ge(sem["sDV"], dvn[0])
                v.tensor_scalar(stq_[:], psAB[0:1, 512:1024], 1.0 / 2048.0, None,
                                ALU.mult).then_inc(sem["sP2F"], 1)
                if tb >= 1:
                    v.wait_ge(sem["sSQT"], tb)
                DV(v.tensor_tensor(stv_[:], stm_[:], stm_[:], ALU.mult))
                v.wait_ge(sem["sDV"], dvn[0])
                v.wait_ge(sem["sP2F"], tb + 1)
                v.tensor_tensor(stv_[:], stq_[:], stv_[:], ALU.subtract
                                ).then_inc(sem["sVAR"], 1)
                v.wait_ge(sem["sSQT"], tb + 1)
                DV(v.reciprocal(stv_[:], stq_[:]))
                v.wait_ge(sem["sDV"], dvn[0])
                DV(v.tensor_copy(abR[0:1, 512 * tb:512 * tb + 512], stv_[:]))
                v.wait_ge(sem["sDV"], dvn[0])
                v.scalar_tensor_tensor(bR[0:1, 512 * tb:512 * tb + 512], stm_[:], -1.0,
                                       abR[0:1, 512 * tb:512 * tb + 512],
                                       ALU.mult, ALU.mult).then_inc(sem["sSC"], 1)
            for tb in range(NTB):
                v.wait_ge(sem["sBC"], 2 * (tb + 1))
                v.wait_ge(sem["sDV"], dvn[0])
                DV(v.tensor_copy(abc[:], psMix[:, 0:512]))
                v.tensor_copy(bbc[:], psMix[:, 512:1024]).then_inc(sem["sBCe"], 1)
                v.wait_ge(sem["sDV"], dvn[0])
                v.wait_ge(sem["sBCe"], tb + 1)
                for dc in range(NDC):
                    idx = 64 + tb * NDC + dc
                    u = 16 + tb * 4 + dc // 4
                    slot = u % 2
                    v.wait_ge(sXA[slot], marks["xa_dma"][u])
                    t1 = xt1[dc % 2]
                    if dc >= 2 or tb >= 1:
                        v.wait_ge(sem["sDV"], dvn[0] - 1)
                    v.tensor_tensor(t1[:],
                                    xbuf[slot][:, 512 * (dc % 4):512 * (dc % 4) + 512],
                                    abc[:], ALU.mult).then_inc(sXA[slot], 8)
                    v.wait_ge(sXA[slot], marks["xa_t1m"][(u, dc % 4)])
                    DV(v.tensor_tensor(
                        xnA[:, 2048 * dc + 512 * tb:2048 * dc + 512 * tb + 512],
                        t1[:], bbc[:], ALU.add))
                marks["xn_tb"].append(dvn[0])
            for gp in (range(96) if PHASES != "A0" else []):
                if gp >= 2 and (gp - 2) in vci_of:
                    vci2 = vci_of[gp - 2]
                    v.wait_ge(sem["sVA"], vci2 + 1)
                    v.tensor_copy(kvst[contrib_idx[gp - 2] % 2][:], psVtA[0][:]
                                  ).then_inc(sem["sVB"], 1)
                if gp % 2 == 1:
                    c = gp // 2 + 2
                    if _is_cast(c):
                        ws = c % 3
                        xs = cast_pos[c] % 2
                        v.wait_ge(sem["sQm"], 2 * (c - 2))
                        v.wait_ge(sXW[xs], stage_xw[c])
                        v.tensor_copy(wbuf[ws][:], xbuf[xs][:]
                                      ).then_inc(sWC[ws], 1)
                oc, pr, kind = gp_info[gp]
                v.wait_ge(sem["sQm"], gp + 1)
                if gp >= 1:
                    v.wait_ge(sem["sKV"], gp)
                if kind == "V":
                    ci = contrib_idx[gp]
                    slot = ci % 2
                    if ci >= 2:
                        rel = kv_release[ci]
                        if rel[0] == "K":
                            v.wait_ge(sCIK[rel[1]], rel[2])
                        else:
                            v.wait_ge(sCIV[0], rel[1])
                    v.tensor_copy(kvst[slot][:], psQ[gp % 2][:]).then_inc(sem["sKV"], 1)
                else:
                    ri = rope_idx[gp]
                    if ri >= 1:
                        v.wait_ge(sem["sRm"], ri)
                    v.tensor_copy(krawB[:], psQ[gp % 2][:]).then_inc(sem["sKR"], 1)
                    DV(v.tensor_tensor(s1b[:], psQ[gp % 2][:],
                                       cosT[:, 1024 * pr:1024 * pr + 1024], ALU.mult))
                    v.wait_ge(sem["sRm"], ri + 1)
                    v.tensor_tensor(s2b[:], psMix[:],
                                    sinT[:, 1024 * pr:1024 * pr + 1024], ALU.mult
                                    ).then_inc(sem["sRf"], 1)
                    v.wait_ge(sem["sDV"], dvn[0])
                    v.wait_ge(sem["sRf"], ri + 1)
                    if kind == "K":
                        ci = contrib_idx[gp]
                        slot = ci % 2
                        if ci >= 2:
                            rel = kv_release[ci]
                            if rel[0] == "K":
                                v.wait_ge(sCIK[rel[1]], rel[2])
                            else:
                                v.wait_ge(sCIV[0], rel[1])
                        dst = kvst[slot][:]
                    else:
                        dst = bufB[:, 2048 * oc + 1024 * pr:2048 * oc + 1024 * pr + 1024]
                    v.tensor_tensor(dst, s1b[:], s2b[:], ALU.add).then_inc(sem["sKV"], 1)

        @block.tensor
        def _(t):
            t.wait_ge(sem["sCN"], NCN)

            def stats_tb(tb):
                if tb >= 1:
                    t.wait_ge(sem["sP2F"], tb)
                for dc in range(NDC):
                    idx = tb * NDC + dc
                    t.wait_ge(sem["sCB"], idx + 1)
                    t.wait_ge(sem["sSQ"], idx + 1)
                    t.matmul(psAB[0:1, 0:512], onC[:], xcb[idx % 2][:],
                             start=(dc == 0), stop=(dc == 15),
                             skip_group_check=True).then_inc(sem["sXC"], 1)
                    t.matmul(psAB[0:1, 512:1024], onC[:], xsq[idx % 2][:],
                             start=(dc == 0), stop=(dc == 15),
                             skip_group_check=True).then_inc(sem["sXS"], 1)

            def bcast_tb(tb):
                t.wait_ge(sem["sSC"], tb + 1)
                if tb >= 1:
                    t.wait_ge(sem["sBCe"], tb)
                t.matmul(psMix[:, 0:512], onR[:], abR[0:1, 512 * tb:512 * tb + 512],
                         start=True, stop=True)
                t.matmul(psMix[:, 512:1024], onR[:], bR[0:1, 512 * tb:512 * tb + 512],
                         start=True, stop=True).then_inc(sem["sBC"], 2)

            for tb in range(NTB):
                stats_tb(tb)
            for tb in range(NTB):
                bcast_tb(tb)
            if PHASES == "A0":
                return
            pend_rope = []
            pend_v = []

            def emit_vtr(vgp):
                vci = vci_of[vgp]
                t.wait_ge(sem["sKV"], vgp + 1)
                if vci >= 1:
                    t.wait_ge(sem["sVB"], vci)
                kslot = contrib_idx[vgp] % 2
                for i8 in range(8):
                    tr = t.transpose(psVtA[0][:, 128 * i8:128 * i8 + 128],
                                     kvst[kslot][:, 128 * i8:128 * i8 + 128], idC[:])
                    if i8 == 7:
                        tr.then_inc(sem["sVA"], 1)

            def emit_rot(rgp):
                ri = rope_idx[rgp]
                t.wait_ge(sem["sKR"], ri + 1)
                if ri >= 1:
                    t.wait_ge(sem["sRf"], ri)
                t.matmul(psMix[:, 0:512], rtC[:], krawB[:, 0:512], start=True, stop=True)
                t.matmul(psMix[:, 512:1024], rtC[:], krawB[:, 512:1024],
                         start=True, stop=True).then_inc(sem["sRm"], 1)

            for oci, oc in enumerate(chunk_order):
                slot = oci % 3
                _tag, _val = wq_wait[oci]
                t.wait_ge(sWL[slot] if _tag == "L" else sWC[slot], _val)
                for pr in range(2):
                    gp = oci * 2 + pr
                    if oci == 0:
                        t.wait_ge(sem["sDV"], marks["xn_tb"][2 * pr + 1])
                    if gp >= 2:
                        t.wait_ge(sem["sKV"], gp - 1)
                    for tin in range(2):
                        tcol = 1024 * pr + 512 * tin
                        for dc in range(NDC):
                            mm = t.matmul(psQ[gp % 2][:, 512 * tin:512 * tin + 512],
                                          wbuf[slot][:, 128 * dc:128 * dc + 128],
                                          xnA[:, 2048 * dc + tcol:2048 * dc + tcol + 512],
                                          start=(dc == 0), stop=(dc == 15))
                            if tin == 1 and dc == NDC - 1:
                                mm.then_inc(sem["sQm"], 1)
                    if pend_rope:
                        emit_rot(pend_rope.pop(0))
                    if pend_v:
                        emit_vtr(pend_v.pop(0))
                    if gp in rope_idx:
                        pend_rope.append(gp)
                    if gp in vci_of:
                        pend_v.append(gp)
            while pend_rope:
                emit_rot(pend_rope.pop(0))
            while pend_v:
                emit_vtr(pend_v.pop(0))

        tapsA = (1 if "xn" in TAPS else 0)
        if PHASES != "A0":
            tapsA += (1 if "q" in TAPS else 0)

        @block.sync
        def _(sy):
            if "xn" in TAPS:
                sy.wait_ge(sem["sDV"], marks["xn_tb"][3])
                sy.dma_start(out=taps["xn"][:, :], in_=xnA[:]).then_inc(sem["sOD"], 16)
            if "q" in TAPS and PHASES != "A0":
                sy.wait_ge(sem["sKV"], 96)
                sy.dma_start(out=taps["q"][:, :], in_=bufB[:]).then_inc(sem["sOD"], 16)
            bar(sy, 1)
        for engname in ["gpsimd", "scalar", "vector", "tensor"]:
            getattr(block, engname)(lambda e: bar(e, 1))

        actx.close()

        if PHASES in ("A", "A0"):
            @block.sync
            def _(sy):
                sy.wait_ge(sem["sOD"], 16 * tapsA)
            return nc

        bctx = contextlib.ExitStack()
        EB = bctx.enter_context
        kbuf = [EB(nc.sbuf_tensor(f"kbuf{i}", [128, 8192], BF16)) for i in range(2)]
        vone = [EB(nc.sbuf_tensor(f"vone{i}", [128, 64 * 129], BF16)) for i in range(2)]
        ptb = [EB(nc.sbuf_tensor(f"ptb{i}", [128, 512], BF16)) for i in range(4)]
        onb = EB(nc.sbuf_tensor("onb", [128, 512], BF16))
        orawA = EB(nc.sbuf_tensor("orawA", [128, 258], F32))
        orawB = EB(nc.sbuf_tensor("orawB", [128, 258], F32))
        rlb = EB(nc.sbuf_tensor("rlb", [128, 4], F32))
        attst = [EB(nc.sbuf_tensor(f"attst{i}", [128, 2048], BF16)) for i in range(2)]
        psS = [EB(nc.psum_tensor(f"psS{i}", [128, 512], F32)) for i in range(4)]
        psOA = EB(nc.psum_tensor("psOA", [128, 258], F32))
        psOB = EB(nc.psum_tensor("psOB", [128, 258], F32))
        psT = EB(nc.psum_tensor("psT", [128, 512], BF16))

        @block.sync
        def _(sy):
            sy.wait_ge(sem["sKV"], 96)
            sy.wait_ge(sem["sRm"], 64)
            sy.wait_ge(sem["sSQT"], 4)

            def k_load(h):
                slot = h % 2
                sy.wait_ge(sem["sCC"], 2 * (h // 4) + 1)
                if h >= 2:
                    sy.wait_ge(sem["sS"], 256 * (h - 1))
                base = 2048 * (h // 4)
                for r in range(4):
                    sy.dma_start(out=kbuf[slot][:, 2048 * r:2048 * r + 2048],
                                 in_=cc_outK[base + 512 * r + 128 * (h % 4):
                                             base + 512 * r + 128 * (h % 4) + 128, :]
                                 ).then_inc(sKL[slot], 16)

            def v_load(ht):
                slot = ht % 2
                sy.wait_ge(sem["sCC"], 2 * (ht // 4) + 2)
                sy.wait_ge(sem["sVM"], slot + 1)
                if ht >= 2:
                    sy.wait_ge(sem["sPV"], 256 * (ht - 1))
                g2 = ht // 4
                dstv = vone[slot][:].rearrange("p (kt c) -> p kt c", kt=64, c=129)
                sy.dma_start(out=dstv[:, 0:32, 0:128],
                             in_=cc_outV[8192 * g2:8192 * g2 + 4096,
                                         128 * (ht % 4):128 * (ht % 4) + 128]
                             .rearrange("(kt p) d -> p kt d", p=128)
                             ).then_inc(sVL[slot], 16)

            k_load(0)
            k_load(1)
            v_load(0)
            for h in range(H):
                if h + 1 < H:
                    v_load(h + 1)
                if h + 2 < H:
                    k_load(h + 2)
                sy.wait_ge(sem["sPTf"], 4 * (h + 1))
                sy.dma_start(out=attd[128 * h:128 * h + 128, :],
                             in_=attst[h % 2][:]).then_inc(sASs[h % 2], 16)

        @block.scalar
        def _(sc):
            NK = H * NQB * 64
            for k in range(NK):
                sc.wait_ge(sem["sS"], k + 1)
                if k >= 4:
                    sc.wait_ge(sem["sPV"], k - 3)
                sc.activation(out=ptb[k % 4][:], in_=psS[k % 4][:],
                              func=AF.Exp, scale=SCALE).then_inc(sem["sE"], 1)

        @block.tensor
        def _(t):
            NK = H * NQB * 64
            LAG = 3
            kdesc = [(h, qb, lk) for h in range(H) for qb in range(NQB)
                     for lk in range(64)]
            ep_of_k = [h * NQB + qb for (h, qb, lk) in kdesc]
            for ki in range(NK + LAG):
                if ki >= LAG:
                    k = ki - LAG
                    e_h, e_qb, e_lk = kdesc[k]
                    ep = e_h * NQB + e_qb
                    if e_lk == 0 and e_qb == 0:
                        t.wait_ge(sVL[e_h % 2], 16 * (e_h // 2 + 1))
                        t.wait_ge(sVG[e_h % 2], 16 * (e_h // 2 + 1))
                    if e_lk == 0 and ep >= 1:
                        t.wait_ge(sem["sOn"], ep)
                    t.wait_ge(sem["sE"], k + 1)
                    for qt in range(4):
                        po = psOA if qt < 2 else psOB
                        col = 129 * (qt % 2)
                        mm = t.matmul(po[:, col:col + 129],
                                      ptb[k % 4][:, 128 * qt:128 * qt + 128],
                                      vone[e_h % 2][:, 129 * e_lk:129 * e_lk + 129],
                                      start=(e_lk == 0 and qt in (0, 2)),
                                      stop=(e_lk == 63), skip_group_check=True)
                        if qt == 3:
                            mm.then_inc(sem["sPV"], 1)
                    do_tr = None
                    if e_lk == 63 and e_qb == NQB - 1:
                        do_tr = ep
                    elif e_lk == 4 and k >= 64 and ep_of_k[k] % NQB != 0:
                        do_tr = ep_of_k[k] - 1
                    if do_tr is not None:
                        t.wait_ge(sem["sOnb"], do_tr + 1)
                        if do_tr >= 1:
                            t.wait_ge(sem["sPTf"], do_tr)
                        for qt in range(4):
                            tr = t.transpose(psT[:, 128 * qt:128 * qt + 128],
                                             onb[:, 128 * qt:128 * qt + 128], idC[:])
                            if qt == 3:
                                tr.then_inc(sem["sPT"], 1)
                if ki < NK:
                    h, qb, lk = kdesc[ki]
                    if qb == 0 and lk == 0:
                        t.wait_ge(sKL[h % 2], 64 * (h // 2 + 1))
                    mm = t.matmul(psS[ki % 4][:],
                                  kbuf[h % 2][:, 128 * lk:128 * lk + 128],
                                  bufB[:, 2048 * h + 512 * qb:2048 * h + 512 * qb + 512],
                                  start=True, stop=True).then_inc(sem["sS"], 1)

        @block.vector
        def _(v):
            DV = mkDV(v)
            v.wait_ge(sem["sKV"], 96)
            v.memset(vone[0][:], 1.0).then_inc(sem["sVM"], 1)
            v.memset(vone[1][:], 1.0).then_inc(sem["sVM"], 1)

            for ep in range(NEP):
                h, qb = ep // NQB, ep % NQB
                v.wait_ge(sem["sPV"], 64 * (ep + 1))
                if qb == 0 and h >= 2:
                    v.wait_ge(sASs[h % 2], 16 * (h // 2))
                DV(v.tensor_copy(orawA[:], psOA[:]))
                v.tensor_copy(orawB[:], psOB[:]).then_inc(sem["sOn"], 1)
                v.wait_ge(sem["sDV"], dvn[0])
                v.wait_ge(sem["sOn"], ep + 1)
                DV(v.reciprocal(rlb[:, 0:2],
                                orawA[:].rearrange("p (a c) -> p a c", a=2, c=129)[:, :, 128]))
                DV(v.reciprocal(rlb[:, 2:4],
                                orawB[:].rearrange("p (a c) -> p a c", a=2, c=129)[:, :, 128]))
                v.wait_ge(sem["sDV"], dvn[0])
                for qt in range(4):
                    po = orawA if qt < 2 else orawB
                    col = 129 * (qt % 2)
                    mu = v.tensor_scalar(onb[:, 128 * qt:128 * qt + 128],
                                         po[:, col:col + 128], rlb[:, qt:qt + 1],
                                         None, ALU.mult)
                    if qt == 3:
                        mu.then_inc(sem["sOnb"], 1)
                v.wait_ge(sem["sPT"], ep + 1)
                v.tensor_copy(attst[h % 2][:, 512 * qb:512 * qb + 512],
                              psT[:]).then_inc(sem["sPTf"], 1)

        @block.sync
        def _(sy):
            if "att" in TAPS:
                sy.wait_ge(sASs[0], 128)
                sy.wait_ge(sASs[1], 128)
                sy.dma_start(out=taps["att"][:, :], in_=attd[:, :]).then_inc(sem["sOD"], 16)
            bar(sy, 2)

        @block.gpsimd
        def _(gq):
            for ht in range(H):
                slot = ht % 2
                gq.wait_ge(sem["sCC"], 2 * (ht // 4) + 2)
                gq.wait_ge(sem["sVM"], slot + 1)
                if ht >= 2:
                    gq.wait_ge(sem["sPV"], 256 * (ht - 1))
                g2 = ht // 4
                dstv = vone[slot][:].rearrange("p (kt c) -> p kt c", kt=64, c=129)
                gq.dma_start(out=dstv[:, 32:64, 0:128],
                             in_=cc_outV[8192 * g2 + 4096:8192 * g2 + 8192,
                                         128 * (ht % 4):128 * (ht % 4) + 128]
                             .rearrange("(kt p) d -> p kt d", p=128)
                             ).then_inc(sVG[slot], 16)

        @block.gpsimd
        def _(e):
            e.sem_inc(sem["sPH"], 1)
        for engname in ["scalar", "vector", "tensor"]:
            getattr(block, engname)(lambda e: bar(e, 2))

        bctx.close()

        if PHASES == "AB":
            tapsAB = tapsA + (1 if "att" in TAPS else 0)
            @block.sync
            def _(sy):
                sy.wait_ge(sem["sOD"], 16 * tapsAB)
            return nc

        cctx = contextlib.ExitStack()
        EC = cctx.enter_context
        bufC = EC(nc.sbuf_tensor("bufC", [128, NDC * T], BF16))
        xsqC = [EC(nc.sbuf_tensor(f"xsqC{i}", [128, 512], BF16)) for i in range(2)]
        ostg = [EC(nc.sbuf_tensor(f"ostg{i}", [128, 2048], F32)) for i in range(2)]
        xt1C = [EC(nc.sbuf_tensor("xt1C0", [128, 512], BF16))]
        abc2 = EC(nc.sbuf_tensor("abc2", [128, 512], BF16))
        bbc2 = EC(nc.sbuf_tensor("bbc2", [128, 512], BF16))
        lgb = EC(nc.sbuf_tensor("lgb", [128, 32], F32))
        mxb = EC(nc.sbuf_tensor("mxb", [128, 4], F32))
        mskb = EC(nc.sbuf_tensor("mskb", [128, 32], BF16))
        mskT = EC(nc.sbuf_tensor("mskT", [8, 512], BF16))
        m2b = EC(nc.sbuf_tensor("m2b", [128, 512], BF16))
        hmb = EC(nc.sbuf_tensor("hmb", [128, T], BF16))

        psY = [EC(nc.psum_tensor(f"psY{i}", [128, 512], F32)) for i in range(2)]
        psA2 = EC(nc.psum_tensor("psA2", [128, 512], F32))
        psB2s = EC(nc.psum_tensor("psB2s", [128, 512], F32))
        psMix2 = EC(nc.psum_tensor("psMix2", [128, 1024], F32))
        psMT = EC(nc.psum_tensor("psMT", [8, 512], BF16))
        psEx = EC(nc.psum_tensor("psEx", [128, 512], F32))
        psR = psA2
        psHx = psB2s
        psB2 = psY

        @block.sync
        def _(sy):
            sy.wait_ge(sASs[0], 128)
            sy.wait_ge(sASs[1], 128)
            sy.wait_ge(sem["sPTf"], 64)
            sy.wait_ge(sem["sPT"], 64)
            sy.wait_ge(sem["sE"], 4096)
            for oc in range(16):
                if oc >= 2:
                    sy.wait_ge(sALs[oc % 2], 16 * (oc // 2))
                sy.dma_start(out=bufC[:, 2048 * oc:2048 * oc + 2048],
                             in_=attd[128 * oc:128 * oc + 128, :]).then_inc(sALs[oc % 2], 16)
            for mt in range(16):
                slot = mt % 2
                if mt < 2:
                    sy.wait_ge(sXA[slot], marks["xa_end"][slot])
                else:
                    sy.wait_ge(sem["sYf"], 4 * (mt - 1))
                sy.dma_start(out=xbuf[slot][:], in_=xh[:, mt, :]
                             ).then_inc(sXA[slot], 16)
            for mt in range(16):
                sy.wait_ge(sem["sBf"], 4 * (mt + 1))
                sy.dma_start(out=outT[128 * mt:128 * mt + 128, :],
                             in_=ostg[mt % 2][:]).then_inc(sODs[mt % 2], 16)
            if "x1" in TAPS:
                sy.wait_ge(sem["sYf"], 64)
                sy.dma_start(out=taps["x1"][:, :], in_=bufB[:]).then_inc(sem["sOD"], 16)
            if "xn2" in TAPS:
                sy.wait_ge(sem["sXN2"], 64)
                sy.dma_start(out=taps["xn2"][:, :], in_=bufC[:]).then_inc(sem["sOD"], 16)
            if "hm" in TAPS:
                sy.wait_ge(sem["sHr"], 4)
                sy.dma_start(out=taps["hm"][:, :], in_=hmb[:]).then_inc(sem["sOD"], 16)
            sy.wait_ge(sODs[0], 128)
            sy.wait_ge(sODs[1], 128)
            if n_tap:
                sy.wait_ge(sem["sOD"], 16 * n_tap)

        @block.gpsimd
        def _(gq):
            for i, src in enumerate([wo5[:, mt, :, :] for mt in range(16)]
                                    + [wb5[:, mt, :, :] for mt in range(16)]):
                oci = 48 + i
                slot = oci % 3
                if i < 3:
                    gq.wait_ge(sem["sQm"], 2 * (oci - 2))
                elif i - 3 < 16:
                    gq.wait_ge(sem["sYm"], 4 * (i - 3 + 1))
                else:
                    gq.wait_ge(sem["sBm"], 4 * (i - 3 - 16 + 1))
                gq.dma_start(out=wbuf[slot][:],
                             in_=src.rearrange("p a b -> p (a b)")).then_inc(sWL[slot], 16)

        @block.tensor
        def _(t):
            for mt in range(16):
                oci = 48 + mt
                slot = oci % 3
                _tag, _val = wq_wait[oci]
                t.wait_ge(sWL[slot] if _tag == "L" else sWC[slot], _val)
                for qb in range(NQB):
                    idx = mt * 4 + qb
                    if idx >= 2:
                        t.wait_ge(sem["sYf"], idx - 1)
                    for oc in range(16):
                        if mt == 0 and qb == 0:
                            t.wait_ge(sALs[oc % 2], 16 * (oc // 2 + 1))
                        mm = t.matmul(psY[idx % 2][:],
                                      wbuf[slot][:, 128 * oc:128 * oc + 128],
                                      bufC[:, 2048 * oc + 512 * qb:2048 * oc + 512 * qb + 512],
                                      start=(oc == 0), stop=(oc == 15))
                        if oc == 15:
                            mm.then_inc(sem["sYm"], 1)
            t.wait_ge(sem["sYf"], 64)

            def stats2_tb(tb):
                if tb >= 1:
                    t.wait_ge(sem["sP2F"], 4 + tb)
                for dc in range(NDC):
                    idx = 64 + tb * NDC + dc
                    t.matmul(psA2[0:1, :], onC[:],
                             bufB[:, 2048 * dc + 512 * tb:2048 * dc + 512 * tb + 512],
                             start=(dc == 0), stop=(dc == 15))
                    t.wait_ge(sem["sSQ"], idx + 1)
                    t.matmul(psB2s[0:1, :], onC[:], xsqC[idx % 2][:],
                             start=(dc == 0), stop=(dc == 15)).then_inc(sem["sXS"], 1)

            def bcast2_tb(tb):
                t.wait_ge(sem["sSC"], 4 + tb + 1)
                if tb >= 1:
                    t.wait_ge(sem["sBCe"], 4 + tb)
                t.matmul(psMix2[:, 0:512], onR[:],
                         abR[0:1, 512 * tb:512 * tb + 512], start=True, stop=True)
                t.matmul(psMix2[:, 512:1024], onR[:],
                         bR[0:1, 512 * tb:512 * tb + 512],
                         start=True, stop=True).then_inc(sem["sBC"], 2)

            for tb in range(NTB):
                stats2_tb(tb)
            for tb in range(NTB):
                bcast2_tb(tb)
            for qb in range(NQB):
                t.wait_ge(sem["sXN2"], 16 * (qb + 1))
                if qb == 0:
                    t.wait_ge(sem["sSC"], 8)
                if qb >= 1:
                    t.wait_ge(sem["sLG"], qb)
                for qt in range(4):
                    for dc in range(NDC):
                        mm = t.matmul(psR[:, 8 * qt:8 * qt + 8],
                                      bufC[:, 2048 * dc + 512 * qb + 128 * qt:
                                            2048 * dc + 512 * qb + 128 * qt + 128],
                                      rwb[:, 8 * dc:8 * dc + 8],
                                      start=(dc == 0), stop=(dc == 15))
                        if qt == 3 and dc == NDC - 1:
                            mm.then_inc(sem["sRt"], 1)
                t.wait_ge(sem["sMsk"], qb + 1)
                if qb >= 1:
                    t.wait_ge(sem["sMTf"], qb)
                for qt in range(4):
                    tr = t.transpose(psMT[:, 128 * qt:128 * qt + 128],
                                     mskb[:, 8 * qt:8 * qt + 8], idC[:])
                    if qt == 3:
                        tr.then_inc(sem["sMT"], 1)
                t.wait_ge(sem["sMTf"], qb + 1)
                if qb >= 1:
                    t.wait_ge(sem["sM2"], qb)
                t.matmul(psEx[:], e2C[:], mskT[:], start=True, stop=True
                         ).then_inc(sem["sEm"], 1)
                if qb >= 1:
                    t.wait_ge(sem["sHr"], qb)
                for dc in range(NDC):
                    mm = t.matmul(psHx[:], wab[:, 128 * dc:128 * dc + 128],
                                  bufC[:, 2048 * dc + 512 * qb:2048 * dc + 512 * qb + 512],
                                  start=(dc == 0), stop=(dc == 15))
                    if dc == NDC - 1:
                        mm.then_inc(sem["sHm"], 1)
            for mt in range(16):
                oci = 64 + mt
                slot = oci % 3
                _tag, _val = wq_wait[oci]
                t.wait_ge(sWL[slot] if _tag == "L" else sWC[slot], _val)
                for qb in range(NQB):
                    idx = mt * 4 + qb
                    if idx < 2:
                        t.wait_ge(sem["sYf"], 64)
                    if idx >= 2:
                        t.wait_ge(sem["sBf"], idx - 1)
                    if mt == 0:
                        t.wait_ge(sem["sHr"], qb + 1)
                    for dc in range(NDC):
                        t.matmul(psB2[idx % 2][:],
                                 wbuf[slot][:, 128 * dc:128 * dc + 128],
                                 bufC[:, 2048 * dc + 512 * qb:2048 * dc + 512 * qb + 512],
                                 start=(dc == 0), stop=False)
                    t.matmul(psB2[idx % 2][:], wbwb[:, 128 * mt:128 * mt + 128],
                             hmb[:, 512 * qb:512 * qb + 512],
                             start=False, stop=True).then_inc(sem["sBm"], 1)

        @block.scalar
        def _(sc):
            sc.wait_ge(sem["sYf"], 64)
            for tb in range(NTB):
                for dc in range(NDC):
                    idx = 64 + tb * NDC + dc
                    sc.wait_ge(sem["sXS"], idx - 1)
                    sc.activation(out=xsqC[idx % 2][:],
                                  in_=bufB[:, 2048 * dc + 512 * tb:2048 * dc + 512 * tb + 512],
                                  func=AF.Square).then_inc(sem["sSQ"], 1)
                sc.wait_ge(sem["sVAR"], 4 + tb + 1)
                if tb >= 1:
                    sc.wait_ge(sem["sSC"], 4 + tb)
                sc.activation(out=stq_[:], in_=stv_[:], func=AF.Sqrt,
                              bias=epsC[:]).then_inc(sem["sSQT"], 1)

        @block.vector
        def _(v):
            DV = mkDV(v)
            for mt in range(16):
                for qb in range(NQB):
                    idx = mt * 4 + qb
                    v.wait_ge(sem["sYm"], idx + 1)
                    slot = mt % 2
                    v.wait_ge(sXA[slot], marks["xa_dma"][("C", mt)])
                    v.tensor_tensor(bufB[:, 2048 * mt + 512 * qb:2048 * mt + 512 * qb + 512],
                                    psY[idx % 2][:],
                                    xbuf[slot][:, 512 * qb:512 * qb + 512], ALU.add
                                    ).then_inc(sem["sYf"], 1)
            for tb in range(NTB):
                v.wait_ge(sem["sXS"], 64 + 16 * (tb + 1))
                v.wait_ge(sem["sSC"], 4 + tb)
                DV(v.tensor_scalar(stm_[:], psA2[0:1, :], 1.0 / 2048.0, None, ALU.mult))
                v.wait_ge(sem["sDV"], dvn[0])
                v.tensor_scalar(stq_[:], psB2s[0:1, :], 1.0 / 2048.0, None,
                                ALU.mult).then_inc(sem["sP2F"], 1)
                if tb >= 1:
                    v.wait_ge(sem["sSQT"], 4 + tb)
                DV(v.tensor_tensor(stv_[:], stm_[:], stm_[:], ALU.mult))
                v.wait_ge(sem["sDV"], dvn[0])
                v.wait_ge(sem["sP2F"], 4 + tb + 1)
                v.tensor_tensor(stv_[:], stq_[:], stv_[:], ALU.subtract
                                ).then_inc(sem["sVAR"], 1)
                v.wait_ge(sem["sSQT"], 4 + tb + 1)
                DV(v.reciprocal(stv_[:], stq_[:]))
                v.wait_ge(sem["sDV"], dvn[0])
                DV(v.tensor_copy(abR[0:1, 512 * tb:512 * tb + 512], stv_[:]))
                v.wait_ge(sem["sDV"], dvn[0])
                v.scalar_tensor_tensor(bR[0:1, 512 * tb:512 * tb + 512], stm_[:], -1.0,
                                       abR[0:1, 512 * tb:512 * tb + 512],
                                       ALU.mult, ALU.mult).then_inc(sem["sSC"], 1)
            v.wait_ge(sem["sYm"], 64)
            for tb in range(NTB):
                v.wait_ge(sem["sBC"], 8 + 2 * (tb + 1))
                v.wait_ge(sem["sDV"], dvn[0])
                v.wait_ge(sem["sXN2"], 16 * tb)
                DV(v.tensor_copy(abc2[:], psMix2[:, 0:512]))
                v.tensor_copy(bbc2[:], psMix2[:, 512:1024]).then_inc(sem["sBCe"], 1)
                v.wait_ge(sem["sBCe"], 4 + tb + 1)
                for dc in range(NDC):
                    cnt = tb * NDC + dc
                    t1 = xt1C[0]
                    if cnt >= 1:
                        v.wait_ge(sem["sXN2"], cnt)
                    DV(v.tensor_tensor(t1[:],
                                       bufB[:, 2048 * dc + 512 * tb:2048 * dc + 512 * tb + 512],
                                       abc2[:], ALU.mult))
                    v.wait_ge(sem["sDV"], dvn[0])
                    v.tensor_tensor(bufC[:, 2048 * dc + 512 * tb:2048 * dc + 512 * tb + 512],
                                    t1[:], bbc2[:], ALU.add
                                    ).then_inc(sem["sXN2"], 1)
            for qb in range(NQB):
                v.wait_ge(sem["sRt"], qb + 1)
                v.tensor_copy(lgb[:], psR[:, 0:32]).then_inc(sem["sLG"], 1)
                v.wait_ge(sem["sLG"], qb + 1)
                for qt in range(4):
                    DV(v.reduce_max(mxb[:, qt:qt + 1], lgb[:, 8 * qt:8 * qt + 8], axis=AX.X))
                v.wait_ge(sem["sDV"], dvn[0])
                for qt in range(4):
                    mk = v.tensor_scalar(mskb[:, 8 * qt:8 * qt + 8], lgb[:, 8 * qt:8 * qt + 8],
                                         mxb[:, qt:qt + 1], None, ALU.is_ge)
                    if qt == 3:
                        mk.then_inc(sem["sMsk"], 1)
                v.wait_ge(sem["sMT"], qb + 1)
                v.tensor_copy(mskT[:], psMT[:]).then_inc(sem["sMTf"], 1)
                v.wait_ge(sem["sEm"], qb + 1)
                v.tensor_copy(m2b[:], psEx[:]).then_inc(sem["sM2"], 1)
                v.wait_ge(sem["sHm"], qb + 1)
                v.wait_ge(sem["sM2"], qb + 1)
                v.tensor_tensor(hmb[:, 512 * qb:512 * qb + 512], psHx[:], m2b[:],
                                ALU.mult).then_inc(sem["sHr"], 1)
            for mt in range(16):
                for qb in range(NQB):
                    idx = mt * 4 + qb
                    v.wait_ge(sem["sBm"], idx + 1)
                    if qb == 0 and mt >= 2:
                        v.wait_ge(sODs[mt % 2], 16 * (mt // 2))
                    v.scalar_tensor_tensor(
                        ostg[mt % 2][:, 512 * qb:512 * qb + 512],
                        psB2[idx % 2][:], bbb[:, mt:mt + 1],
                        bufB[:, 2048 * mt + 512 * qb:2048 * mt + 512 * qb + 512],
                        ALU.add, ALU.add).then_inc(sem["sBf"], 1)

        cctx.close()
    return nc


def _host_layout(inputs):
    x = inputs["L0_attn_x"]
    w_qkv = inputs["L0_attn_w_qkv"]
    w_out = inputs["L0_attn_w_out"]
    W_base = inputs["L0_ffn_W_base"]
    b_base = inputs["L0_ffn_b_base"]
    router_W = inputs["L0_ffn_router_W"]
    A = inputs["L0_ffn_A"]
    Bw = inputs["L0_ffn_Bw"]

    wq5 = np.ascontiguousarray(
        np.asarray(w_qkv, np.float32).reshape(48, 128, 16, 128).transpose(3, 0, 2, 1))
    wo5 = np.ascontiguousarray(
        np.asarray(w_out, np.float32).reshape(16, 128, 16, 128).transpose(3, 0, 2, 1))
    wb5 = np.ascontiguousarray(
        np.asarray(W_base, np.float32).reshape(16, 128, 16, 128).transpose(3, 0, 2, 1))
    wa5 = np.ascontiguousarray(
        np.asarray(A, np.float32).reshape(128, 16, 128).transpose(2, 1, 0))
    wbwp = np.ascontiguousarray(
        np.transpose(np.asarray(Bw, np.float32), (0, 2, 1)).reshape(128, 2048))
    rw5 = np.ascontiguousarray(
        np.asarray(router_W, np.float32).T.reshape(16, 128, 8).transpose(1, 0, 2))
    bb = np.ascontiguousarray(np.asarray(b_base, np.float32).reshape(16, 128).T)

    inv_freq = (1.0 / np.float32(10000.0) **
                (np.arange(0, 128, 2, dtype=np.float32) / np.float32(128.0))).astype(np.float32)
    maps = []
    for c in range(8):
        b = c // 4
        j = c % 4
        ts0 = T * j
        xc = np.asarray(x[b, ts0:ts0 + T, :], np.float32)
        xhc = np.ascontiguousarray(xc.T.reshape(16, 128, T).transpose(1, 0, 2))
        tloc = np.arange(ts0, ts0 + T, dtype=np.float32)
        ang = np.concatenate([np.outer(tloc, inv_freq)] * 2, axis=1).astype(np.float32)
        cosTc = np.ascontiguousarray(np.cos(ang).T.astype(np.float32))
        sinTc = np.ascontiguousarray(np.sin(ang).T.astype(np.float32))
        maps.append({"xh": xhc, "wq5": wq5, "wo5": wo5, "wb5": wb5, "wa5": wa5,
                     "wbw": wbwp, "rw5": rw5, "bbi": bb, "cosi": cosTc, "sini": sinTc})
    return maps


def kernel(**inputs):
    maps = _host_layout(inputs)
    key = (frozenset(TAPS), PHASES)
    if key not in _NC_CACHE:
        _NC_CACHE[key] = build_program()
    nc = _NC_CACHE[key]
    res = run_bass_kernel_spmd(nc, maps, core_ids=list(range(8)))
    out = np.empty((B, S, D), np.float32)
    for c in range(8):
        b = c // 4
        j = c % 4
        out[b, T * j:T * j + T, :] = res.results[c]["outT"].T
    if TAPS:
        kernel.last_results = res.results
    return out


# version 19
# speedup vs baseline: 1.0089x; 1.0067x over previous
import math
import contextlib
import numpy as np
import ml_dtypes
import concourse.bass as bass
import concourse.mybir as mybir
from concourse.bass_utils import run_bass_kernel_spmd

F32 = mybir.dt.float32
BF16 = mybir.dt.bfloat16
AF = mybir.ActivationFunctionType
ALU = mybir.AluOpType
AX = mybir.AxisListType

B, S, D = 2, 8192, 2048
H, DH = 16, 128
T = 2048
GROUPS = [[0, 1, 2, 3], [4, 5, 6, 7]]
SCALE = 1.0 / math.sqrt(128.0)
EPS = 1e-5
NTB, NDC, NQB, NPAIR = 4, 16, 4, 32
NG = H * NQB * NPAIR
NEP = H * NQB
ALPHA_SCALING = 2.0

TAPS = set()
PHASES = "ABC"

_NC_CACHE = {}


def _consts():
    Rm = np.zeros((128, 128), np.float32)
    for d in range(64):
        Rm[d, d + 64] = -1.0
    for d in range(64, 128):
        Rm[d, d - 64] = 1.0
    RT = np.ascontiguousarray(Rm.T).astype(ml_dtypes.bfloat16)
    ident = np.eye(128, dtype=ml_dtypes.bfloat16)
    ones = np.ones((128, 1), dtype=ml_dtypes.bfloat16)
    onesrow = np.ones((1, 128), dtype=ml_dtypes.bfloat16)
    E2 = np.zeros((8, 128), dtype=ml_dtypes.bfloat16)
    for e in range(8):
        E2[e, 16 * e:16 * e + 16] = ALPHA_SCALING
    return RT, ident, ones, onesrow, E2


def build_program():
    nc = bass.Bass("TRN2", target_bir_lowering=False, debug=False)

    xh = nc.dram_tensor("xh", [128, NDC, T], F32, kind="ExternalInput").ap()
    wq5 = nc.dram_tensor("wq5", [128, 48, NDC, 128], F32, kind="ExternalInput").ap()
    wo5 = nc.dram_tensor("wo5", [128, 16, 16, 128], F32, kind="ExternalInput").ap()
    wb5 = nc.dram_tensor("wb5", [128, 16, NDC, 128], F32, kind="ExternalInput").ap()
    wa5 = nc.dram_tensor("wa5", [128, NDC, 128], F32, kind="ExternalInput").ap()
    wbw = nc.dram_tensor("wbw", [128, 2048], F32, kind="ExternalInput").ap()
    rw5 = nc.dram_tensor("rw5", [128, NDC, 8], F32, kind="ExternalInput").ap()
    bbi = nc.dram_tensor("bbi", [128, 16], F32, kind="ExternalInput").ap()
    cosi = nc.dram_tensor("cosi", [128, T], F32, kind="ExternalInput").ap()
    sini = nc.dram_tensor("sini", [128, T], F32, kind="ExternalInput").ap()
    outT = nc.dram_tensor("outT", [D, T], F32, kind="ExternalOutput").ap()

    cc_inK = nc.dram_tensor("cc_inK", [D, T], BF16, kind="Internal").ap()
    cc_outK = nc.dram_tensor("cc_outK", [4 * D, T], BF16, kind="Internal").ap()
    cc_inV = nc.dram_tensor("cc_inV", [4 * T, 512], BF16, kind="Internal").ap()
    cc_outV = nc.dram_tensor("cc_outV", [16 * T, 512], BF16, kind="Internal").ap()
    attd = nc.dram_tensor("attd", [D, T], BF16, kind="Internal").ap()

    RTc, IDc, ONESc, ONESRc, E2c = _consts()

    taps = {}
    for name, shape, dt in [("xn", [128, NDC * T], BF16), ("q", [128, H * T], BF16),
                            ("cc", [2 * D, T], BF16), ("att", [D, T], BF16),
                            ("x1", [128, 16 * T], BF16), ("xn2", [128, NDC * T], BF16),
                            ("hm", [128, T], BF16)]:
        if name in TAPS:
            taps[name] = nc.dram_tensor("t_" + name, shape, dt, kind="ExternalOutput").ap()
    n_tap = len(taps)
    n_tap_ab = sum(1 for nm in ("xn", "q", "cc", "att") if nm in TAPS)

    g = contextlib.ExitStack()
    EG = g.enter_context

    bufB = EG(nc.sbuf_tensor("bufB", [128, H * T], BF16))
    rtC = EG(nc.sbuf_tensor("rtC", [128, 128], BF16))
    idC = EG(nc.sbuf_tensor("idC", [128, 128], BF16))
    onC = EG(nc.sbuf_tensor("onC", [128, 1], BF16))
    onR = EG(nc.sbuf_tensor("onR", [1, 128], BF16))
    e2C = EG(nc.sbuf_tensor("e2C", [8, 128], BF16))
    wab = EG(nc.sbuf_tensor("wab", [128, 2048], BF16))
    wbwb = EG(nc.sbuf_tensor("wbwb", [128, 2048], BF16))
    rwb = EG(nc.sbuf_tensor("rwb", [128, NDC * 8], BF16))
    bbb = EG(nc.sbuf_tensor("bbb", [128, 16], F32))
    epsC = EG(nc.sbuf_tensor("epsC", [1, 1], F32))
    wbuf = [EG(nc.sbuf_tensor(f"wbuf{i}", [128, 2048], BF16)) for i in range(3)]
    xbuf = [EG(nc.sbuf_tensor(f"xbuf{i}", [128, 2048], F32)) for i in range(2)]
    abR = EG(nc.sbuf_tensor("abR", [1, T], BF16))
    bR = EG(nc.sbuf_tensor("bR", [1, T], BF16))
    stm_ = EG(nc.sbuf_tensor("stm_", [1, 512], F32))
    stq_ = EG(nc.sbuf_tensor("stq_", [1, 512], F32))
    stv_ = EG(nc.sbuf_tensor("stv_", [1, 512], F32))

    SN = ["sCN", "sDV", "sSQ", "sCB", "sXC", "sXS", "sST", "sVAR", "sSQT",
          "sSC", "sBC", "sXN", "sQm", "sQf", "sKR", "sRm", "sRf", "sKV", "sCI",
          "sCC", "sPH", "sVT", "sVTf", "sVR", "sS", "sE", "sPV", "sOn", "sOf",
          "sPT", "sPTf", "sAS", "sAL", "sYm", "sYf", "sX1", "sC1", "sLG",
          "sMsk", "sMT", "sMTf", "sEm", "sM2", "sHm", "sHr", "sBm", "sBf",
          "sOS", "sOD", "sXN2", "sRt", "sOnb", "sBCe", "sVA", "sVM", "sVB", "sP2F"]
    sem = {nm: EG(nc.semaphore(nm)) for nm in SN}
    sXA = [EG(nc.semaphore(f"sXA{i}")) for i in range(4)]
    sWL = [EG(nc.semaphore(f"sWL{i}")) for i in range(3)]
    sWF = [EG(nc.semaphore(f"sWF{i}")) for i in range(2)]
    sKL = [EG(nc.semaphore(f"sKL{i}")) for i in range(2)]
    sKF = [EG(nc.semaphore(f"sKF{i}")) for i in range(2)]
    sVL = [EG(nc.semaphore(f"sVL{i}")) for i in range(2)]
    sVG = [EG(nc.semaphore(f"sVG{i}")) for i in range(2)]
    sVF = [EG(nc.semaphore(f"sVF{i}")) for i in range(2)]
    sCIK = [EG(nc.semaphore(f"sCIK{i}")) for i in range(2)]
    sCIV = [EG(nc.semaphore(f"sCIV{i}")) for i in range(2)]
    sXW = [EG(nc.semaphore(f"sXW{i}")) for i in range(2)]
    sWC = [EG(nc.semaphore(f"sWC{i}")) for i in range(3)]
    sASs = [EG(nc.semaphore(f"sASs{i}")) for i in range(2)]
    sODs = [EG(nc.semaphore(f"sODs{i}")) for i in range(2)]
    sALs = [EG(nc.semaphore(f"sALs{i}")) for i in range(2)]

    block = EG(nc.Block())

    chunk_order = []
    for cg in range(4):
        chunk_order += [16 + 4 * cg + i for i in range(4)]
        chunk_order += [32 + 4 * cg + i for i in range(4)]
    chunk_order += list(range(16))
    gp_info, rope_gps, contrib_gps = [], [], []
    for oci, oc in enumerate(chunk_order):
        kind = "K" if 16 <= oc < 32 else ("V" if oc >= 32 else "Q")
        for pr in range(2):
            gp_info.append((oc, pr, kind))
            gp = oci * 2 + pr
            if kind in ("K", "Q"):
                rope_gps.append(gp)
            if kind in ("K", "V"):
                contrib_gps.append(gp)
    rope_idx = {gp: i for i, gp in enumerate(rope_gps)}
    contrib_idx = {gp: i for i, gp in enumerate(contrib_gps)}
    _kd = [0, 0]
    _vc = 0
    vci_of = {}
    _rel = {}
    for _ci, _gp in enumerate(contrib_gps):
        if gp_info[_gp][2] == "K":
            _kd[_ci % 2] += 1
            _rel[_ci] = ("K", _ci % 2, 16 * _kd[_ci % 2])
        else:
            vci_of[_gp] = _vc
            _rel[_ci] = ("VD", 16 * (_vc + 1))
            _vc += 1
    kv_release = {ci: _rel[ci - 2] for ci in range(2, 64)}
    pairs = [(h, qb, lp) for h in range(H) for qb in range(NQB) for lp in range(NPAIR)]

    def bar(eng, which):
        eng.sem_inc(sem["sPH"], 1)
        eng.wait_ge(sem["sPH"], 5 * which)

    with g:
        actx = contextlib.ExitStack()
        EA = actx.enter_context
        xnA = EA(nc.sbuf_tensor("xnA", [128, NDC * T], BF16))
        xcb = [EA(nc.sbuf_tensor(f"xcb{i}", [128, 512], BF16)) for i in range(2)]
        xsq = [EA(nc.sbuf_tensor(f"xsq{i}", [128, 512], BF16)) for i in range(2)]
        krawB = EA(nc.sbuf_tensor("krawB", [128, 1024], BF16))
        s1b = EA(nc.sbuf_tensor("s1b", [128, 1024], BF16))
        s2b = EA(nc.sbuf_tensor("s2b", [128, 1024], BF16))
        xt1 = [EA(nc.sbuf_tensor(f"xt1_{i}", [128, 512], BF16)) for i in range(2)]
        kvst = [EA(nc.sbuf_tensor(f"kvst{i}", [128, 1024], BF16)) for i in range(2)]
        abc = EA(nc.sbuf_tensor("abc", [128, 512], BF16))
        bbc = EA(nc.sbuf_tensor("bbc", [128, 512], BF16))
        cosT = EA(nc.sbuf_tensor("cosT", [128, T], BF16))
        sinT = EA(nc.sbuf_tensor("sinT", [128, T], BF16))
        psMix = EA(nc.psum_tensor("psMix", [128, 1024], F32))
        psQ = [EA(nc.psum_tensor(f"psQ{i}", [128, 1024], F32)) for i in range(2)]
        _sctx = contextlib.ExitStack()
        psAB = _sctx.enter_context(nc.psum_tensor("psAB", [1, 1024], F32))
        _sctx.close()
        psVtA = [EA(nc.psum_tensor("psVtA0", [128, 1024], BF16))]

        NCN = 12 * 16
        marks = {"xn_tb": []}
        xa = [0, 0]
        xa_dma, xa_t1m = {}, {}
        for u in range(32):
            sl = u % 2
            xa[sl] += 16
            xa_dma[u] = xa[sl]
            if u >= 16:
                for j in range(4):
                    xa[sl] += 8
                    xa_t1m[(u, j)] = xa[sl]
        xa_end = list(xa)
        for mt in range(16):
            sl = mt % 2
            xa[sl] += 16
            xa_dma[("C", mt)] = xa[sl]
        marks["xa_dma"] = xa_dma
        marks["xa_t1m"] = xa_t1m
        marks["xa_end"] = xa_end
        dvn = [0]
        N0W = 14

        def _is_cast(_o):
            return N0W <= _o < 48 and _o % 2 == 1

        _wl = [0, 0, 0]
        _wc = [0, 0, 0]
        wq_wait = {}
        for _o in range(80):
            if _is_cast(_o):
                _wc[_o % 3] += 1
                wq_wait[_o] = ("C", _wc[_o % 3])
            else:
                _wl[_o % 3] += 16
                wq_wait[_o] = ("L", _wl[_o % 3])
        cast_chunks = [c for c in range(48) if _is_cast(c)]
        cast_pos = {c: i for i, c in enumerate(cast_chunks)}
        stage_xw = {}

        def mkDV(v):
            def DV(inst):
                inst.then_inc(sem["sDV"], 1)
                dvn[0] += 1
                return inst
            return DV

        @block.gpsimd
        def _(gq):
            for dst, src in [(cosT, cosi[:, :]), (sinT, sini[:, :]),
                             (rtC, nc.inline_tensor(RTc, "c_rt").ap()),
                             (idC, nc.inline_tensor(IDc, "c_id").ap()),
                             (onC, nc.inline_tensor(ONESc, "c_on").ap()),
                             (onR, nc.inline_tensor(ONESRc, "c_or").ap()),
                             (e2C, nc.inline_tensor(E2c, "c_e2").ap()),
                             (wab, wa5.rearrange("p dc er -> p (dc er)")),
                             (wbwb, wbw[:, :]),
                             (rwb, rw5.rearrange("p dc e -> p (dc e)")),
                             (bbb, bbi[:, :]),
                             (epsC, nc.inline_tensor(np.full((1, 1), EPS, np.float32), "c_eps").ap())]:
                gq.dma_start(out=dst[:], in_=src).then_inc(sem["sCN"], 16)
            if PHASES == "A0":
                return
            agk_after = {8: 0}
            agv_after = {14: 0}
            for oci in range(48):
                if _is_cast(oci):
                    continue
                oc = chunk_order[oci]
                slot = oci % 3
                if oci >= 3:
                    gq.wait_ge(sem["sQm"], 2 * (oci - 2))
                gq.dma_start(out=wbuf[slot][:],
                             in_=wq5[:, oc, :, :].rearrange("p dc o -> p (dc o)")
                             ).then_inc(sWL[slot], 16)
                if oci in agk_after:
                    cg = agk_after[oci]
                    gq.wait_ge(sCIK[0], 64 * (cg + 1))
                    gq.wait_ge(sCIK[1], 64 * (cg + 1))
                    gq.collective_compute(
                        "AllGather", ALU.bypass, replica_groups=GROUPS,
                        ins=[cc_inK[512 * cg:512 * cg + 512, :].opt()],
                        outs=[cc_outK[2048 * cg:2048 * cg + 2048, :].opt()]
                    ).then_inc(sem["sCC"], 1)
                if oci in agv_after:
                    cg = agv_after[oci]
                    gq.wait_ge(sCIV[0], 16 * 8 * (cg + 1))
                    gq.collective_compute(
                        "AllGather", ALU.bypass, replica_groups=GROUPS,
                        ins=[cc_inV[2048 * cg:2048 * cg + 2048, :].opt()],
                        outs=[cc_outV[8192 * cg:8192 * cg + 8192, :].opt()]
                    ).then_inc(sem["sCC"], 1)
            for cg in range(1, 4):
                gq.wait_ge(sCIK[0], 64 * (cg + 1))
                gq.wait_ge(sCIK[1], 64 * (cg + 1))
                gq.collective_compute(
                    "AllGather", ALU.bypass, replica_groups=GROUPS,
                    ins=[cc_inK[512 * cg:512 * cg + 512, :].opt()],
                    outs=[cc_outK[2048 * cg:2048 * cg + 2048, :].opt()]
                ).then_inc(sem["sCC"], 1)
                gq.wait_ge(sCIV[0], 16 * 8 * (cg + 1))
                gq.collective_compute(
                    "AllGather", ALU.bypass, replica_groups=GROUPS,
                    ins=[cc_inV[2048 * cg:2048 * cg + 2048, :].opt()],
                    outs=[cc_outV[8192 * cg:8192 * cg + 8192, :].opt()]
                ).then_inc(sem["sCC"], 1)


        @block.sync
        def _(sy):
            for p in range(2):
                for tb in range(NTB):
                    for qd in range(4):
                        u = p * 16 + tb * 4 + qd
                        slot = u % 2
                        if 2 <= u < 18:
                            sy.wait_ge(sem["sXC"], 4 * (u - 1))
                            sy.wait_ge(sem["sXS"], 4 * (u - 1))
                        elif u >= 18:
                            sy.wait_ge(sXA[slot], marks["xa_t1m"][(u - 2, 3)])
                        sy.dma_start(out=xbuf[slot][:].rearrange("p (a b) -> p a b", a=4, b=512),
                                     in_=xh[:, 4 * qd:4 * qd + 4, 512 * tb:512 * tb + 512]
                                     ).then_inc(sXA[slot], 16)

            if PHASES == "A0":
                return
            sxw = [0, 0]
            for j in range(48):
                c = j + 2
                if _is_cast(c):
                    ip = cast_pos[c]
                    xs = ip % 2
                    if ip < 2:
                        sy.wait_ge(sXA[xs], marks["xa_t1m"][(30 + xs, 3)])
                    else:
                        pc = cast_chunks[ip - 2]
                        sy.wait_ge(sWC[pc % 3], wq_wait[pc][1])
                    sy.dma_start(out=xbuf[xs][:],
                                 in_=wq5[:, chunk_order[c], :, :]
                                 .rearrange("p dc o -> p (dc o)")
                                 ).then_inc(sXW[xs], 16)
                    sxw[xs] += 16
                    stage_xw[c] = sxw[xs]
                if j > 31:
                    continue
                for ci in (2 * j, 2 * j + 1):
                    gp = contrib_gps[ci]
                    oc, pr, kind = gp_info[gp]
                    slot = ci % 2
                    if kind == "K":
                        sy.wait_ge(sem["sKV"], ci + 1)
                        hh = oc - 16
                        row = 512 * (hh // 4) + 128 * (hh % 4)
                        sy.dma_start(out=cc_inK[row:row + 128,
                                                1024 * pr:1024 * pr + 1024],
                                     in_=kvst[slot][:]).then_inc(sCIK[slot], 16)
                    else:
                        vci = vci_of[gp]
                        sy.wait_ge(sem["sVB"], vci + 1)
                        hh = oc - 32
                        dst = cc_inV[2048 * (hh // 4) + 1024 * pr:
                                     2048 * (hh // 4) + 1024 * pr + 1024,
                                     128 * (hh % 4):128 * (hh % 4) + 128]
                        sy.dma_start(out=dst.rearrange("(kt p) d -> p kt d", p=128),
                                     in_=kvst[slot][:]
                                     .rearrange("p (kt d) -> p kt d", d=128)
                                     ).then_inc(sCIV[0], 16)

        @block.scalar
        def _(sc):
            for tb in range(NTB):
                for dc in range(NDC):
                    idx = tb * NDC + dc
                    u = tb * 4 + dc // 4
                    slot = u % 2
                    sc.wait_ge(sXA[slot], marks["xa_dma"][u])
                    if idx >= 2:
                        sc.wait_ge(sem["sXS"], idx - 1)
                    sc.activation(out=xsq[idx % 2][:],
                                  in_=xbuf[slot][:, 512 * (dc % 4):512 * (dc % 4) + 512],
                                  func=AF.Square).then_inc(sem["sSQ"], 1)
                sc.wait_ge(sem["sVAR"], tb + 1)
                if tb >= 1:
                    sc.wait_ge(sem["sSC"], tb)
                sc.activation(out=stq_[:], in_=stv_[:], func=AF.Sqrt,
                              bias=epsC[:]).then_inc(sem["sSQT"], 1)

        @block.vector
        def _(v):
            DV = mkDV(v)
            for tb in range(NTB):
                for dc in range(NDC):
                    idx = tb * NDC + dc
                    u = tb * 4 + dc // 4
                    slot = u % 2
                    v.wait_ge(sXA[slot], marks["xa_dma"][u])
                    if idx >= 2:
                        v.wait_ge(sem["sXC"], idx - 1)
                    v.tensor_copy(xcb[idx % 2][:],
                                  xbuf[slot][:, 512 * (dc % 4):512 * (dc % 4) + 512]
                                  ).then_inc(sem["sCB"], 1)
                v.wait_ge(sem["sXS"], 16 * (tb + 1))
                if tb >= 1:
                    v.wait_ge(sem["sSC"], tb)
                DV(v.tensor_scalar(stm_[:], psAB[0:1, 0:512], 1.0 / 2048.0, None, ALU.mult))
                v.wait_ge(sem["sDV"], dvn[0])
                v.tensor_scalar(stq_[:], psAB[0:1, 512:1024], 1.0 / 2048.0, None,
                                ALU.mult).then_inc(sem["sP2F"], 1)
                if tb >= 1:
                    v.wait_ge(sem["sSQT"], tb)
                DV(v.tensor_tensor(stv_[:], stm_[:], stm_[:], ALU.mult))
                v.wait_ge(sem["sDV"], dvn[0])
                v.wait_ge(sem["sP2F"], tb + 1)
                v.tensor_tensor(stv_[:], stq_[:], stv_[:], ALU.subtract
                                ).then_inc(sem["sVAR"], 1)
                v.wait_ge(sem["sSQT"], tb + 1)
                DV(v.reciprocal(stv_[:], stq_[:]))
                v.wait_ge(sem["sDV"], dvn[0])
                DV(v.tensor_copy(abR[0:1, 512 * tb:512 * tb + 512], stv_[:]))
                v.wait_ge(sem["sDV"], dvn[0])
                v.scalar_tensor_tensor(bR[0:1, 512 * tb:512 * tb + 512], stm_[:], -1.0,
                                       abR[0:1, 512 * tb:512 * tb + 512],
                                       ALU.mult, ALU.mult).then_inc(sem["sSC"], 1)
            for tb in range(NTB):
                v.wait_ge(sem["sBC"], 2 * (tb + 1))
                v.wait_ge(sem["sDV"], dvn[0])
                DV(v.tensor_copy(abc[:], psMix[:, 0:512]))
                v.tensor_copy(bbc[:], psMix[:, 512:1024]).then_inc(sem["sBCe"], 1)
                v.wait_ge(sem["sDV"], dvn[0])
                v.wait_ge(sem["sBCe"], tb + 1)
                for dc in range(NDC):
                    idx = 64 + tb * NDC + dc
                    u = 16 + tb * 4 + dc // 4
                    slot = u % 2
                    v.wait_ge(sXA[slot], marks["xa_dma"][u])
                    t1 = xt1[dc % 2]
                    if dc >= 2 or tb >= 1:
                        v.wait_ge(sem["sDV"], dvn[0] - 1)
                    v.tensor_tensor(t1[:],
                                    xbuf[slot][:, 512 * (dc % 4):512 * (dc % 4) + 512],
                                    abc[:], ALU.mult).then_inc(sXA[slot], 8)
                    v.wait_ge(sXA[slot], marks["xa_t1m"][(u, dc % 4)])
                    DV(v.tensor_tensor(
                        xnA[:, 2048 * dc + 512 * tb:2048 * dc + 512 * tb + 512],
                        t1[:], bbc[:], ALU.add))
                marks["xn_tb"].append(dvn[0])
            for gp in (range(96) if PHASES != "A0" else []):
                if gp >= 2 and (gp - 2) in vci_of:
                    vci2 = vci_of[gp - 2]
                    v.wait_ge(sem["sVA"], vci2 + 1)
                    v.tensor_copy(kvst[contrib_idx[gp - 2] % 2][:], psVtA[0][:]
                                  ).then_inc(sem["sVB"], 1)
                if gp % 2 == 1:
                    c = gp // 2 + 2
                    if _is_cast(c):
                        ws = c % 3
                        xs = cast_pos[c] % 2
                        v.wait_ge(sem["sQm"], 2 * (c - 2))
                        v.wait_ge(sXW[xs], stage_xw[c])
                        v.tensor_copy(wbuf[ws][:], xbuf[xs][:]
                                      ).then_inc(sWC[ws], 1)
                oc, pr, kind = gp_info[gp]
                v.wait_ge(sem["sQm"], gp + 1)
                if gp >= 1:
                    v.wait_ge(sem["sKV"], gp)
                if kind == "V":
                    ci = contrib_idx[gp]
                    slot = ci % 2
                    if ci >= 2:
                        rel = kv_release[ci]
                        if rel[0] == "K":
                            v.wait_ge(sCIK[rel[1]], rel[2])
                        else:
                            v.wait_ge(sCIV[0], rel[1])
                    v.tensor_copy(kvst[slot][:], psQ[gp % 2][:]).then_inc(sem["sKV"], 1)
                else:
                    ri = rope_idx[gp]
                    if ri >= 1:
                        v.wait_ge(sem["sRm"], ri)
                    v.tensor_copy(krawB[:], psQ[gp % 2][:]).then_inc(sem["sKR"], 1)
                    DV(v.tensor_tensor(s1b[:], psQ[gp % 2][:],
                                       cosT[:, 1024 * pr:1024 * pr + 1024], ALU.mult))
                    v.wait_ge(sem["sRm"], ri + 1)
                    v.tensor_tensor(s2b[:], psMix[:],
                                    sinT[:, 1024 * pr:1024 * pr + 1024], ALU.mult
                                    ).then_inc(sem["sRf"], 1)
                    v.wait_ge(sem["sDV"], dvn[0])
                    v.wait_ge(sem["sRf"], ri + 1)
                    if kind == "K":
                        ci = contrib_idx[gp]
                        slot = ci % 2
                        if ci >= 2:
                            rel = kv_release[ci]
                            if rel[0] == "K":
                                v.wait_ge(sCIK[rel[1]], rel[2])
                            else:
                                v.wait_ge(sCIV[0], rel[1])
                        dst = kvst[slot][:]
                    else:
                        dst = bufB[:, 2048 * oc + 1024 * pr:2048 * oc + 1024 * pr + 1024]
                    v.tensor_tensor(dst, s1b[:], s2b[:], ALU.add).then_inc(sem["sKV"], 1)

        @block.tensor
        def _(t):
            t.wait_ge(sem["sCN"], NCN)

            def stats_tb(tb):
                if tb >= 1:
                    t.wait_ge(sem["sP2F"], tb)
                for dc in range(NDC):
                    idx = tb * NDC + dc
                    t.wait_ge(sem["sCB"], idx + 1)
                    t.wait_ge(sem["sSQ"], idx + 1)
                    t.matmul(psAB[0:1, 0:512], onC[:], xcb[idx % 2][:],
                             start=(dc == 0), stop=(dc == 15),
                             skip_group_check=True).then_inc(sem["sXC"], 1)
                    t.matmul(psAB[0:1, 512:1024], onC[:], xsq[idx % 2][:],
                             start=(dc == 0), stop=(dc == 15),
                             skip_group_check=True).then_inc(sem["sXS"], 1)

            def bcast_tb(tb):
                t.wait_ge(sem["sSC"], tb + 1)
                if tb >= 1:
                    t.wait_ge(sem["sBCe"], tb)
                t.matmul(psMix[:, 0:512], onR[:], abR[0:1, 512 * tb:512 * tb + 512],
                         start=True, stop=True)
                t.matmul(psMix[:, 512:1024], onR[:], bR[0:1, 512 * tb:512 * tb + 512],
                         start=True, stop=True).then_inc(sem["sBC"], 2)

            for tb in range(NTB):
                stats_tb(tb)
            for tb in range(NTB):
                bcast_tb(tb)
            if PHASES == "A0":
                return
            pend_rope = []
            pend_v = []

            def emit_vtr(vgp):
                vci = vci_of[vgp]
                t.wait_ge(sem["sKV"], vgp + 1)
                if vci >= 1:
                    t.wait_ge(sem["sVB"], vci)
                kslot = contrib_idx[vgp] % 2
                for i8 in range(8):
                    tr = t.transpose(psVtA[0][:, 128 * i8:128 * i8 + 128],
                                     kvst[kslot][:, 128 * i8:128 * i8 + 128], idC[:])
                    if i8 == 7:
                        tr.then_inc(sem["sVA"], 1)

            def emit_rot(rgp):
                ri = rope_idx[rgp]
                t.wait_ge(sem["sKR"], ri + 1)
                if ri >= 1:
                    t.wait_ge(sem["sRf"], ri)
                t.matmul(psMix[:, 0:512], rtC[:], krawB[:, 0:512], start=True, stop=True)
                t.matmul(psMix[:, 512:1024], rtC[:], krawB[:, 512:1024],
                         start=True, stop=True).then_inc(sem["sRm"], 1)

            for oci, oc in enumerate(chunk_order):
                slot = oci % 3
                _tag, _val = wq_wait[oci]
                t.wait_ge(sWL[slot] if _tag == "L" else sWC[slot], _val)
                for pr in range(2):
                    gp = oci * 2 + pr
                    if oci == 0:
                        t.wait_ge(sem["sDV"], marks["xn_tb"][2 * pr + 1])
                    if gp >= 2:
                        t.wait_ge(sem["sKV"], gp - 1)
                    for tin in range(2):
                        tcol = 1024 * pr + 512 * tin
                        for dc in range(NDC):
                            mm = t.matmul(psQ[gp % 2][:, 512 * tin:512 * tin + 512],
                                          wbuf[slot][:, 128 * dc:128 * dc + 128],
                                          xnA[:, 2048 * dc + tcol:2048 * dc + tcol + 512],
                                          start=(dc == 0), stop=(dc == 15))
                            if tin == 1 and dc == NDC - 1:
                                mm.then_inc(sem["sQm"], 1)
                    if pend_rope:
                        emit_rot(pend_rope.pop(0))
                    if pend_v:
                        emit_vtr(pend_v.pop(0))
                    if gp in rope_idx:
                        pend_rope.append(gp)
                    if gp in vci_of:
                        pend_v.append(gp)
            while pend_rope:
                emit_rot(pend_rope.pop(0))
            while pend_v:
                emit_vtr(pend_v.pop(0))

        tapsA = (1 if "xn" in TAPS else 0)
        if PHASES != "A0":
            tapsA += (1 if "q" in TAPS else 0)

        @block.sync
        def _(sy):
            if "xn" in TAPS:
                sy.wait_ge(sem["sDV"], marks["xn_tb"][3])
                sy.dma_start(out=taps["xn"][:, :], in_=xnA[:]).then_inc(sem["sOD"], 16)
            if "q" in TAPS and PHASES != "A0":
                sy.wait_ge(sem["sKV"], 96)
                sy.dma_start(out=taps["q"][:, :], in_=bufB[:]).then_inc(sem["sOD"], 16)
            bar(sy, 1)
        for engname in ["gpsimd", "scalar", "vector", "tensor"]:
            getattr(block, engname)(lambda e: bar(e, 1))

        actx.close()

        if PHASES in ("A", "A0"):
            @block.sync
            def _(sy):
                sy.wait_ge(sem["sOD"], 16 * tapsA)
            return nc

        bctx = contextlib.ExitStack()
        EB = bctx.enter_context
        kbuf = [EB(nc.sbuf_tensor(f"kbuf{i}", [128, 8192], BF16)) for i in range(2)]
        vone = [EB(nc.sbuf_tensor(f"vone{i}", [128, 64 * 129], BF16)) for i in range(2)]
        ptb = [EB(nc.sbuf_tensor(f"ptb{i}", [128, 512], BF16)) for i in range(8)]
        onb = EB(nc.sbuf_tensor("onb", [128, 512], BF16))
        orawA = EB(nc.sbuf_tensor("orawA", [128, 258], F32))
        orawB = EB(nc.sbuf_tensor("orawB", [128, 258], F32))
        rlb = EB(nc.sbuf_tensor("rlb", [128, 4], F32))
        attst = [EB(nc.sbuf_tensor(f"attst{i}", [128, 2048], BF16)) for i in range(2)]
        psS = [EB(nc.psum_tensor(f"psS{i}", [128, 512], F32)) for i in range(4)]
        psOA = EB(nc.psum_tensor("psOA", [128, 258], F32))
        psOB = EB(nc.psum_tensor("psOB", [128, 258], F32))
        psT = EB(nc.psum_tensor("psT", [128, 512], BF16))

        @block.sync
        def _(sy):
            sy.wait_ge(sem["sKV"], 96)
            sy.wait_ge(sem["sRm"], 64)
            sy.wait_ge(sem["sSQT"], 4)

            def k_load(h):
                slot = h % 2
                sy.wait_ge(sem["sCC"], 2 * (h // 4) + 1)
                if h >= 2:
                    sy.wait_ge(sem["sS"], 256 * (h - 1))
                base = 2048 * (h // 4)
                for r in range(4):
                    sy.dma_start(out=kbuf[slot][:, 2048 * r:2048 * r + 2048],
                                 in_=cc_outK[base + 512 * r + 128 * (h % 4):
                                             base + 512 * r + 128 * (h % 4) + 128, :]
                                 ).then_inc(sKL[slot], 16)

            def v_load(ht):
                slot = ht % 2
                sy.wait_ge(sem["sCC"], 2 * (ht // 4) + 2)
                sy.wait_ge(sem["sVM"], slot + 1)
                if ht >= 2:
                    sy.wait_ge(sem["sPV"], 256 * (ht - 1))
                g2 = ht // 4
                dstv = vone[slot][:].rearrange("p (kt c) -> p kt c", kt=64, c=129)
                sy.dma_start(out=dstv[:, 0:32, 0:128],
                             in_=cc_outV[8192 * g2:8192 * g2 + 4096,
                                         128 * (ht % 4):128 * (ht % 4) + 128]
                             .rearrange("(kt p) d -> p kt d", p=128)
                             ).then_inc(sVL[slot], 16)

            k_load(0)
            k_load(1)
            v_load(0)
            for h in range(H):
                if h + 1 < H:
                    v_load(h + 1)
                if h + 2 < H:
                    k_load(h + 2)
                sy.wait_ge(sem["sPTf"], 4 * (h + 1))
                sy.dma_start(out=attd[128 * h:128 * h + 128, :],
                             in_=attst[h % 2][:]).then_inc(sASs[h % 2], 16)

        @block.scalar
        def _(sc):
            NK = H * NQB * 64
            for k in range(NK):
                sc.wait_ge(sem["sS"], k + 1)
                if k >= 8:
                    sc.wait_ge(sem["sPV"], k - 7)
                sc.activation(out=ptb[k % 8][:], in_=psS[k % 4][:],
                              func=AF.Exp, scale=SCALE).then_inc(sem["sE"], 1)

        @block.tensor
        def _(t):
            NK = H * NQB * 64
            LAG = 8
            kdesc = [(h, qb, lk) for h in range(H) for qb in range(NQB)
                     for lk in range(64)]
            ep_of_k = [h * NQB + qb for (h, qb, lk) in kdesc]
            for ki in range(NK + LAG):
                if ki >= LAG:
                    k = ki - LAG
                    e_h, e_qb, e_lk = kdesc[k]
                    ep = e_h * NQB + e_qb
                    if e_lk == 0 and e_qb == 0:
                        t.wait_ge(sVL[e_h % 2], 16 * (e_h // 2 + 1))
                        t.wait_ge(sVG[e_h % 2], 16 * (e_h // 2 + 1))
                    if e_lk == 0 and ep >= 1:
                        t.wait_ge(sem["sOn"], ep)
                    t.wait_ge(sem["sE"], k + 1)
                    for qt in range(4):
                        po = psOA if qt < 2 else psOB
                        col = 129 * (qt % 2)
                        mm = t.matmul(po[:, col:col + 129],
                                      ptb[k % 8][:, 128 * qt:128 * qt + 128],
                                      vone[e_h % 2][:, 129 * e_lk:129 * e_lk + 129],
                                      start=(e_lk == 0 and qt in (0, 2)),
                                      stop=(e_lk == 63), skip_group_check=True)
                        if qt == 3:
                            mm.then_inc(sem["sPV"], 1)
                    do_tr = None
                    if e_lk == 4 and k >= 64:
                        do_tr = ep_of_k[k] - 1
                    if do_tr is not None:
                        t.wait_ge(sem["sOnb"], do_tr + 1)
                        if do_tr >= 1:
                            t.wait_ge(sem["sPTf"], do_tr)
                        for qt in range(4):
                            tr = t.transpose(psT[:, 128 * qt:128 * qt + 128],
                                             onb[:, 128 * qt:128 * qt + 128], idC[:])
                            if qt == 3:
                                tr.then_inc(sem["sPT"], 1)
                if ki < NK:
                    h, qb, lk = kdesc[ki]
                    if qb == 0 and lk == 0:
                        t.wait_ge(sKL[h % 2], 64 * (h // 2 + 1))
                    if ki >= 4:
                        t.wait_ge(sem["sE"], ki - 3)
                    mm = t.matmul(psS[ki % 4][:],
                                  kbuf[h % 2][:, 128 * lk:128 * lk + 128],
                                  bufB[:, 2048 * h + 512 * qb:2048 * h + 512 * qb + 512],
                                  start=True, stop=True).then_inc(sem["sS"], 1)
            t.wait_ge(sem["sOnb"], NEP)
            t.wait_ge(sem["sPTf"], NEP - 1)
            for qt in range(4):
                tr = t.transpose(psT[:, 128 * qt:128 * qt + 128],
                                 onb[:, 128 * qt:128 * qt + 128], idC[:])
                if qt == 3:
                    tr.then_inc(sem["sPT"], 1)

        @block.vector
        def _(v):
            DV = mkDV(v)
            v.wait_ge(sem["sKV"], 96)
            v.memset(vone[0][:], 1.0).then_inc(sem["sVM"], 1)
            v.memset(vone[1][:], 1.0).then_inc(sem["sVM"], 1)

            for ep in range(NEP):
                h, qb = ep // NQB, ep % NQB
                v.wait_ge(sem["sPV"], 64 * (ep + 1))
                if qb == 0 and h >= 2:
                    v.wait_ge(sASs[h % 2], 16 * (h // 2))
                DV(v.tensor_copy(orawA[:], psOA[:]))
                v.tensor_copy(orawB[:], psOB[:]).then_inc(sem["sOn"], 1)
                v.wait_ge(sem["sDV"], dvn[0])
                v.wait_ge(sem["sOn"], ep + 1)
                DV(v.reciprocal(rlb[:, 0:2],
                                orawA[:].rearrange("p (a c) -> p a c", a=2, c=129)[:, :, 128]))
                DV(v.reciprocal(rlb[:, 2:4],
                                orawB[:].rearrange("p (a c) -> p a c", a=2, c=129)[:, :, 128]))
                v.wait_ge(sem["sDV"], dvn[0])
                for qt in range(4):
                    po = orawA if qt < 2 else orawB
                    col = 129 * (qt % 2)
                    mu = v.tensor_scalar(onb[:, 128 * qt:128 * qt + 128],
                                         po[:, col:col + 128], rlb[:, qt:qt + 1],
                                         None, ALU.mult)
                    if qt == 3:
                        mu.then_inc(sem["sOnb"], 1)
                v.wait_ge(sem["sPT"], ep + 1)
                v.tensor_copy(attst[h % 2][:, 512 * qb:512 * qb + 512],
                              psT[:]).then_inc(sem["sPTf"], 1)

        @block.sync
        def _(sy):
            if "att" in TAPS:
                sy.wait_ge(sASs[0], 128)
                sy.wait_ge(sASs[1], 128)
                sy.dma_start(out=taps["att"][:, :], in_=attd[:, :]).then_inc(sem["sOD"], 16)
            bar(sy, 2)

        @block.gpsimd
        def _(gq):
            for ht in range(H):
                slot = ht % 2
                gq.wait_ge(sem["sCC"], 2 * (ht // 4) + 2)
                gq.wait_ge(sem["sVM"], slot + 1)
                if ht >= 2:
                    gq.wait_ge(sem["sPV"], 256 * (ht - 1))
                g2 = ht // 4
                dstv = vone[slot][:].rearrange("p (kt c) -> p kt c", kt=64, c=129)
                gq.dma_start(out=dstv[:, 32:64, 0:128],
                             in_=cc_outV[8192 * g2 + 4096:8192 * g2 + 8192,
                                         128 * (ht % 4):128 * (ht % 4) + 128]
                             .rearrange("(kt p) d -> p kt d", p=128)
                             ).then_inc(sVG[slot], 16)

        @block.gpsimd
        def _(e):
            e.sem_inc(sem["sPH"], 1)
        for engname in ["scalar", "vector", "tensor"]:
            getattr(block, engname)(lambda e: bar(e, 2))

        bctx.close()

        if PHASES == "AB":
            tapsAB = tapsA + (1 if "att" in TAPS else 0)
            @block.sync
            def _(sy):
                sy.wait_ge(sem["sOD"], 16 * tapsAB)
            return nc

        cctx = contextlib.ExitStack()
        EC = cctx.enter_context
        bufC = EC(nc.sbuf_tensor("bufC", [128, NDC * T], BF16))
        xsqC = [EC(nc.sbuf_tensor(f"xsqC{i}", [128, 512], BF16)) for i in range(2)]
        ostg = [EC(nc.sbuf_tensor(f"ostg{i}", [128, 2048], F32)) for i in range(2)]
        xt1C = [EC(nc.sbuf_tensor("xt1C0", [128, 512], BF16))]
        abc2 = EC(nc.sbuf_tensor("abc2", [128, 512], BF16))
        bbc2 = EC(nc.sbuf_tensor("bbc2", [128, 512], BF16))
        lgb = EC(nc.sbuf_tensor("lgb", [128, 32], F32))
        mxb = EC(nc.sbuf_tensor("mxb", [128, 4], F32))
        mskb = EC(nc.sbuf_tensor("mskb", [128, 32], BF16))
        mskT = EC(nc.sbuf_tensor("mskT", [8, 512], BF16))
        m2b = EC(nc.sbuf_tensor("m2b", [128, 512], BF16))
        hmb = EC(nc.sbuf_tensor("hmb", [128, T], BF16))

        psY = [EC(nc.psum_tensor(f"psY{i}", [128, 512], F32)) for i in range(2)]
        psA2 = EC(nc.psum_tensor("psA2", [128, 512], F32))
        psB2s = EC(nc.psum_tensor("psB2s", [128, 512], F32))
        psMix2 = EC(nc.psum_tensor("psMix2", [128, 1024], F32))
        psMT = EC(nc.psum_tensor("psMT", [8, 512], BF16))
        psEx = EC(nc.psum_tensor("psEx", [128, 512], F32))
        psR = psA2
        psHx = psB2s
        psB2 = psY

        @block.sync
        def _(sy):
            sy.wait_ge(sASs[0], 128)
            sy.wait_ge(sASs[1], 128)
            sy.wait_ge(sem["sPTf"], 64)
            sy.wait_ge(sem["sPT"], 64)
            sy.wait_ge(sem["sE"], 4096)
            for oc in range(16):
                if oc >= 2:
                    sy.wait_ge(sALs[oc % 2], 16 * (oc // 2))
                sy.dma_start(out=bufC[:, 2048 * oc:2048 * oc + 2048],
                             in_=attd[128 * oc:128 * oc + 128, :]).then_inc(sALs[oc % 2], 16)
            for mt in range(16):
                slot = mt % 2
                if mt < 2:
                    sy.wait_ge(sXA[slot], marks["xa_end"][slot])
                else:
                    sy.wait_ge(sem["sYf"], 4 * (mt - 1))
                sy.dma_start(out=xbuf[slot][:], in_=xh[:, mt, :]
                             ).then_inc(sXA[slot], 16)
            for mt in range(16):
                sy.wait_ge(sem["sBf"], 4 * (mt + 1))
                sy.dma_start(out=outT[128 * mt:128 * mt + 128, :],
                             in_=ostg[mt % 2][:]).then_inc(sODs[mt % 2], 16)
            if "x1" in TAPS:
                sy.wait_ge(sem["sYf"], 64)
                sy.dma_start(out=taps["x1"][:, :], in_=bufB[:]).then_inc(sem["sOD"], 16)
            if "xn2" in TAPS:
                sy.wait_ge(sem["sXN2"], 64)
                sy.dma_start(out=taps["xn2"][:, :], in_=bufC[:]).then_inc(sem["sOD"], 16)
            if "hm" in TAPS:
                sy.wait_ge(sem["sHr"], 4)
                sy.dma_start(out=taps["hm"][:, :], in_=hmb[:]).then_inc(sem["sOD"], 16)
            sy.wait_ge(sODs[0], 128)
            sy.wait_ge(sODs[1], 128)
            if n_tap:
                sy.wait_ge(sem["sOD"], 16 * n_tap)

        @block.gpsimd
        def _(gq):
            for i, src in enumerate([wo5[:, mt, :, :] for mt in range(16)]
                                    + [wb5[:, mt, :, :] for mt in range(16)]):
                oci = 48 + i
                slot = oci % 3
                if i < 3:
                    gq.wait_ge(sem["sQm"], 2 * (oci - 2))
                elif i - 3 < 16:
                    gq.wait_ge(sem["sYm"], 4 * (i - 3 + 1))
                else:
                    gq.wait_ge(sem["sBm"], 4 * (i - 3 - 16 + 1))
                gq.dma_start(out=wbuf[slot][:],
                             in_=src.rearrange("p a b -> p (a b)")).then_inc(sWL[slot], 16)

        @block.tensor
        def _(t):
            for mt in range(16):
                oci = 48 + mt
                slot = oci % 3
                _tag, _val = wq_wait[oci]
                t.wait_ge(sWL[slot] if _tag == "L" else sWC[slot], _val)
                for qb in range(NQB):
                    idx = mt * 4 + qb
                    if idx >= 2:
                        t.wait_ge(sem["sYf"], idx - 1)
                    for oc in range(16):
                        if mt == 0 and qb == 0:
                            t.wait_ge(sALs[oc % 2], 16 * (oc // 2 + 1))
                        mm = t.matmul(psY[idx % 2][:],
                                      wbuf[slot][:, 128 * oc:128 * oc + 128],
                                      bufC[:, 2048 * oc + 512 * qb:2048 * oc + 512 * qb + 512],
                                      start=(oc == 0), stop=(oc == 15))
                        if oc == 15:
                            mm.then_inc(sem["sYm"], 1)
            t.wait_ge(sem["sYf"], 64)

            def stats2_tb(tb):
                if tb >= 1:
                    t.wait_ge(sem["sP2F"], 4 + tb)
                for dc in range(NDC):
                    idx = 64 + tb * NDC + dc
                    t.matmul(psA2[0:1, :], onC[:],
                             bufB[:, 2048 * dc + 512 * tb:2048 * dc + 512 * tb + 512],
                             start=(dc == 0), stop=(dc == 15))
                    t.wait_ge(sem["sSQ"], idx + 1)
                    t.matmul(psB2s[0:1, :], onC[:], xsqC[idx % 2][:],
                             start=(dc == 0), stop=(dc == 15)).then_inc(sem["sXS"], 1)

            def bcast2_tb(tb):
                t.wait_ge(sem["sSC"], 4 + tb + 1)
                if tb >= 1:
                    t.wait_ge(sem["sBCe"], 4 + tb)
                t.matmul(psMix2[:, 0:512], onR[:],
                         abR[0:1, 512 * tb:512 * tb + 512], start=True, stop=True)
                t.matmul(psMix2[:, 512:1024], onR[:],
                         bR[0:1, 512 * tb:512 * tb + 512],
                         start=True, stop=True).then_inc(sem["sBC"], 2)

            for tb in range(NTB):
                stats2_tb(tb)
            for tb in range(NTB):
                bcast2_tb(tb)
            for qb in range(NQB):
                t.wait_ge(sem["sXN2"], 16 * (qb + 1))
                if qb == 0:
                    t.wait_ge(sem["sSC"], 8)
                if qb >= 1:
                    t.wait_ge(sem["sLG"], qb)
                for qt in range(4):
                    for dc in range(NDC):
                        mm = t.matmul(psR[:, 8 * qt:8 * qt + 8],
                                      bufC[:, 2048 * dc + 512 * qb + 128 * qt:
                                            2048 * dc + 512 * qb + 128 * qt + 128],
                                      rwb[:, 8 * dc:8 * dc + 8],
                                      start=(dc == 0), stop=(dc == 15))
                        if qt == 3 and dc == NDC - 1:
                            mm.then_inc(sem["sRt"], 1)
                t.wait_ge(sem["sMsk"], qb + 1)
                if qb >= 1:
                    t.wait_ge(sem["sMTf"], qb)
                for qt in range(4):
                    tr = t.transpose(psMT[:, 128 * qt:128 * qt + 128],
                                     mskb[:, 8 * qt:8 * qt + 8], idC[:])
                    if qt == 3:
                        tr.then_inc(sem["sMT"], 1)
                t.wait_ge(sem["sMTf"], qb + 1)
                if qb >= 1:
                    t.wait_ge(sem["sM2"], qb)
                t.matmul(psEx[:], e2C[:], mskT[:], start=True, stop=True
                         ).then_inc(sem["sEm"], 1)
                if qb >= 1:
                    t.wait_ge(sem["sHr"], qb)
                for dc in range(NDC):
                    mm = t.matmul(psHx[:], wab[:, 128 * dc:128 * dc + 128],
                                  bufC[:, 2048 * dc + 512 * qb:2048 * dc + 512 * qb + 512],
                                  start=(dc == 0), stop=(dc == 15))
                    if dc == NDC - 1:
                        mm.then_inc(sem["sHm"], 1)
            for mt in range(16):
                oci = 64 + mt
                slot = oci % 3
                _tag, _val = wq_wait[oci]
                t.wait_ge(sWL[slot] if _tag == "L" else sWC[slot], _val)
                for qb in range(NQB):
                    idx = mt * 4 + qb
                    if idx < 2:
                        t.wait_ge(sem["sYf"], 64)
                    if idx >= 2:
                        t.wait_ge(sem["sBf"], idx - 1)
                    if mt == 0:
                        t.wait_ge(sem["sHr"], qb + 1)
                    for dc in range(NDC):
                        t.matmul(psB2[idx % 2][:],
                                 wbuf[slot][:, 128 * dc:128 * dc + 128],
                                 bufC[:, 2048 * dc + 512 * qb:2048 * dc + 512 * qb + 512],
                                 start=(dc == 0), stop=False)
                    t.matmul(psB2[idx % 2][:], wbwb[:, 128 * mt:128 * mt + 128],
                             hmb[:, 512 * qb:512 * qb + 512],
                             start=False, stop=True).then_inc(sem["sBm"], 1)

        @block.scalar
        def _(sc):
            sc.wait_ge(sem["sYf"], 64)
            for tb in range(NTB):
                for dc in range(NDC):
                    idx = 64 + tb * NDC + dc
                    sc.wait_ge(sem["sXS"], idx - 1)
                    sc.activation(out=xsqC[idx % 2][:],
                                  in_=bufB[:, 2048 * dc + 512 * tb:2048 * dc + 512 * tb + 512],
                                  func=AF.Square).then_inc(sem["sSQ"], 1)
                sc.wait_ge(sem["sVAR"], 4 + tb + 1)
                if tb >= 1:
                    sc.wait_ge(sem["sSC"], 4 + tb)
                sc.activation(out=stq_[:], in_=stv_[:], func=AF.Sqrt,
                              bias=epsC[:]).then_inc(sem["sSQT"], 1)

        @block.vector
        def _(v):
            DV = mkDV(v)
            for mt in range(16):
                for qb in range(NQB):
                    idx = mt * 4 + qb
                    v.wait_ge(sem["sYm"], idx + 1)
                    slot = mt % 2
                    v.wait_ge(sXA[slot], marks["xa_dma"][("C", mt)])
                    v.tensor_tensor(bufB[:, 2048 * mt + 512 * qb:2048 * mt + 512 * qb + 512],
                                    psY[idx % 2][:],
                                    xbuf[slot][:, 512 * qb:512 * qb + 512], ALU.add
                                    ).then_inc(sem["sYf"], 1)
            for tb in range(NTB):
                v.wait_ge(sem["sXS"], 64 + 16 * (tb + 1))
                v.wait_ge(sem["sSC"], 4 + tb)
                DV(v.tensor_scalar(stm_[:], psA2[0:1, :], 1.0 / 2048.0, None, ALU.mult))
                v.wait_ge(sem["sDV"], dvn[0])
                v.tensor_scalar(stq_[:], psB2s[0:1, :], 1.0 / 2048.0, None,
                                ALU.mult).then_inc(sem["sP2F"], 1)
                if tb >= 1:
                    v.wait_ge(sem["sSQT"], 4 + tb)
                DV(v.tensor_tensor(stv_[:], stm_[:], stm_[:], ALU.mult))
                v.wait_ge(sem["sDV"], dvn[0])
                v.wait_ge(sem["sP2F"], 4 + tb + 1)
                v.tensor_tensor(stv_[:], stq_[:], stv_[:], ALU.subtract
                                ).then_inc(sem["sVAR"], 1)
                v.wait_ge(sem["sSQT"], 4 + tb + 1)
                DV(v.reciprocal(stv_[:], stq_[:]))
                v.wait_ge(sem["sDV"], dvn[0])
                DV(v.tensor_copy(abR[0:1, 512 * tb:512 * tb + 512], stv_[:]))
                v.wait_ge(sem["sDV"], dvn[0])
                v.scalar_tensor_tensor(bR[0:1, 512 * tb:512 * tb + 512], stm_[:], -1.0,
                                       abR[0:1, 512 * tb:512 * tb + 512],
                                       ALU.mult, ALU.mult).then_inc(sem["sSC"], 1)
            v.wait_ge(sem["sYm"], 64)
            for tb in range(NTB):
                v.wait_ge(sem["sBC"], 8 + 2 * (tb + 1))
                v.wait_ge(sem["sDV"], dvn[0])
                v.wait_ge(sem["sXN2"], 16 * tb)
                DV(v.tensor_copy(abc2[:], psMix2[:, 0:512]))
                v.tensor_copy(bbc2[:], psMix2[:, 512:1024]).then_inc(sem["sBCe"], 1)
                v.wait_ge(sem["sBCe"], 4 + tb + 1)
                for dc in range(NDC):
                    cnt = tb * NDC + dc
                    t1 = xt1C[0]
                    if cnt >= 1:
                        v.wait_ge(sem["sXN2"], cnt)
                    DV(v.tensor_tensor(t1[:],
                                       bufB[:, 2048 * dc + 512 * tb:2048 * dc + 512 * tb + 512],
                                       abc2[:], ALU.mult))
                    v.wait_ge(sem["sDV"], dvn[0])
                    v.tensor_tensor(bufC[:, 2048 * dc + 512 * tb:2048 * dc + 512 * tb + 512],
                                    t1[:], bbc2[:], ALU.add
                                    ).then_inc(sem["sXN2"], 1)
            for qb in range(NQB):
                v.wait_ge(sem["sRt"], qb + 1)
                v.tensor_copy(lgb[:], psR[:, 0:32]).then_inc(sem["sLG"], 1)
                v.wait_ge(sem["sLG"], qb + 1)
                for qt in range(4):
                    DV(v.reduce_max(mxb[:, qt:qt + 1], lgb[:, 8 * qt:8 * qt + 8], axis=AX.X))
                v.wait_ge(sem["sDV"], dvn[0])
                for qt in range(4):
                    mk = v.tensor_scalar(mskb[:, 8 * qt:8 * qt + 8], lgb[:, 8 * qt:8 * qt + 8],
                                         mxb[:, qt:qt + 1], None, ALU.is_ge)
                    if qt == 3:
                        mk.then_inc(sem["sMsk"], 1)
                v.wait_ge(sem["sMT"], qb + 1)
                v.tensor_copy(mskT[:], psMT[:]).then_inc(sem["sMTf"], 1)
                v.wait_ge(sem["sEm"], qb + 1)
                v.tensor_copy(m2b[:], psEx[:]).then_inc(sem["sM2"], 1)
                v.wait_ge(sem["sHm"], qb + 1)
                v.wait_ge(sem["sM2"], qb + 1)
                v.tensor_tensor(hmb[:, 512 * qb:512 * qb + 512], psHx[:], m2b[:],
                                ALU.mult).then_inc(sem["sHr"], 1)
            for mt in range(16):
                for qb in range(NQB):
                    idx = mt * 4 + qb
                    v.wait_ge(sem["sBm"], idx + 1)
                    if qb == 0 and mt >= 2:
                        v.wait_ge(sODs[mt % 2], 16 * (mt // 2))
                    v.scalar_tensor_tensor(
                        ostg[mt % 2][:, 512 * qb:512 * qb + 512],
                        psB2[idx % 2][:], bbb[:, mt:mt + 1],
                        bufB[:, 2048 * mt + 512 * qb:2048 * mt + 512 * qb + 512],
                        ALU.add, ALU.add).then_inc(sem["sBf"], 1)

        cctx.close()
    return nc


def _host_layout(inputs):
    x = inputs["L0_attn_x"]
    w_qkv = inputs["L0_attn_w_qkv"]
    w_out = inputs["L0_attn_w_out"]
    W_base = inputs["L0_ffn_W_base"]
    b_base = inputs["L0_ffn_b_base"]
    router_W = inputs["L0_ffn_router_W"]
    A = inputs["L0_ffn_A"]
    Bw = inputs["L0_ffn_Bw"]

    wq5 = np.ascontiguousarray(
        np.asarray(w_qkv, np.float32).reshape(48, 128, 16, 128).transpose(3, 0, 2, 1))
    wo5 = np.ascontiguousarray(
        np.asarray(w_out, np.float32).reshape(16, 128, 16, 128).transpose(3, 0, 2, 1))
    wb5 = np.ascontiguousarray(
        np.asarray(W_base, np.float32).reshape(16, 128, 16, 128).transpose(3, 0, 2, 1))
    wa5 = np.ascontiguousarray(
        np.asarray(A, np.float32).reshape(128, 16, 128).transpose(2, 1, 0))
    wbwp = np.ascontiguousarray(
        np.transpose(np.asarray(Bw, np.float32), (0, 2, 1)).reshape(128, 2048))
    rw5 = np.ascontiguousarray(
        np.asarray(router_W, np.float32).T.reshape(16, 128, 8).transpose(1, 0, 2))
    bb = np.ascontiguousarray(np.asarray(b_base, np.float32).reshape(16, 128).T)

    inv_freq = (1.0 / np.float32(10000.0) **
                (np.arange(0, 128, 2, dtype=np.float32) / np.float32(128.0))).astype(np.float32)
    maps = []
    for c in range(8):
        b = c // 4
        j = c % 4
        ts0 = T * j
        xc = np.asarray(x[b, ts0:ts0 + T, :], np.float32)
        xhc = np.ascontiguousarray(xc.T.reshape(16, 128, T).transpose(1, 0, 2))
        tloc = np.arange(ts0, ts0 + T, dtype=np.float32)
        ang = np.concatenate([np.outer(tloc, inv_freq)] * 2, axis=1).astype(np.float32)
        cosTc = np.ascontiguousarray(np.cos(ang).T.astype(np.float32))
        sinTc = np.ascontiguousarray(np.sin(ang).T.astype(np.float32))
        maps.append({"xh": xhc, "wq5": wq5, "wo5": wo5, "wb5": wb5, "wa5": wa5,
                     "wbw": wbwp, "rw5": rw5, "bbi": bb, "cosi": cosTc, "sini": sinTc})
    return maps


def kernel(**inputs):
    maps = _host_layout(inputs)
    key = (frozenset(TAPS), PHASES)
    if key not in _NC_CACHE:
        _NC_CACHE[key] = build_program()
    nc = _NC_CACHE[key]
    res = run_bass_kernel_spmd(nc, maps, core_ids=list(range(8)))
    out = np.empty((B, S, D), np.float32)
    for c in range(8):
        b = c // 4
        j = c % 4
        out[b, T * j:T * j + T, :] = res.results[c]["outT"].T
    if TAPS:
        kernel.last_results = res.results
    return out


# version 20
# speedup vs baseline: 1.0380x; 1.0288x over previous
import math
import contextlib
import numpy as np
import ml_dtypes
import concourse.bass as bass
import concourse.mybir as mybir
from concourse.bass_utils import run_bass_kernel_spmd

F32 = mybir.dt.float32
BF16 = mybir.dt.bfloat16
AF = mybir.ActivationFunctionType
ALU = mybir.AluOpType
AX = mybir.AxisListType

B, S, D = 2, 8192, 2048
H, DH = 16, 128
T = 2048
GROUPS = [[0, 1, 2, 3], [4, 5, 6, 7]]
SCALE = 1.0 / math.sqrt(128.0)
EPS = 1e-5
NTB, NDC, NQB, NPAIR = 4, 16, 4, 32
NG = H * NQB * NPAIR
NEP = H * NQB
ALPHA_SCALING = 2.0

TAPS = set()
PHASES = "ABC"

_NC_CACHE = {}


def _consts():
    Rm = np.zeros((128, 128), np.float32)
    for d in range(64):
        Rm[d, d + 64] = -1.0
    for d in range(64, 128):
        Rm[d, d - 64] = 1.0
    RT = np.ascontiguousarray(Rm.T).astype(ml_dtypes.bfloat16)
    ident = np.eye(128, dtype=ml_dtypes.bfloat16)
    ones = np.ones((128, 1), dtype=ml_dtypes.bfloat16)
    onesrow = np.ones((1, 128), dtype=ml_dtypes.bfloat16)
    E2 = np.zeros((8, 128), dtype=ml_dtypes.bfloat16)
    for e in range(8):
        E2[e, 16 * e:16 * e + 16] = ALPHA_SCALING
    return RT, ident, ones, onesrow, E2


def build_program():
    nc = bass.Bass("TRN2", target_bir_lowering=False, debug=False)

    xh = nc.dram_tensor("xh", [128, NDC, T], F32, kind="ExternalInput").ap()
    wq5 = nc.dram_tensor("wq5", [128, 48, NDC, 128], F32, kind="ExternalInput").ap()
    wo5 = nc.dram_tensor("wo5", [128, 16, 16, 128], F32, kind="ExternalInput").ap()
    wb5 = nc.dram_tensor("wb5", [128, 16, NDC, 128], F32, kind="ExternalInput").ap()
    wa5 = nc.dram_tensor("wa5", [128, NDC, 128], F32, kind="ExternalInput").ap()
    wbw = nc.dram_tensor("wbw", [128, 2048], F32, kind="ExternalInput").ap()
    rw5 = nc.dram_tensor("rw5", [128, NDC, 8], F32, kind="ExternalInput").ap()
    bbi = nc.dram_tensor("bbi", [128, 16], F32, kind="ExternalInput").ap()
    cosi = nc.dram_tensor("cosi", [128, T], F32, kind="ExternalInput").ap()
    sini = nc.dram_tensor("sini", [128, T], F32, kind="ExternalInput").ap()
    outT = nc.dram_tensor("outT", [D, T], F32, kind="ExternalOutput").ap()

    cc_inK = nc.dram_tensor("cc_inK", [D, T], BF16, kind="Internal").ap()
    cc_outK = nc.dram_tensor("cc_outK", [4 * D, T], BF16, kind="Internal").ap()
    cc_inV = nc.dram_tensor("cc_inV", [4 * T, 512], BF16, kind="Internal").ap()
    cc_outV = nc.dram_tensor("cc_outV", [16 * T, 512], BF16, kind="Internal").ap()
    attd = nc.dram_tensor("attd", [D, T], BF16, kind="Internal").ap()

    RTc, IDc, ONESc, ONESRc, E2c = _consts()

    taps = {}
    for name, shape, dt in [("xn", [128, NDC * T], BF16), ("q", [128, H * T], BF16),
                            ("cc", [2 * D, T], BF16), ("att", [D, T], BF16),
                            ("x1", [128, 16 * T], BF16), ("xn2", [128, NDC * T], BF16),
                            ("hm", [128, T], BF16)]:
        if name in TAPS:
            taps[name] = nc.dram_tensor("t_" + name, shape, dt, kind="ExternalOutput").ap()
    n_tap = len(taps)
    n_tap_ab = sum(1 for nm in ("xn", "q", "cc", "att") if nm in TAPS)

    g = contextlib.ExitStack()
    EG = g.enter_context

    bufB = EG(nc.sbuf_tensor("bufB", [128, H * T], BF16))
    rtC = EG(nc.sbuf_tensor("rtC", [128, 128], BF16))
    idC = EG(nc.sbuf_tensor("idC", [128, 128], BF16))
    onC = EG(nc.sbuf_tensor("onC", [128, 1], BF16))
    onR = EG(nc.sbuf_tensor("onR", [1, 128], BF16))
    e2C = EG(nc.sbuf_tensor("e2C", [8, 128], BF16))
    wab = EG(nc.sbuf_tensor("wab", [128, 2048], BF16))
    wbwb = EG(nc.sbuf_tensor("wbwb", [128, 2048], BF16))
    rwb = EG(nc.sbuf_tensor("rwb", [128, NDC * 8], BF16))
    bbb = EG(nc.sbuf_tensor("bbb", [128, 16], F32))
    epsC = EG(nc.sbuf_tensor("epsC", [1, 1], F32))
    wbuf = [EG(nc.sbuf_tensor(f"wbuf{i}", [128, 2048], BF16)) for i in range(3)]
    xbuf = [EG(nc.sbuf_tensor(f"xbuf{i}", [128, 2048], F32)) for i in range(2)]
    abR = EG(nc.sbuf_tensor("abR", [1, T], BF16))
    bR = EG(nc.sbuf_tensor("bR", [1, T], BF16))
    stm_ = EG(nc.sbuf_tensor("stm_", [1, 512], F32))
    stq_ = EG(nc.sbuf_tensor("stq_", [1, 512], F32))
    stv_ = EG(nc.sbuf_tensor("stv_", [1, 512], F32))

    SN = ["sCN", "sDV", "sSQ", "sCB", "sXC", "sXS", "sST", "sVAR", "sSQT",
          "sSC", "sBC", "sXN", "sQm", "sQf", "sKR", "sRm", "sRf", "sKV", "sCI",
          "sCC", "sPH", "sVT", "sVTf", "sVR", "sS", "sE", "sPV", "sOn", "sOf",
          "sPT", "sPTf", "sAS", "sAL", "sYm", "sYf", "sX1", "sC1", "sLG",
          "sMsk", "sMT", "sMTf", "sEm", "sM2", "sHm", "sHr", "sBm", "sBf",
          "sOS", "sOD", "sXN2", "sRt", "sOnb", "sBCe", "sVA", "sVM", "sVB", "sP2F"]
    sem = {nm: EG(nc.semaphore(nm)) for nm in SN}
    sXA = [EG(nc.semaphore(f"sXA{i}")) for i in range(4)]
    sWL = [EG(nc.semaphore(f"sWL{i}")) for i in range(3)]
    sWF = [EG(nc.semaphore(f"sWF{i}")) for i in range(2)]
    sKL = [EG(nc.semaphore(f"sKL{i}")) for i in range(2)]
    sKF = [EG(nc.semaphore(f"sKF{i}")) for i in range(2)]
    sVL = [EG(nc.semaphore(f"sVL{i}")) for i in range(2)]
    sVG = [EG(nc.semaphore(f"sVG{i}")) for i in range(2)]
    sVF = [EG(nc.semaphore(f"sVF{i}")) for i in range(2)]
    sCIK = [EG(nc.semaphore(f"sCIK{i}")) for i in range(2)]
    sCIV = [EG(nc.semaphore(f"sCIV{i}")) for i in range(2)]
    sXW = [EG(nc.semaphore(f"sXW{i}")) for i in range(2)]
    sWC = [EG(nc.semaphore(f"sWC{i}")) for i in range(3)]
    sASs = [EG(nc.semaphore(f"sASs{i}")) for i in range(2)]
    sODs = [EG(nc.semaphore(f"sODs{i}")) for i in range(2)]
    sALs = [EG(nc.semaphore(f"sALs{i}")) for i in range(2)]

    block = EG(nc.Block())

    chunk_order = []
    for cg in range(4):
        chunk_order += [16 + 4 * cg + i for i in range(4)]
        chunk_order += [32 + 4 * cg + i for i in range(4)]
    chunk_order += list(range(16))
    gp_info, rope_gps, contrib_gps = [], [], []
    for oci, oc in enumerate(chunk_order):
        kind = "K" if 16 <= oc < 32 else ("V" if oc >= 32 else "Q")
        for pr in range(2):
            gp_info.append((oc, pr, kind))
            gp = oci * 2 + pr
            if kind in ("K", "Q"):
                rope_gps.append(gp)
            if kind in ("K", "V"):
                contrib_gps.append(gp)
    rope_idx = {gp: i for i, gp in enumerate(rope_gps)}
    contrib_idx = {gp: i for i, gp in enumerate(contrib_gps)}
    _kd = [0, 0]
    _vc = 0
    vci_of = {}
    _rel = {}
    for _ci, _gp in enumerate(contrib_gps):
        if gp_info[_gp][2] == "K":
            _kd[_ci % 2] += 1
            _rel[_ci] = ("K", _ci % 2, 16 * _kd[_ci % 2])
        else:
            vci_of[_gp] = _vc
            _rel[_ci] = ("VD", 16 * (_vc + 1))
            _vc += 1
    kv_release = {ci: _rel[ci - 2] for ci in range(2, 64)}
    pairs = [(h, qb, lp) for h in range(H) for qb in range(NQB) for lp in range(NPAIR)]

    def bar(eng, which):
        eng.sem_inc(sem["sPH"], 1)
        eng.wait_ge(sem["sPH"], 5 * which)

    with g:
        actx = contextlib.ExitStack()
        EA = actx.enter_context
        xnA = EA(nc.sbuf_tensor("xnA", [128, NDC * T], BF16))
        xcb = [EA(nc.sbuf_tensor(f"xcb{i}", [128, 512], BF16)) for i in range(2)]
        xsq = [EA(nc.sbuf_tensor(f"xsq{i}", [128, 512], BF16)) for i in range(2)]
        krawB = EA(nc.sbuf_tensor("krawB", [128, 1024], BF16))
        s1b = EA(nc.sbuf_tensor("s1b", [128, 1024], BF16))
        s2b = EA(nc.sbuf_tensor("s2b", [128, 1024], BF16))
        xt1 = [EA(nc.sbuf_tensor(f"xt1_{i}", [128, 512], BF16)) for i in range(2)]
        kvst = [EA(nc.sbuf_tensor(f"kvst{i}", [128, 1024], BF16)) for i in range(2)]
        abc = EA(nc.sbuf_tensor("abc", [128, 512], BF16))
        bbc = EA(nc.sbuf_tensor("bbc", [128, 512], BF16))
        cosT = EA(nc.sbuf_tensor("cosT", [128, T], BF16))
        sinT = EA(nc.sbuf_tensor("sinT", [128, T], BF16))
        psMix = EA(nc.psum_tensor("psMix", [128, 1024], F32))
        psQ = [EA(nc.psum_tensor(f"psQ{i}", [128, 1024], F32)) for i in range(2)]
        _sctx = contextlib.ExitStack()
        psAB = _sctx.enter_context(nc.psum_tensor("psAB", [1, 1024], F32))
        _sctx.close()
        psVtA = [EA(nc.psum_tensor("psVtA0", [128, 1024], BF16))]

        NCN = 12 * 16
        marks = {"xn_tb": []}
        xa = [0, 0]
        xa_dma, xa_t1m = {}, {}
        for u in range(32):
            sl = u % 2
            xa[sl] += 16
            xa_dma[u] = xa[sl]
            if u >= 16:
                for j in range(4):
                    xa[sl] += 8
                    xa_t1m[(u, j)] = xa[sl]
        xa_end = list(xa)
        for mt in range(16):
            sl = mt % 2
            xa[sl] += 16
            xa_dma[("C", mt)] = xa[sl]
        marks["xa_dma"] = xa_dma
        marks["xa_t1m"] = xa_t1m
        marks["xa_end"] = xa_end
        dvn = [0]
        N0W = 14

        def _is_cast(_o):
            return N0W <= _o < 48 and _o % 2 == 1

        _wl = [0, 0, 0]
        _wc = [0, 0, 0]
        wq_wait = {}
        for _o in range(80):
            if _is_cast(_o):
                _wc[_o % 3] += 1
                wq_wait[_o] = ("C", _wc[_o % 3])
            else:
                _wl[_o % 3] += 16
                wq_wait[_o] = ("L", _wl[_o % 3])
        cast_chunks = [c for c in range(48) if _is_cast(c)]
        cast_pos = {c: i for i, c in enumerate(cast_chunks)}
        stage_xw = {}

        def mkDV(v):
            def DV(inst):
                inst.then_inc(sem["sDV"], 1)
                dvn[0] += 1
                return inst
            return DV

        @block.gpsimd
        def _(gq):
            for dst, src in [(cosT, cosi[:, :]), (sinT, sini[:, :]),
                             (rtC, nc.inline_tensor(RTc, "c_rt").ap()),
                             (idC, nc.inline_tensor(IDc, "c_id").ap()),
                             (onC, nc.inline_tensor(ONESc, "c_on").ap()),
                             (onR, nc.inline_tensor(ONESRc, "c_or").ap()),
                             (e2C, nc.inline_tensor(E2c, "c_e2").ap()),
                             (wab, wa5.rearrange("p dc er -> p (dc er)")),
                             (wbwb, wbw[:, :]),
                             (rwb, rw5.rearrange("p dc e -> p (dc e)")),
                             (bbb, bbi[:, :]),
                             (epsC, nc.inline_tensor(np.full((1, 1), EPS, np.float32), "c_eps").ap())]:
                gq.dma_start(out=dst[:], in_=src).then_inc(sem["sCN"], 16)
            if PHASES == "A0":
                return
            agk_after = {8: 0}
            agv_after = {14: 0}
            for oci in range(48):
                if _is_cast(oci):
                    continue
                oc = chunk_order[oci]
                slot = oci % 3
                if oci >= 3:
                    gq.wait_ge(sem["sQm"], 2 * (oci - 2))
                gq.dma_start(out=wbuf[slot][:],
                             in_=wq5[:, oc, :, :].rearrange("p dc o -> p (dc o)")
                             ).then_inc(sWL[slot], 16)
                if oci in agk_after:
                    cg = agk_after[oci]
                    gq.wait_ge(sCIK[0], 64 * (cg + 1))
                    gq.wait_ge(sCIK[1], 64 * (cg + 1))
                    gq.collective_compute(
                        "AllGather", ALU.bypass, replica_groups=GROUPS,
                        ins=[cc_inK[512 * cg:512 * cg + 512, :].opt()],
                        outs=[cc_outK[2048 * cg:2048 * cg + 2048, :].opt()]
                    ).then_inc(sem["sCC"], 1)
                if oci in agv_after:
                    cg = agv_after[oci]
                    gq.wait_ge(sCIV[0], 16 * 8 * (cg + 1))
                    gq.collective_compute(
                        "AllGather", ALU.bypass, replica_groups=GROUPS,
                        ins=[cc_inV[2048 * cg:2048 * cg + 2048, :].opt()],
                        outs=[cc_outV[8192 * cg:8192 * cg + 8192, :].opt()]
                    ).then_inc(sem["sCC"], 1)
            for cg in range(1, 4):
                gq.wait_ge(sCIK[0], 64 * (cg + 1))
                gq.wait_ge(sCIK[1], 64 * (cg + 1))
                gq.collective_compute(
                    "AllGather", ALU.bypass, replica_groups=GROUPS,
                    ins=[cc_inK[512 * cg:512 * cg + 512, :].opt()],
                    outs=[cc_outK[2048 * cg:2048 * cg + 2048, :].opt()]
                ).then_inc(sem["sCC"], 1)
                gq.wait_ge(sCIV[0], 16 * 8 * (cg + 1))
                gq.collective_compute(
                    "AllGather", ALU.bypass, replica_groups=GROUPS,
                    ins=[cc_inV[2048 * cg:2048 * cg + 2048, :].opt()],
                    outs=[cc_outV[8192 * cg:8192 * cg + 8192, :].opt()]
                ).then_inc(sem["sCC"], 1)


        @block.sync
        def _(sy):
            for p in range(2):
                for tb in range(NTB):
                    for qd in range(4):
                        u = p * 16 + tb * 4 + qd
                        slot = u % 2
                        if 2 <= u < 18:
                            sy.wait_ge(sem["sXC"], 4 * (u - 1))
                            sy.wait_ge(sem["sXS"], 4 * (u - 1))
                        elif u >= 18:
                            sy.wait_ge(sXA[slot], marks["xa_t1m"][(u - 2, 3)])
                        sy.dma_start(out=xbuf[slot][:].rearrange("p (a b) -> p a b", a=4, b=512),
                                     in_=xh[:, 4 * qd:4 * qd + 4, 512 * tb:512 * tb + 512]
                                     ).then_inc(sXA[slot], 16)

            if PHASES == "A0":
                return
            sxw = [0, 0]
            for j in range(48):
                c = j + 2
                if _is_cast(c):
                    ip = cast_pos[c]
                    xs = ip % 2
                    if ip < 2:
                        sy.wait_ge(sXA[xs], marks["xa_t1m"][(30 + xs, 3)])
                    else:
                        pc = cast_chunks[ip - 2]
                        sy.wait_ge(sWC[pc % 3], wq_wait[pc][1])
                    sy.dma_start(out=xbuf[xs][:],
                                 in_=wq5[:, chunk_order[c], :, :]
                                 .rearrange("p dc o -> p (dc o)")
                                 ).then_inc(sXW[xs], 16)
                    sxw[xs] += 16
                    stage_xw[c] = sxw[xs]
                if j > 31:
                    continue
                for ci in (2 * j, 2 * j + 1):
                    gp = contrib_gps[ci]
                    oc, pr, kind = gp_info[gp]
                    slot = ci % 2
                    if kind == "K":
                        sy.wait_ge(sem["sKV"], ci + 1)
                        hh = oc - 16
                        row = 512 * (hh // 4) + 128 * (hh % 4)
                        sy.dma_start(out=cc_inK[row:row + 128,
                                                1024 * pr:1024 * pr + 1024],
                                     in_=kvst[slot][:]).then_inc(sCIK[slot], 16)
                    else:
                        vci = vci_of[gp]
                        sy.wait_ge(sem["sVB"], vci + 1)
                        hh = oc - 32
                        dst = cc_inV[2048 * (hh // 4) + 1024 * pr:
                                     2048 * (hh // 4) + 1024 * pr + 1024,
                                     128 * (hh % 4):128 * (hh % 4) + 128]
                        sy.dma_start(out=dst.rearrange("(kt p) d -> p kt d", p=128),
                                     in_=kvst[slot][:]
                                     .rearrange("p (kt d) -> p kt d", d=128)
                                     ).then_inc(sCIV[0], 16)

        @block.scalar
        def _(sc):
            for tb in range(NTB):
                for dc in range(NDC):
                    idx = tb * NDC + dc
                    u = tb * 4 + dc // 4
                    slot = u % 2
                    sc.wait_ge(sXA[slot], marks["xa_dma"][u])
                    if idx >= 2:
                        sc.wait_ge(sem["sXS"], idx - 1)
                    sc.activation(out=xsq[idx % 2][:],
                                  in_=xbuf[slot][:, 512 * (dc % 4):512 * (dc % 4) + 512],
                                  func=AF.Square).then_inc(sem["sSQ"], 1)
                sc.wait_ge(sem["sVAR"], tb + 1)
                if tb >= 1:
                    sc.wait_ge(sem["sSC"], tb)
                sc.activation(out=stq_[:], in_=stv_[:], func=AF.Sqrt,
                              bias=epsC[:]).then_inc(sem["sSQT"], 1)

        @block.vector
        def _(v):
            DV = mkDV(v)
            for tb in range(NTB):
                for dc in range(NDC):
                    idx = tb * NDC + dc
                    u = tb * 4 + dc // 4
                    slot = u % 2
                    v.wait_ge(sXA[slot], marks["xa_dma"][u])
                    if idx >= 2:
                        v.wait_ge(sem["sXC"], idx - 1)
                    v.tensor_copy(xcb[idx % 2][:],
                                  xbuf[slot][:, 512 * (dc % 4):512 * (dc % 4) + 512]
                                  ).then_inc(sem["sCB"], 1)
                v.wait_ge(sem["sXS"], 16 * (tb + 1))
                if tb >= 1:
                    v.wait_ge(sem["sSC"], tb)
                DV(v.tensor_scalar(stm_[:], psAB[0:1, 0:512], 1.0 / 2048.0, None, ALU.mult))
                v.wait_ge(sem["sDV"], dvn[0])
                v.tensor_scalar(stq_[:], psAB[0:1, 512:1024], 1.0 / 2048.0, None,
                                ALU.mult).then_inc(sem["sP2F"], 1)
                if tb >= 1:
                    v.wait_ge(sem["sSQT"], tb)
                DV(v.tensor_tensor(stv_[:], stm_[:], stm_[:], ALU.mult))
                v.wait_ge(sem["sDV"], dvn[0])
                v.wait_ge(sem["sP2F"], tb + 1)
                v.tensor_tensor(stv_[:], stq_[:], stv_[:], ALU.subtract
                                ).then_inc(sem["sVAR"], 1)
                v.wait_ge(sem["sSQT"], tb + 1)
                DV(v.reciprocal(stv_[:], stq_[:]))
                v.wait_ge(sem["sDV"], dvn[0])
                DV(v.tensor_copy(abR[0:1, 512 * tb:512 * tb + 512], stv_[:]))
                v.wait_ge(sem["sDV"], dvn[0])
                v.scalar_tensor_tensor(bR[0:1, 512 * tb:512 * tb + 512], stm_[:], -1.0,
                                       abR[0:1, 512 * tb:512 * tb + 512],
                                       ALU.mult, ALU.mult).then_inc(sem["sSC"], 1)
            for tb in range(NTB):
                v.wait_ge(sem["sBC"], 2 * (tb + 1))
                v.wait_ge(sem["sDV"], dvn[0])
                DV(v.tensor_copy(abc[:], psMix[:, 0:512]))
                v.tensor_copy(bbc[:], psMix[:, 512:1024]).then_inc(sem["sBCe"], 1)
                v.wait_ge(sem["sDV"], dvn[0])
                v.wait_ge(sem["sBCe"], tb + 1)
                for dc in range(NDC):
                    idx = 64 + tb * NDC + dc
                    u = 16 + tb * 4 + dc // 4
                    slot = u % 2
                    v.wait_ge(sXA[slot], marks["xa_dma"][u])
                    t1 = xt1[dc % 2]
                    if dc >= 2 or tb >= 1:
                        v.wait_ge(sem["sDV"], dvn[0] - 1)
                    v.tensor_tensor(t1[:],
                                    xbuf[slot][:, 512 * (dc % 4):512 * (dc % 4) + 512],
                                    abc[:], ALU.mult).then_inc(sXA[slot], 8)
                    v.wait_ge(sXA[slot], marks["xa_t1m"][(u, dc % 4)])
                    DV(v.tensor_tensor(
                        xnA[:, 2048 * dc + 512 * tb:2048 * dc + 512 * tb + 512],
                        t1[:], bbc[:], ALU.add))
                marks["xn_tb"].append(dvn[0])
            for gp in (range(96) if PHASES != "A0" else []):
                if gp >= 2 and (gp - 2) in vci_of:
                    vci2 = vci_of[gp - 2]
                    v.wait_ge(sem["sVA"], vci2 + 1)
                    v.tensor_copy(kvst[contrib_idx[gp - 2] % 2][:], psVtA[0][:]
                                  ).then_inc(sem["sVB"], 1)
                if gp % 2 == 1:
                    c = gp // 2 + 2
                    if _is_cast(c):
                        ws = c % 3
                        xs = cast_pos[c] % 2
                        v.wait_ge(sem["sQm"], 2 * (c - 2))
                        v.wait_ge(sXW[xs], stage_xw[c])
                        v.tensor_copy(wbuf[ws][:], xbuf[xs][:]
                                      ).then_inc(sWC[ws], 1)
                oc, pr, kind = gp_info[gp]
                v.wait_ge(sem["sQm"], gp + 1)
                if gp >= 1:
                    v.wait_ge(sem["sKV"], gp)
                if kind == "V":
                    ci = contrib_idx[gp]
                    slot = ci % 2
                    if ci >= 2:
                        rel = kv_release[ci]
                        if rel[0] == "K":
                            v.wait_ge(sCIK[rel[1]], rel[2])
                        else:
                            v.wait_ge(sCIV[0], rel[1])
                    v.tensor_copy(kvst[slot][:], psQ[gp % 2][:]).then_inc(sem["sKV"], 1)
                else:
                    ri = rope_idx[gp]
                    if ri >= 1:
                        v.wait_ge(sem["sRm"], ri)
                    v.tensor_copy(krawB[:], psQ[gp % 2][:]).then_inc(sem["sKR"], 1)
                    DV(v.tensor_tensor(s1b[:], psQ[gp % 2][:],
                                       cosT[:, 1024 * pr:1024 * pr + 1024], ALU.mult))
                    v.wait_ge(sem["sRm"], ri + 1)
                    v.tensor_tensor(s2b[:], psMix[:],
                                    sinT[:, 1024 * pr:1024 * pr + 1024], ALU.mult
                                    ).then_inc(sem["sRf"], 1)
                    v.wait_ge(sem["sDV"], dvn[0])
                    v.wait_ge(sem["sRf"], ri + 1)
                    if kind == "K":
                        ci = contrib_idx[gp]
                        slot = ci % 2
                        if ci >= 2:
                            rel = kv_release[ci]
                            if rel[0] == "K":
                                v.wait_ge(sCIK[rel[1]], rel[2])
                            else:
                                v.wait_ge(sCIV[0], rel[1])
                        dst = kvst[slot][:]
                    else:
                        dst = bufB[:, 2048 * oc + 1024 * pr:2048 * oc + 1024 * pr + 1024]
                    v.tensor_tensor(dst, s1b[:], s2b[:], ALU.add).then_inc(sem["sKV"], 1)

        @block.tensor
        def _(t):
            t.wait_ge(sem["sCN"], NCN)

            def stats_tb(tb):
                if tb >= 1:
                    t.wait_ge(sem["sP2F"], tb)
                for dc in range(NDC):
                    idx = tb * NDC + dc
                    t.wait_ge(sem["sCB"], idx + 1)
                    t.wait_ge(sem["sSQ"], idx + 1)
                    t.matmul(psAB[0:1, 0:512], onC[:], xcb[idx % 2][:],
                             start=(dc == 0), stop=(dc == 15),
                             skip_group_check=True).then_inc(sem["sXC"], 1)
                    t.matmul(psAB[0:1, 512:1024], onC[:], xsq[idx % 2][:],
                             start=(dc == 0), stop=(dc == 15),
                             skip_group_check=True).then_inc(sem["sXS"], 1)

            def bcast_tb(tb):
                t.wait_ge(sem["sSC"], tb + 1)
                if tb >= 1:
                    t.wait_ge(sem["sBCe"], tb)
                t.matmul(psMix[:, 0:512], onR[:], abR[0:1, 512 * tb:512 * tb + 512],
                         start=True, stop=True)
                t.matmul(psMix[:, 512:1024], onR[:], bR[0:1, 512 * tb:512 * tb + 512],
                         start=True, stop=True).then_inc(sem["sBC"], 2)

            for tb in range(NTB):
                stats_tb(tb)
            for tb in range(NTB):
                bcast_tb(tb)
            if PHASES == "A0":
                return
            pend_rope = []
            pend_v = []

            def emit_vtr(vgp):
                vci = vci_of[vgp]
                t.wait_ge(sem["sKV"], vgp + 1)
                if vci >= 1:
                    t.wait_ge(sem["sVB"], vci)
                kslot = contrib_idx[vgp] % 2
                for i8 in range(8):
                    tr = t.transpose(psVtA[0][:, 128 * i8:128 * i8 + 128],
                                     kvst[kslot][:, 128 * i8:128 * i8 + 128], idC[:])
                    if i8 == 7:
                        tr.then_inc(sem["sVA"], 1)

            def emit_rot(rgp):
                ri = rope_idx[rgp]
                t.wait_ge(sem["sKR"], ri + 1)
                if ri >= 1:
                    t.wait_ge(sem["sRf"], ri)
                t.matmul(psMix[:, 0:512], rtC[:], krawB[:, 0:512], start=True, stop=True)
                t.matmul(psMix[:, 512:1024], rtC[:], krawB[:, 512:1024],
                         start=True, stop=True).then_inc(sem["sRm"], 1)

            for oci, oc in enumerate(chunk_order):
                slot = oci % 3
                _tag, _val = wq_wait[oci]
                t.wait_ge(sWL[slot] if _tag == "L" else sWC[slot], _val)
                for pr in range(2):
                    gp = oci * 2 + pr
                    if oci == 0:
                        t.wait_ge(sem["sDV"], marks["xn_tb"][2 * pr + 1])
                    if gp >= 2:
                        t.wait_ge(sem["sKV"], gp - 1)
                    for tin in range(2):
                        tcol = 1024 * pr + 512 * tin
                        for dc in range(NDC):
                            mm = t.matmul(psQ[gp % 2][:, 512 * tin:512 * tin + 512],
                                          wbuf[slot][:, 128 * dc:128 * dc + 128],
                                          xnA[:, 2048 * dc + tcol:2048 * dc + tcol + 512],
                                          start=(dc == 0), stop=(dc == 15))
                            if tin == 1 and dc == NDC - 1:
                                mm.then_inc(sem["sQm"], 1)
                    if pend_rope:
                        emit_rot(pend_rope.pop(0))
                    if pend_v:
                        emit_vtr(pend_v.pop(0))
                    if gp in rope_idx:
                        pend_rope.append(gp)
                    if gp in vci_of:
                        pend_v.append(gp)
            while pend_rope:
                emit_rot(pend_rope.pop(0))
            while pend_v:
                emit_vtr(pend_v.pop(0))

        tapsA = (1 if "xn" in TAPS else 0)
        if PHASES != "A0":
            tapsA += (1 if "q" in TAPS else 0)

        @block.sync
        def _(sy):
            if "xn" in TAPS:
                sy.wait_ge(sem["sDV"], marks["xn_tb"][3])
                sy.dma_start(out=taps["xn"][:, :], in_=xnA[:]).then_inc(sem["sOD"], 16)
            if "q" in TAPS and PHASES != "A0":
                sy.wait_ge(sem["sKV"], 96)
                sy.dma_start(out=taps["q"][:, :], in_=bufB[:]).then_inc(sem["sOD"], 16)
            bar(sy, 1)
        for engname in ["gpsimd", "scalar", "vector", "tensor"]:
            getattr(block, engname)(lambda e: bar(e, 1))

        actx.close()

        if PHASES in ("A", "A0"):
            @block.sync
            def _(sy):
                sy.wait_ge(sem["sOD"], 16 * tapsA)
            return nc

        bctx = contextlib.ExitStack()
        EB = bctx.enter_context
        kbuf = [EB(nc.sbuf_tensor(f"kbuf{i}", [128, 8192], BF16)) for i in range(2)]
        vone = [EB(nc.sbuf_tensor(f"vone{i}", [128, 64 * 129], BF16)) for i in range(2)]
        ptb = [EB(nc.sbuf_tensor(f"ptb{i}", [128, 512], BF16)) for i in range(8)]
        onb = EB(nc.sbuf_tensor("onb", [128, 512], BF16))
        orawA = EB(nc.sbuf_tensor("orawA", [128, 258], F32))
        orawB = EB(nc.sbuf_tensor("orawB", [128, 258], F32))
        rlb = EB(nc.sbuf_tensor("rlb", [128, 4], F32))
        attst = [EB(nc.sbuf_tensor(f"attst{i}", [128, 2048], BF16)) for i in range(2)]
        psS = [EB(nc.psum_tensor(f"psS{i}", [128, 512], F32)) for i in range(5)]
        psOA = EB(nc.psum_tensor("psOA", [128, 258], F32))
        psOB = EB(nc.psum_tensor("psOB", [128, 258], F32))
        psT = EB(nc.psum_tensor("psT", [128, 512], BF16))

        @block.sync
        def _(sy):
            sy.wait_ge(sem["sKV"], 96)
            sy.wait_ge(sem["sRm"], 64)
            sy.wait_ge(sem["sSQT"], 4)

            def k_load(h):
                slot = h % 2
                sy.wait_ge(sem["sCC"], 2 * (h // 4) + 1)
                if h >= 2:
                    sy.wait_ge(sem["sS"], 256 * (h - 1))
                base = 2048 * (h // 4)
                for r in range(4):
                    sy.dma_start(out=kbuf[slot][:, 2048 * r:2048 * r + 2048],
                                 in_=cc_outK[base + 512 * r + 128 * (h % 4):
                                             base + 512 * r + 128 * (h % 4) + 128, :]
                                 ).then_inc(sKL[slot], 16)

            def v_load(ht):
                slot = ht % 2
                sy.wait_ge(sem["sCC"], 2 * (ht // 4) + 2)
                sy.wait_ge(sem["sVM"], slot + 1)
                if ht >= 2:
                    sy.wait_ge(sem["sPV"], 256 * (ht - 1))
                g2 = ht // 4
                dstv = vone[slot][:].rearrange("p (kt c) -> p kt c", kt=64, c=129)
                sy.dma_start(out=dstv[:, 0:32, 0:128],
                             in_=cc_outV[8192 * g2:8192 * g2 + 4096,
                                         128 * (ht % 4):128 * (ht % 4) + 128]
                             .rearrange("(kt p) d -> p kt d", p=128)
                             ).then_inc(sVL[slot], 16)

            k_load(0)
            k_load(1)
            v_load(0)
            for h in range(H):
                if h + 1 < H:
                    v_load(h + 1)
                if h + 2 < H:
                    k_load(h + 2)
                sy.wait_ge(sem["sPTf"], 4 * (h + 1))
                sy.dma_start(out=attd[128 * h:128 * h + 128, :],
                             in_=attst[h % 2][:]).then_inc(sASs[h % 2], 16)

        @block.scalar
        def _(sc):
            NK = H * NQB * 64
            for k in range(NK):
                sc.wait_ge(sem["sS"], k + 1)
                if k >= 8:
                    sc.wait_ge(sem["sPV"], k - 7)
                sc.activation(out=ptb[k % 8][:], in_=psS[k % 5][:],
                              func=AF.Exp, scale=SCALE).then_inc(sem["sE"], 1)

        @block.tensor
        def _(t):
            NK = H * NQB * 64
            LAG = 8
            kdesc = [(h, qb, lk) for h in range(H) for qb in range(NQB)
                     for lk in range(64)]
            ep_of_k = [h * NQB + qb for (h, qb, lk) in kdesc]
            for ki in range(NK + LAG):
                if ki >= LAG:
                    k = ki - LAG
                    e_h, e_qb, e_lk = kdesc[k]
                    ep = e_h * NQB + e_qb
                    if e_lk == 0 and e_qb == 0:
                        t.wait_ge(sVL[e_h % 2], 16 * (e_h // 2 + 1))
                        t.wait_ge(sVG[e_h % 2], 16 * (e_h // 2 + 1))
                    if e_lk == 0 and ep >= 1:
                        t.wait_ge(sem["sOn"], ep)
                    t.wait_ge(sem["sE"], k + 1)
                    for qt in range(4):
                        po = psOA if qt < 2 else psOB
                        col = 129 * (qt % 2)
                        mm = t.matmul(po[:, col:col + 129],
                                      ptb[k % 8][:, 128 * qt:128 * qt + 128],
                                      vone[e_h % 2][:, 129 * e_lk:129 * e_lk + 129],
                                      start=(e_lk == 0 and qt in (0, 2)),
                                      stop=(e_lk == 63), skip_group_check=True)
                        if qt == 3:
                            mm.then_inc(sem["sPV"], 1)
                    do_tr = None
                    if e_lk == 4 and k >= 64:
                        do_tr = ep_of_k[k] - 1
                    if do_tr is not None:
                        t.wait_ge(sem["sOnb"], do_tr + 1)
                        if do_tr >= 1:
                            t.wait_ge(sem["sPTf"], do_tr)
                        for qt in range(4):
                            tr = t.transpose(psT[:, 128 * qt:128 * qt + 128],
                                             onb[:, 128 * qt:128 * qt + 128], idC[:])
                            if qt == 3:
                                tr.then_inc(sem["sPT"], 1)
                if ki < NK:
                    h, qb, lk = kdesc[ki]
                    if qb == 0 and lk == 0:
                        t.wait_ge(sKL[h % 2], 64 * (h // 2 + 1))
                    if ki >= 5:
                        t.wait_ge(sem["sE"], ki - 4)
                    mm = t.matmul(psS[ki % 5][:],
                                  kbuf[h % 2][:, 128 * lk:128 * lk + 128],
                                  bufB[:, 2048 * h + 512 * qb:2048 * h + 512 * qb + 512],
                                  start=True, stop=True).then_inc(sem["sS"], 1)
            t.wait_ge(sem["sOnb"], NEP)
            t.wait_ge(sem["sPTf"], NEP - 1)
            for qt in range(4):
                tr = t.transpose(psT[:, 128 * qt:128 * qt + 128],
                                 onb[:, 128 * qt:128 * qt + 128], idC[:])
                if qt == 3:
                    tr.then_inc(sem["sPT"], 1)

        @block.vector
        def _(v):
            DV = mkDV(v)
            v.wait_ge(sem["sKV"], 96)
            v.memset(vone[0][:], 1.0).then_inc(sem["sVM"], 1)
            v.memset(vone[1][:], 1.0).then_inc(sem["sVM"], 1)

            for ep in range(NEP):
                h, qb = ep // NQB, ep % NQB
                v.wait_ge(sem["sPV"], 64 * (ep + 1))
                if qb == 0 and h >= 2:
                    v.wait_ge(sASs[h % 2], 16 * (h // 2))
                DV(v.tensor_copy(orawA[:], psOA[:]))
                v.tensor_copy(orawB[:], psOB[:]).then_inc(sem["sOn"], 1)
                v.wait_ge(sem["sDV"], dvn[0])
                v.wait_ge(sem["sOn"], ep + 1)
                DV(v.reciprocal(rlb[:, 0:2],
                                orawA[:].rearrange("p (a c) -> p a c", a=2, c=129)[:, :, 128]))
                DV(v.reciprocal(rlb[:, 2:4],
                                orawB[:].rearrange("p (a c) -> p a c", a=2, c=129)[:, :, 128]))
                v.wait_ge(sem["sDV"], dvn[0])
                for qt in range(4):
                    po = orawA if qt < 2 else orawB
                    col = 129 * (qt % 2)
                    mu = v.tensor_scalar(onb[:, 128 * qt:128 * qt + 128],
                                         po[:, col:col + 128], rlb[:, qt:qt + 1],
                                         None, ALU.mult)
                    if qt == 3:
                        mu.then_inc(sem["sOnb"], 1)
                v.wait_ge(sem["sPT"], ep + 1)
                v.tensor_copy(attst[h % 2][:, 512 * qb:512 * qb + 512],
                              psT[:]).then_inc(sem["sPTf"], 1)

        @block.sync
        def _(sy):
            if "att" in TAPS:
                sy.wait_ge(sASs[0], 128)
                sy.wait_ge(sASs[1], 128)
                sy.dma_start(out=taps["att"][:, :], in_=attd[:, :]).then_inc(sem["sOD"], 16)
            bar(sy, 2)

        @block.gpsimd
        def _(gq):
            for ht in range(H):
                slot = ht % 2
                gq.wait_ge(sem["sCC"], 2 * (ht // 4) + 2)
                gq.wait_ge(sem["sVM"], slot + 1)
                if ht >= 2:
                    gq.wait_ge(sem["sPV"], 256 * (ht - 1))
                g2 = ht // 4
                dstv = vone[slot][:].rearrange("p (kt c) -> p kt c", kt=64, c=129)
                gq.dma_start(out=dstv[:, 32:64, 0:128],
                             in_=cc_outV[8192 * g2 + 4096:8192 * g2 + 8192,
                                         128 * (ht % 4):128 * (ht % 4) + 128]
                             .rearrange("(kt p) d -> p kt d", p=128)
                             ).then_inc(sVG[slot], 16)

        @block.gpsimd
        def _(e):
            e.sem_inc(sem["sPH"], 1)
        for engname in ["scalar", "vector", "tensor"]:
            getattr(block, engname)(lambda e: bar(e, 2))

        bctx.close()

        if PHASES == "AB":
            tapsAB = tapsA + (1 if "att" in TAPS else 0)
            @block.sync
            def _(sy):
                sy.wait_ge(sem["sOD"], 16 * tapsAB)
            return nc

        cctx = contextlib.ExitStack()
        EC = cctx.enter_context
        bufC = EC(nc.sbuf_tensor("bufC", [128, NDC * T], BF16))
        xsqC = [EC(nc.sbuf_tensor(f"xsqC{i}", [128, 512], BF16)) for i in range(2)]
        ostg = [EC(nc.sbuf_tensor(f"ostg{i}", [128, 2048], F32)) for i in range(2)]
        xt1C = [EC(nc.sbuf_tensor("xt1C0", [128, 512], BF16))]
        abc2 = EC(nc.sbuf_tensor("abc2", [128, 512], BF16))
        bbc2 = EC(nc.sbuf_tensor("bbc2", [128, 512], BF16))
        lgb = EC(nc.sbuf_tensor("lgb", [128, 32], F32))
        mxb = EC(nc.sbuf_tensor("mxb", [128, 4], F32))
        mskb = EC(nc.sbuf_tensor("mskb", [128, 32], BF16))
        mskT = EC(nc.sbuf_tensor("mskT", [8, 512], BF16))
        m2b = EC(nc.sbuf_tensor("m2b", [128, 512], BF16))
        hmb = EC(nc.sbuf_tensor("hmb", [128, T], BF16))

        psY = [EC(nc.psum_tensor(f"psY{i}", [128, 512], F32)) for i in range(2)]
        psA2 = EC(nc.psum_tensor("psA2", [128, 512], F32))
        psB2s = EC(nc.psum_tensor("psB2s", [128, 512], F32))
        psMix2 = EC(nc.psum_tensor("psMix2", [128, 1024], F32))
        psMT = EC(nc.psum_tensor("psMT", [8, 512], BF16))
        psEx = EC(nc.psum_tensor("psEx", [128, 512], F32))
        psR = psA2
        psHx = psB2s
        psB2 = psY

        @block.sync
        def _(sy):
            sy.wait_ge(sASs[0], 128)
            sy.wait_ge(sASs[1], 128)
            sy.wait_ge(sem["sPTf"], 64)
            sy.wait_ge(sem["sPT"], 64)
            sy.wait_ge(sem["sE"], 4096)
            for oc in range(16):
                if oc >= 2:
                    sy.wait_ge(sALs[oc % 2], 16 * (oc // 2))
                sy.dma_start(out=bufC[:, 2048 * oc:2048 * oc + 2048],
                             in_=attd[128 * oc:128 * oc + 128, :]).then_inc(sALs[oc % 2], 16)
            for mt in range(16):
                slot = mt % 2
                if mt < 2:
                    sy.wait_ge(sXA[slot], marks["xa_end"][slot])
                else:
                    sy.wait_ge(sem["sYf"], 4 * (mt - 1))
                sy.dma_start(out=xbuf[slot][:], in_=xh[:, mt, :]
                             ).then_inc(sXA[slot], 16)
            for mt in range(16):
                sy.wait_ge(sem["sBf"], 4 * (mt + 1))
                sy.dma_start(out=outT[128 * mt:128 * mt + 128, :],
                             in_=ostg[mt % 2][:]).then_inc(sODs[mt % 2], 16)
            if "x1" in TAPS:
                sy.wait_ge(sem["sYf"], 64)
                sy.dma_start(out=taps["x1"][:, :], in_=bufB[:]).then_inc(sem["sOD"], 16)
            if "xn2" in TAPS:
                sy.wait_ge(sem["sXN2"], 64)
                sy.dma_start(out=taps["xn2"][:, :], in_=bufC[:]).then_inc(sem["sOD"], 16)
            if "hm" in TAPS:
                sy.wait_ge(sem["sHr"], 4)
                sy.dma_start(out=taps["hm"][:, :], in_=hmb[:]).then_inc(sem["sOD"], 16)
            sy.wait_ge(sODs[0], 128)
            sy.wait_ge(sODs[1], 128)
            if n_tap:
                sy.wait_ge(sem["sOD"], 16 * n_tap)

        @block.gpsimd
        def _(gq):
            for i, src in enumerate([wo5[:, mt, :, :] for mt in range(16)]
                                    + [wb5[:, mt, :, :] for mt in range(16)]):
                oci = 48 + i
                slot = oci % 3
                if i < 3:
                    gq.wait_ge(sem["sQm"], 2 * (oci - 2))
                elif i - 3 < 16:
                    gq.wait_ge(sem["sYm"], 4 * (i - 3 + 1))
                else:
                    gq.wait_ge(sem["sBm"], 4 * (i - 3 - 16 + 1))
                gq.dma_start(out=wbuf[slot][:],
                             in_=src.rearrange("p a b -> p (a b)")).then_inc(sWL[slot], 16)

        @block.tensor
        def _(t):
            for mt in range(16):
                oci = 48 + mt
                slot = oci % 3
                _tag, _val = wq_wait[oci]
                t.wait_ge(sWL[slot] if _tag == "L" else sWC[slot], _val)
                for qb in range(NQB):
                    idx = mt * 4 + qb
                    if idx >= 2:
                        t.wait_ge(sem["sYf"], idx - 1)
                    for oc in range(16):
                        if mt == 0 and qb == 0:
                            t.wait_ge(sALs[oc % 2], 16 * (oc // 2 + 1))
                        mm = t.matmul(psY[idx % 2][:],
                                      wbuf[slot][:, 128 * oc:128 * oc + 128],
                                      bufC[:, 2048 * oc + 512 * qb:2048 * oc + 512 * qb + 512],
                                      start=(oc == 0), stop=(oc == 15))
                        if oc == 15:
                            mm.then_inc(sem["sYm"], 1)
            t.wait_ge(sem["sYf"], 64)

            def stats2_tb(tb):
                if tb >= 1:
                    t.wait_ge(sem["sP2F"], 4 + tb)
                for dc in range(NDC):
                    idx = 64 + tb * NDC + dc
                    t.matmul(psA2[0:1, :], onC[:],
                             bufB[:, 2048 * dc + 512 * tb:2048 * dc + 512 * tb + 512],
                             start=(dc == 0), stop=(dc == 15))
                    t.wait_ge(sem["sSQ"], idx + 1)
                    t.matmul(psB2s[0:1, :], onC[:], xsqC[idx % 2][:],
                             start=(dc == 0), stop=(dc == 15)).then_inc(sem["sXS"], 1)

            def bcast2_tb(tb):
                t.wait_ge(sem["sSC"], 4 + tb + 1)
                if tb >= 1:
                    t.wait_ge(sem["sBCe"], 4 + tb)
                t.matmul(psMix2[:, 0:512], onR[:],
                         abR[0:1, 512 * tb:512 * tb + 512], start=True, stop=True)
                t.matmul(psMix2[:, 512:1024], onR[:],
                         bR[0:1, 512 * tb:512 * tb + 512],
                         start=True, stop=True).then_inc(sem["sBC"], 2)

            for tb in range(NTB):
                stats2_tb(tb)
            for tb in range(NTB):
                bcast2_tb(tb)
            for qb in range(NQB):
                t.wait_ge(sem["sXN2"], 16 * (qb + 1))
                if qb == 0:
                    t.wait_ge(sem["sSC"], 8)
                if qb >= 1:
                    t.wait_ge(sem["sLG"], qb)
                for qt in range(4):
                    for dc in range(NDC):
                        mm = t.matmul(psR[:, 8 * qt:8 * qt + 8],
                                      bufC[:, 2048 * dc + 512 * qb + 128 * qt:
                                            2048 * dc + 512 * qb + 128 * qt + 128],
                                      rwb[:, 8 * dc:8 * dc + 8],
                                      start=(dc == 0), stop=(dc == 15))
                        if qt == 3 and dc == NDC - 1:
                            mm.then_inc(sem["sRt"], 1)
                t.wait_ge(sem["sMsk"], qb + 1)
                if qb >= 1:
                    t.wait_ge(sem["sMTf"], qb)
                for qt in range(4):
                    tr = t.transpose(psMT[:, 128 * qt:128 * qt + 128],
                                     mskb[:, 8 * qt:8 * qt + 8], idC[:])
                    if qt == 3:
                        tr.then_inc(sem["sMT"], 1)
                t.wait_ge(sem["sMTf"], qb + 1)
                if qb >= 1:
                    t.wait_ge(sem["sM2"], qb)
                t.matmul(psEx[:], e2C[:], mskT[:], start=True, stop=True
                         ).then_inc(sem["sEm"], 1)
                if qb >= 1:
                    t.wait_ge(sem["sHr"], qb)
                for dc in range(NDC):
                    mm = t.matmul(psHx[:], wab[:, 128 * dc:128 * dc + 128],
                                  bufC[:, 2048 * dc + 512 * qb:2048 * dc + 512 * qb + 512],
                                  start=(dc == 0), stop=(dc == 15))
                    if dc == NDC - 1:
                        mm.then_inc(sem["sHm"], 1)
            for mt in range(16):
                oci = 64 + mt
                slot = oci % 3
                _tag, _val = wq_wait[oci]
                t.wait_ge(sWL[slot] if _tag == "L" else sWC[slot], _val)
                for qb in range(NQB):
                    idx = mt * 4 + qb
                    if idx < 2:
                        t.wait_ge(sem["sYf"], 64)
                    if idx >= 2:
                        t.wait_ge(sem["sBf"], idx - 1)
                    if mt == 0:
                        t.wait_ge(sem["sHr"], qb + 1)
                    for dc in range(NDC):
                        t.matmul(psB2[idx % 2][:],
                                 wbuf[slot][:, 128 * dc:128 * dc + 128],
                                 bufC[:, 2048 * dc + 512 * qb:2048 * dc + 512 * qb + 512],
                                 start=(dc == 0), stop=False)
                    t.matmul(psB2[idx % 2][:], wbwb[:, 128 * mt:128 * mt + 128],
                             hmb[:, 512 * qb:512 * qb + 512],
                             start=False, stop=True).then_inc(sem["sBm"], 1)

        @block.scalar
        def _(sc):
            sc.wait_ge(sem["sYf"], 64)
            for tb in range(NTB):
                for dc in range(NDC):
                    idx = 64 + tb * NDC + dc
                    sc.wait_ge(sem["sXS"], idx - 1)
                    sc.activation(out=xsqC[idx % 2][:],
                                  in_=bufB[:, 2048 * dc + 512 * tb:2048 * dc + 512 * tb + 512],
                                  func=AF.Square).then_inc(sem["sSQ"], 1)
                sc.wait_ge(sem["sVAR"], 4 + tb + 1)
                if tb >= 1:
                    sc.wait_ge(sem["sSC"], 4 + tb)
                sc.activation(out=stq_[:], in_=stv_[:], func=AF.Sqrt,
                              bias=epsC[:]).then_inc(sem["sSQT"], 1)

        @block.vector
        def _(v):
            DV = mkDV(v)
            for mt in range(16):
                for qb in range(NQB):
                    idx = mt * 4 + qb
                    v.wait_ge(sem["sYm"], idx + 1)
                    slot = mt % 2
                    v.wait_ge(sXA[slot], marks["xa_dma"][("C", mt)])
                    v.tensor_tensor(bufB[:, 2048 * mt + 512 * qb:2048 * mt + 512 * qb + 512],
                                    psY[idx % 2][:],
                                    xbuf[slot][:, 512 * qb:512 * qb + 512], ALU.add
                                    ).then_inc(sem["sYf"], 1)
            for tb in range(NTB):
                v.wait_ge(sem["sXS"], 64 + 16 * (tb + 1))
                v.wait_ge(sem["sSC"], 4 + tb)
                DV(v.tensor_scalar(stm_[:], psA2[0:1, :], 1.0 / 2048.0, None, ALU.mult))
                v.wait_ge(sem["sDV"], dvn[0])
                v.tensor_scalar(stq_[:], psB2s[0:1, :], 1.0 / 2048.0, None,
                                ALU.mult).then_inc(sem["sP2F"], 1)
                if tb >= 1:
                    v.wait_ge(sem["sSQT"], 4 + tb)
                DV(v.tensor_tensor(stv_[:], stm_[:], stm_[:], ALU.mult))
                v.wait_ge(sem["sDV"], dvn[0])
                v.wait_ge(sem["sP2F"], 4 + tb + 1)
                v.tensor_tensor(stv_[:], stq_[:], stv_[:], ALU.subtract
                                ).then_inc(sem["sVAR"], 1)
                v.wait_ge(sem["sSQT"], 4 + tb + 1)
                DV(v.reciprocal(stv_[:], stq_[:]))
                v.wait_ge(sem["sDV"], dvn[0])
                DV(v.tensor_copy(abR[0:1, 512 * tb:512 * tb + 512], stv_[:]))
                v.wait_ge(sem["sDV"], dvn[0])
                v.scalar_tensor_tensor(bR[0:1, 512 * tb:512 * tb + 512], stm_[:], -1.0,
                                       abR[0:1, 512 * tb:512 * tb + 512],
                                       ALU.mult, ALU.mult).then_inc(sem["sSC"], 1)
            v.wait_ge(sem["sYm"], 64)
            for tb in range(NTB):
                v.wait_ge(sem["sBC"], 8 + 2 * (tb + 1))
                v.wait_ge(sem["sDV"], dvn[0])
                v.wait_ge(sem["sXN2"], 16 * tb)
                DV(v.tensor_copy(abc2[:], psMix2[:, 0:512]))
                v.tensor_copy(bbc2[:], psMix2[:, 512:1024]).then_inc(sem["sBCe"], 1)
                v.wait_ge(sem["sBCe"], 4 + tb + 1)
                for dc in range(NDC):
                    cnt = tb * NDC + dc
                    t1 = xt1C[0]
                    if cnt >= 1:
                        v.wait_ge(sem["sXN2"], cnt)
                    DV(v.tensor_tensor(t1[:],
                                       bufB[:, 2048 * dc + 512 * tb:2048 * dc + 512 * tb + 512],
                                       abc2[:], ALU.mult))
                    v.wait_ge(sem["sDV"], dvn[0])
                    v.tensor_tensor(bufC[:, 2048 * dc + 512 * tb:2048 * dc + 512 * tb + 512],
                                    t1[:], bbc2[:], ALU.add
                                    ).then_inc(sem["sXN2"], 1)
            for qb in range(NQB):
                v.wait_ge(sem["sRt"], qb + 1)
                v.tensor_copy(lgb[:], psR[:, 0:32]).then_inc(sem["sLG"], 1)
                v.wait_ge(sem["sLG"], qb + 1)
                for qt in range(4):
                    DV(v.reduce_max(mxb[:, qt:qt + 1], lgb[:, 8 * qt:8 * qt + 8], axis=AX.X))
                v.wait_ge(sem["sDV"], dvn[0])
                for qt in range(4):
                    mk = v.tensor_scalar(mskb[:, 8 * qt:8 * qt + 8], lgb[:, 8 * qt:8 * qt + 8],
                                         mxb[:, qt:qt + 1], None, ALU.is_ge)
                    if qt == 3:
                        mk.then_inc(sem["sMsk"], 1)
                v.wait_ge(sem["sMT"], qb + 1)
                v.tensor_copy(mskT[:], psMT[:]).then_inc(sem["sMTf"], 1)
                v.wait_ge(sem["sEm"], qb + 1)
                v.tensor_copy(m2b[:], psEx[:]).then_inc(sem["sM2"], 1)
                v.wait_ge(sem["sHm"], qb + 1)
                v.wait_ge(sem["sM2"], qb + 1)
                v.tensor_tensor(hmb[:, 512 * qb:512 * qb + 512], psHx[:], m2b[:],
                                ALU.mult).then_inc(sem["sHr"], 1)
            for mt in range(16):
                for qb in range(NQB):
                    idx = mt * 4 + qb
                    v.wait_ge(sem["sBm"], idx + 1)
                    if qb == 0 and mt >= 2:
                        v.wait_ge(sODs[mt % 2], 16 * (mt // 2))
                    v.scalar_tensor_tensor(
                        ostg[mt % 2][:, 512 * qb:512 * qb + 512],
                        psB2[idx % 2][:], bbb[:, mt:mt + 1],
                        bufB[:, 2048 * mt + 512 * qb:2048 * mt + 512 * qb + 512],
                        ALU.add, ALU.add).then_inc(sem["sBf"], 1)

        cctx.close()
    return nc


def _host_layout(inputs):
    x = inputs["L0_attn_x"]
    w_qkv = inputs["L0_attn_w_qkv"]
    w_out = inputs["L0_attn_w_out"]
    W_base = inputs["L0_ffn_W_base"]
    b_base = inputs["L0_ffn_b_base"]
    router_W = inputs["L0_ffn_router_W"]
    A = inputs["L0_ffn_A"]
    Bw = inputs["L0_ffn_Bw"]

    wq5 = np.ascontiguousarray(
        np.asarray(w_qkv, np.float32).reshape(48, 128, 16, 128).transpose(3, 0, 2, 1))
    wo5 = np.ascontiguousarray(
        np.asarray(w_out, np.float32).reshape(16, 128, 16, 128).transpose(3, 0, 2, 1))
    wb5 = np.ascontiguousarray(
        np.asarray(W_base, np.float32).reshape(16, 128, 16, 128).transpose(3, 0, 2, 1))
    wa5 = np.ascontiguousarray(
        np.asarray(A, np.float32).reshape(128, 16, 128).transpose(2, 1, 0))
    wbwp = np.ascontiguousarray(
        np.transpose(np.asarray(Bw, np.float32), (0, 2, 1)).reshape(128, 2048))
    rw5 = np.ascontiguousarray(
        np.asarray(router_W, np.float32).T.reshape(16, 128, 8).transpose(1, 0, 2))
    bb = np.ascontiguousarray(np.asarray(b_base, np.float32).reshape(16, 128).T)

    inv_freq = (1.0 / np.float32(10000.0) **
                (np.arange(0, 128, 2, dtype=np.float32) / np.float32(128.0))).astype(np.float32)
    maps = []
    for c in range(8):
        b = c // 4
        j = c % 4
        ts0 = T * j
        xc = np.asarray(x[b, ts0:ts0 + T, :], np.float32)
        xhc = np.ascontiguousarray(xc.T.reshape(16, 128, T).transpose(1, 0, 2))
        tloc = np.arange(ts0, ts0 + T, dtype=np.float32)
        ang = np.concatenate([np.outer(tloc, inv_freq)] * 2, axis=1).astype(np.float32)
        cosTc = np.ascontiguousarray(np.cos(ang).T.astype(np.float32))
        sinTc = np.ascontiguousarray(np.sin(ang).T.astype(np.float32))
        maps.append({"xh": xhc, "wq5": wq5, "wo5": wo5, "wb5": wb5, "wa5": wa5,
                     "wbw": wbwp, "rw5": rw5, "bbi": bb, "cosi": cosTc, "sini": sinTc})
    return maps


def kernel(**inputs):
    maps = _host_layout(inputs)
    key = (frozenset(TAPS), PHASES)
    if key not in _NC_CACHE:
        _NC_CACHE[key] = build_program()
    nc = _NC_CACHE[key]
    res = run_bass_kernel_spmd(nc, maps, core_ids=list(range(8)))
    out = np.empty((B, S, D), np.float32)
    for c in range(8):
        b = c // 4
        j = c % 4
        out[b, T * j:T * j + T, :] = res.results[c]["outT"].T
    if TAPS:
        kernel.last_results = res.results
    return out


# version 21
# speedup vs baseline: 1.0572x; 1.0185x over previous
import math
import contextlib
import numpy as np
import ml_dtypes
import concourse.bass as bass
import concourse.mybir as mybir
from concourse.bass_utils import run_bass_kernel_spmd

F32 = mybir.dt.float32
BF16 = mybir.dt.bfloat16
AF = mybir.ActivationFunctionType
ALU = mybir.AluOpType
AX = mybir.AxisListType

B, S, D = 2, 8192, 2048
H, DH = 16, 128
T = 2048
GROUPS = [[0, 1, 2, 3], [4, 5, 6, 7]]
SCALE = 1.0 / math.sqrt(128.0)
EPS = 1e-5
NTB, NDC, NQB, NPAIR = 4, 16, 4, 32
NG = H * NQB * NPAIR
NEP = H * NQB
ALPHA_SCALING = 2.0

TAPS = set()
PHASES = "ABC"

_NC_CACHE = {}


def _consts():
    Rm = np.zeros((128, 128), np.float32)
    for d in range(64):
        Rm[d, d + 64] = -1.0
    for d in range(64, 128):
        Rm[d, d - 64] = 1.0
    RT = np.ascontiguousarray(Rm.T).astype(ml_dtypes.bfloat16)
    ident = np.eye(128, dtype=ml_dtypes.bfloat16)
    ones = np.ones((128, 1), dtype=ml_dtypes.bfloat16)
    onesrow = np.ones((1, 128), dtype=ml_dtypes.bfloat16)
    E2 = np.zeros((8, 128), dtype=ml_dtypes.bfloat16)
    for e in range(8):
        E2[e, 16 * e:16 * e + 16] = ALPHA_SCALING
    return RT, ident, ones, onesrow, E2


def build_program():
    nc = bass.Bass("TRN2", target_bir_lowering=False, debug=False)

    xh = nc.dram_tensor("xh", [128, NDC, T], F32, kind="ExternalInput").ap()
    wq5 = nc.dram_tensor("wq5", [128, 48, NDC, 128], F32, kind="ExternalInput").ap()
    wo5 = nc.dram_tensor("wo5", [128, 16, 16, 128], F32, kind="ExternalInput").ap()
    wb5 = nc.dram_tensor("wb5", [128, 16, NDC, 128], F32, kind="ExternalInput").ap()
    wa5 = nc.dram_tensor("wa5", [128, NDC, 128], F32, kind="ExternalInput").ap()
    wbw = nc.dram_tensor("wbw", [128, 2048], F32, kind="ExternalInput").ap()
    rw5 = nc.dram_tensor("rw5", [128, NDC, 8], F32, kind="ExternalInput").ap()
    bbi = nc.dram_tensor("bbi", [128, 16], F32, kind="ExternalInput").ap()
    cosi = nc.dram_tensor("cosi", [128, T], F32, kind="ExternalInput").ap()
    sini = nc.dram_tensor("sini", [128, T], F32, kind="ExternalInput").ap()
    outT = nc.dram_tensor("outT", [D, T], F32, kind="ExternalOutput").ap()

    cc_inK = nc.dram_tensor("cc_inK", [D, T], BF16, kind="Internal").ap()
    cc_outK = nc.dram_tensor("cc_outK", [4 * D, T], BF16, kind="Internal").ap()
    cc_inV = nc.dram_tensor("cc_inV", [4 * T, 512], BF16, kind="Internal").ap()
    cc_outV = nc.dram_tensor("cc_outV", [16 * T, 512], BF16, kind="Internal").ap()
    attd = nc.dram_tensor("attd", [D, T], BF16, kind="Internal").ap()

    RTc, IDc, ONESc, ONESRc, E2c = _consts()

    taps = {}
    for name, shape, dt in [("xn", [128, NDC * T], BF16), ("q", [128, H * T], BF16),
                            ("cc", [2 * D, T], BF16), ("att", [D, T], BF16),
                            ("x1", [128, 16 * T], BF16), ("xn2", [128, NDC * T], BF16),
                            ("hm", [128, T], BF16)]:
        if name in TAPS:
            taps[name] = nc.dram_tensor("t_" + name, shape, dt, kind="ExternalOutput").ap()
    n_tap = len(taps)
    n_tap_ab = sum(1 for nm in ("xn", "q", "cc", "att") if nm in TAPS)

    g = contextlib.ExitStack()
    EG = g.enter_context

    bufB = EG(nc.sbuf_tensor("bufB", [128, H * T], BF16))
    rtC = EG(nc.sbuf_tensor("rtC", [128, 128], BF16))
    idC = EG(nc.sbuf_tensor("idC", [128, 128], BF16))
    onC = EG(nc.sbuf_tensor("onC", [128, 1], BF16))
    onR = EG(nc.sbuf_tensor("onR", [1, 128], BF16))
    e2C = EG(nc.sbuf_tensor("e2C", [8, 128], BF16))
    wab = EG(nc.sbuf_tensor("wab", [128, 2048], BF16))
    wbwb = EG(nc.sbuf_tensor("wbwb", [128, 2048], BF16))
    rwb = EG(nc.sbuf_tensor("rwb", [128, NDC * 8], BF16))
    bbb = EG(nc.sbuf_tensor("bbb", [128, 16], F32))
    epsC = EG(nc.sbuf_tensor("epsC", [1, 1], F32))
    wbuf = [EG(nc.sbuf_tensor(f"wbuf{i}", [128, 2048], BF16)) for i in range(3)]
    xbuf = [EG(nc.sbuf_tensor(f"xbuf{i}", [128, 2048], F32)) for i in range(2)]
    abR = EG(nc.sbuf_tensor("abR", [1, T], BF16))
    bR = EG(nc.sbuf_tensor("bR", [1, T], BF16))
    stm_ = EG(nc.sbuf_tensor("stm_", [1, 512], F32))
    stq_ = EG(nc.sbuf_tensor("stq_", [1, 512], F32))
    stv_ = EG(nc.sbuf_tensor("stv_", [1, 512], F32))

    SN = ["sCN", "sDV", "sSQ", "sCB", "sXC", "sXS", "sST", "sVAR", "sSQT",
          "sSC", "sBC", "sXN", "sQm", "sQf", "sKR", "sRm", "sRf", "sKV", "sCI",
          "sCC", "sPH", "sVT", "sVTf", "sVR", "sS", "sE", "sPV", "sOn", "sOf",
          "sPT", "sPTf", "sAS", "sAL", "sYm", "sYf", "sX1", "sC1", "sLG",
          "sMsk", "sMT", "sMTf", "sEm", "sM2", "sHm", "sHr", "sBm", "sBf",
          "sOS", "sOD", "sXN2", "sRt", "sOnb", "sBCe", "sVA", "sVM", "sVB", "sP2F"]
    sem = {nm: EG(nc.semaphore(nm)) for nm in SN}
    sXA = [EG(nc.semaphore(f"sXA{i}")) for i in range(4)]
    sWL = [EG(nc.semaphore(f"sWL{i}")) for i in range(3)]
    sWF = [EG(nc.semaphore(f"sWF{i}")) for i in range(2)]
    sKL = [EG(nc.semaphore(f"sKL{i}")) for i in range(2)]
    sKF = [EG(nc.semaphore(f"sKF{i}")) for i in range(2)]
    sVL = [EG(nc.semaphore(f"sVL{i}")) for i in range(2)]
    sVG = [EG(nc.semaphore(f"sVG{i}")) for i in range(2)]
    sVF = [EG(nc.semaphore(f"sVF{i}")) for i in range(2)]
    sCIK = [EG(nc.semaphore(f"sCIK{i}")) for i in range(2)]
    sCIV = [EG(nc.semaphore(f"sCIV{i}")) for i in range(2)]
    sXW = [EG(nc.semaphore(f"sXW{i}")) for i in range(2)]
    sWC = [EG(nc.semaphore(f"sWC{i}")) for i in range(3)]
    sASs = [EG(nc.semaphore(f"sASs{i}")) for i in range(2)]
    sODs = [EG(nc.semaphore(f"sODs{i}")) for i in range(2)]
    sALs = [EG(nc.semaphore(f"sALs{i}")) for i in range(2)]

    block = EG(nc.Block())

    chunk_order = []
    for cg in range(4):
        chunk_order += [16 + 4 * cg + i for i in range(4)]
        chunk_order += [32 + 4 * cg + i for i in range(4)]
    chunk_order += list(range(16))
    gp_info, rope_gps, contrib_gps = [], [], []
    for oci, oc in enumerate(chunk_order):
        kind = "K" if 16 <= oc < 32 else ("V" if oc >= 32 else "Q")
        for pr in range(2):
            gp_info.append((oc, pr, kind))
            gp = oci * 2 + pr
            if kind in ("K", "Q"):
                rope_gps.append(gp)
            if kind in ("K", "V"):
                contrib_gps.append(gp)
    rope_idx = {gp: i for i, gp in enumerate(rope_gps)}
    contrib_idx = {gp: i for i, gp in enumerate(contrib_gps)}
    _kd = [0, 0]
    _vc = 0
    vci_of = {}
    _rel = {}
    for _ci, _gp in enumerate(contrib_gps):
        if gp_info[_gp][2] == "K":
            _kd[_ci % 2] += 1
            _rel[_ci] = ("K", _ci % 2, 16 * _kd[_ci % 2])
        else:
            vci_of[_gp] = _vc
            _rel[_ci] = ("VD", 16 * (_vc + 1))
            _vc += 1
    kv_release = {ci: _rel[ci - 2] for ci in range(2, 64)}
    pairs = [(h, qb, lp) for h in range(H) for qb in range(NQB) for lp in range(NPAIR)]

    def bar(eng, which):
        eng.sem_inc(sem["sPH"], 1)
        eng.wait_ge(sem["sPH"], 5 * which)

    with g:
        actx = contextlib.ExitStack()
        EA = actx.enter_context
        xnA = EA(nc.sbuf_tensor("xnA", [128, NDC * T], BF16))
        xcb = [EA(nc.sbuf_tensor(f"xcb{i}", [128, 512], BF16)) for i in range(2)]
        xsq = [EA(nc.sbuf_tensor(f"xsq{i}", [128, 512], BF16)) for i in range(2)]
        krawB = EA(nc.sbuf_tensor("krawB", [128, 1024], BF16))
        s1b = EA(nc.sbuf_tensor("s1b", [128, 1024], BF16))
        s2b = EA(nc.sbuf_tensor("s2b", [128, 1024], BF16))
        xt1 = [EA(nc.sbuf_tensor(f"xt1_{i}", [128, 512], BF16)) for i in range(2)]
        kvst = [EA(nc.sbuf_tensor(f"kvst{i}", [128, 1024], BF16)) for i in range(2)]
        abc = EA(nc.sbuf_tensor("abc", [128, 512], BF16))
        bbc = EA(nc.sbuf_tensor("bbc", [128, 512], BF16))
        cosT = EA(nc.sbuf_tensor("cosT", [128, T], BF16))
        sinT = EA(nc.sbuf_tensor("sinT", [128, T], BF16))
        psMix = EA(nc.psum_tensor("psMix", [128, 1024], F32))
        psQ = [EA(nc.psum_tensor(f"psQ{i}", [128, 1024], F32)) for i in range(2)]
        _sctx = contextlib.ExitStack()
        psAB = _sctx.enter_context(nc.psum_tensor("psAB", [1, 1024], F32))
        _sctx.close()
        psVtA = [EA(nc.psum_tensor("psVtA0", [128, 1024], BF16))]

        NCN = 12 * 16
        marks = {"xn_tb": []}
        xa = [0, 0]
        xa_dma, xa_t1m = {}, {}
        for u in range(32):
            sl = u % 2
            xa[sl] += 16
            xa_dma[u] = xa[sl]
            if u >= 16:
                for j in range(4):
                    xa[sl] += 8
                    xa_t1m[(u, j)] = xa[sl]
        xa_end = list(xa)
        for mt in range(16):
            sl = mt % 2
            xa[sl] += 16
            xa_dma[("C", mt)] = xa[sl]
        marks["xa_dma"] = xa_dma
        marks["xa_t1m"] = xa_t1m
        marks["xa_end"] = xa_end
        dvn = [0]
        N0W = 14

        def _is_cast(_o):
            return N0W <= _o < 48 and _o % 2 == 1

        _wl = [0, 0, 0]
        _wc = [0, 0, 0]
        wq_wait = {}
        for _o in range(80):
            if _is_cast(_o):
                _wc[_o % 3] += 1
                wq_wait[_o] = ("C", _wc[_o % 3])
            else:
                _wl[_o % 3] += 16
                wq_wait[_o] = ("L", _wl[_o % 3])
        cast_chunks = [c for c in range(48) if _is_cast(c)]
        cast_pos = {c: i for i, c in enumerate(cast_chunks)}
        stage_xw = {}

        def mkDV(v):
            def DV(inst):
                inst.then_inc(sem["sDV"], 1)
                dvn[0] += 1
                return inst
            return DV

        @block.gpsimd
        def _(gq):
            for dst, src in [(cosT, cosi[:, :]), (sinT, sini[:, :]),
                             (rtC, nc.inline_tensor(RTc, "c_rt").ap()),
                             (idC, nc.inline_tensor(IDc, "c_id").ap()),
                             (onC, nc.inline_tensor(ONESc, "c_on").ap()),
                             (onR, nc.inline_tensor(ONESRc, "c_or").ap()),
                             (e2C, nc.inline_tensor(E2c, "c_e2").ap()),
                             (wab, wa5.rearrange("p dc er -> p (dc er)")),
                             (wbwb, wbw[:, :]),
                             (rwb, rw5.rearrange("p dc e -> p (dc e)")),
                             (bbb, bbi[:, :]),
                             (epsC, nc.inline_tensor(np.full((1, 1), EPS, np.float32), "c_eps").ap())]:
                gq.dma_start(out=dst[:], in_=src).then_inc(sem["sCN"], 16)
            if PHASES == "A0":
                return
            agk_after = {8: 0}
            agv_after = {14: 0}
            for oci in range(48):
                if _is_cast(oci):
                    continue
                oc = chunk_order[oci]
                slot = oci % 3
                if oci >= 3:
                    gq.wait_ge(sem["sQm"], 2 * (oci - 2))
                gq.dma_start(out=wbuf[slot][:],
                             in_=wq5[:, oc, :, :].rearrange("p dc o -> p (dc o)")
                             ).then_inc(sWL[slot], 16)
                if oci in agk_after:
                    cg = agk_after[oci]
                    gq.wait_ge(sCIK[0], 64 * (cg + 1))
                    gq.wait_ge(sCIK[1], 64 * (cg + 1))
                    gq.collective_compute(
                        "AllGather", ALU.bypass, replica_groups=GROUPS,
                        ins=[cc_inK[512 * cg:512 * cg + 512, :].opt()],
                        outs=[cc_outK[2048 * cg:2048 * cg + 2048, :].opt()]
                    ).then_inc(sem["sCC"], 1)
                if oci in agv_after:
                    cg = agv_after[oci]
                    gq.wait_ge(sCIV[0], 16 * 8 * (cg + 1))
                    gq.collective_compute(
                        "AllGather", ALU.bypass, replica_groups=GROUPS,
                        ins=[cc_inV[2048 * cg:2048 * cg + 2048, :].opt()],
                        outs=[cc_outV[8192 * cg:8192 * cg + 8192, :].opt()]
                    ).then_inc(sem["sCC"], 1)
            for cg in range(1, 4):
                gq.wait_ge(sCIK[0], 64 * (cg + 1))
                gq.wait_ge(sCIK[1], 64 * (cg + 1))
                gq.collective_compute(
                    "AllGather", ALU.bypass, replica_groups=GROUPS,
                    ins=[cc_inK[512 * cg:512 * cg + 512, :].opt()],
                    outs=[cc_outK[2048 * cg:2048 * cg + 2048, :].opt()]
                ).then_inc(sem["sCC"], 1)
                gq.wait_ge(sCIV[0], 16 * 8 * (cg + 1))
                gq.collective_compute(
                    "AllGather", ALU.bypass, replica_groups=GROUPS,
                    ins=[cc_inV[2048 * cg:2048 * cg + 2048, :].opt()],
                    outs=[cc_outV[8192 * cg:8192 * cg + 8192, :].opt()]
                ).then_inc(sem["sCC"], 1)


        @block.sync
        def _(sy):
            for p in range(2):
                for tb in range(NTB):
                    for qd in range(4):
                        u = p * 16 + tb * 4 + qd
                        if u % 2 == 1:
                            continue
                        if 2 <= u < 18:
                            sy.wait_ge(sem["sXC"], 4 * (u - 1))
                            sy.wait_ge(sem["sXS"], 4 * (u - 1))
                        elif u >= 18:
                            sy.wait_ge(sXA[0], marks["xa_t1m"][(u - 2, 3)])
                        sy.dma_start(out=xbuf[0][:].rearrange("p (a b) -> p a b", a=4, b=512),
                                     in_=xh[:, 4 * qd:4 * qd + 4, 512 * tb:512 * tb + 512]
                                     ).then_inc(sXA[0], 16)

            if PHASES == "A0":
                return
            sxw = [0, 0]
            for j in range(48):
                c = j + 2
                if _is_cast(c):
                    ip = cast_pos[c]
                    xs = ip % 2
                    if ip < 2:
                        sy.wait_ge(sXA[xs], marks["xa_t1m"][(30 + xs, 3)])
                    else:
                        pc = cast_chunks[ip - 2]
                        sy.wait_ge(sWC[pc % 3], wq_wait[pc][1])
                    sy.dma_start(out=xbuf[xs][:],
                                 in_=wq5[:, chunk_order[c], :, :]
                                 .rearrange("p dc o -> p (dc o)")
                                 ).then_inc(sXW[xs], 16)
                    sxw[xs] += 16
                    stage_xw[c] = sxw[xs]
                if j > 31:
                    continue
                for ci in (2 * j, 2 * j + 1):
                    gp = contrib_gps[ci]
                    oc, pr, kind = gp_info[gp]
                    slot = ci % 2
                    if kind == "K":
                        sy.wait_ge(sem["sKV"], ci + 1)
                        hh = oc - 16
                        row = 512 * (hh // 4) + 128 * (hh % 4)
                        sy.dma_start(out=cc_inK[row:row + 128,
                                                1024 * pr:1024 * pr + 1024],
                                     in_=kvst[slot][:]).then_inc(sCIK[slot], 16)
                    else:
                        vci = vci_of[gp]
                        sy.wait_ge(sem["sVB"], vci + 1)
                        hh = oc - 32
                        dst = cc_inV[2048 * (hh // 4) + 1024 * pr:
                                     2048 * (hh // 4) + 1024 * pr + 1024,
                                     128 * (hh % 4):128 * (hh % 4) + 128]
                        sy.dma_start(out=dst.rearrange("(kt p) d -> p kt d", p=128),
                                     in_=kvst[slot][:]
                                     .rearrange("p (kt d) -> p kt d", d=128)
                                     ).then_inc(sCIV[0], 16)

        @block.scalar
        def _(sc):
            def xload(u):
                tb2, qd = (u % 16) // 4, (u % 16) % 4
                if 2 <= u < 18:
                    sc.wait_ge(sem["sXC"], 4 * (u - 1))
                    sc.wait_ge(sem["sXS"], 4 * (u - 1))
                elif u >= 18:
                    sc.wait_ge(sXA[1], marks["xa_t1m"][(u - 2, 3)])
                sc.dma_start(out=xbuf[1][:].rearrange("p (a b) -> p a b", a=4, b=512),
                             in_=xh[:, 4 * qd:4 * qd + 4, 512 * tb2:512 * tb2 + 512]
                             ).then_inc(sXA[1], 16)

            xload(1)
            for tb in range(NTB):
                for dc in range(NDC):
                    idx = tb * NDC + dc
                    u = tb * 4 + dc // 4
                    slot = u % 2
                    sc.wait_ge(sXA[slot], marks["xa_dma"][u])
                    if idx >= 2:
                        sc.wait_ge(sem["sXS"], idx - 1)
                    sc.activation(out=xsq[idx % 2][:],
                                  in_=xbuf[slot][:, 512 * (dc % 4):512 * (dc % 4) + 512],
                                  func=AF.Square).then_inc(sem["sSQ"], 1)
                    if idx % 8 == 0 and idx > 0 and idx // 4 + 1 < 16:
                        xload(idx // 4 + 1)
                sc.wait_ge(sem["sVAR"], tb + 1)
                if tb >= 1:
                    sc.wait_ge(sem["sSC"], tb)
                sc.activation(out=stq_[:], in_=stv_[:], func=AF.Sqrt,
                              bias=epsC[:]).then_inc(sem["sSQT"], 1)
            for u in range(17, 32, 2):
                xload(u)

        @block.vector
        def _(v):
            DV = mkDV(v)
            for tb in range(NTB):
                for dc in range(NDC):
                    idx = tb * NDC + dc
                    u = tb * 4 + dc // 4
                    slot = u % 2
                    v.wait_ge(sXA[slot], marks["xa_dma"][u])
                    if idx >= 2:
                        v.wait_ge(sem["sXC"], idx - 1)
                    v.tensor_copy(xcb[idx % 2][:],
                                  xbuf[slot][:, 512 * (dc % 4):512 * (dc % 4) + 512]
                                  ).then_inc(sem["sCB"], 1)
                v.wait_ge(sem["sXS"], 16 * (tb + 1))
                if tb >= 1:
                    v.wait_ge(sem["sSC"], tb)
                DV(v.tensor_scalar(stm_[:], psAB[0:1, 0:512], 1.0 / 2048.0, None, ALU.mult))
                v.wait_ge(sem["sDV"], dvn[0])
                v.tensor_scalar(stq_[:], psAB[0:1, 512:1024], 1.0 / 2048.0, None,
                                ALU.mult).then_inc(sem["sP2F"], 1)
                if tb >= 1:
                    v.wait_ge(sem["sSQT"], tb)
                DV(v.tensor_tensor(stv_[:], stm_[:], stm_[:], ALU.mult))
                v.wait_ge(sem["sDV"], dvn[0])
                v.wait_ge(sem["sP2F"], tb + 1)
                v.tensor_tensor(stv_[:], stq_[:], stv_[:], ALU.subtract
                                ).then_inc(sem["sVAR"], 1)
                v.wait_ge(sem["sSQT"], tb + 1)
                DV(v.reciprocal(stv_[:], stq_[:]))
                v.wait_ge(sem["sDV"], dvn[0])
                DV(v.tensor_copy(abR[0:1, 512 * tb:512 * tb + 512], stv_[:]))
                v.wait_ge(sem["sDV"], dvn[0])
                v.scalar_tensor_tensor(bR[0:1, 512 * tb:512 * tb + 512], stm_[:], -1.0,
                                       abR[0:1, 512 * tb:512 * tb + 512],
                                       ALU.mult, ALU.mult).then_inc(sem["sSC"], 1)
            for tb in range(NTB):
                v.wait_ge(sem["sBC"], 2 * (tb + 1))
                v.wait_ge(sem["sDV"], dvn[0])
                DV(v.tensor_copy(abc[:], psMix[:, 0:512]))
                v.tensor_copy(bbc[:], psMix[:, 512:1024]).then_inc(sem["sBCe"], 1)
                v.wait_ge(sem["sDV"], dvn[0])
                v.wait_ge(sem["sBCe"], tb + 1)
                for dc in range(NDC):
                    idx = 64 + tb * NDC + dc
                    u = 16 + tb * 4 + dc // 4
                    slot = u % 2
                    v.wait_ge(sXA[slot], marks["xa_dma"][u])
                    t1 = xt1[dc % 2]
                    if dc >= 2 or tb >= 1:
                        v.wait_ge(sem["sDV"], dvn[0] - 1)
                    v.tensor_tensor(t1[:],
                                    xbuf[slot][:, 512 * (dc % 4):512 * (dc % 4) + 512],
                                    abc[:], ALU.mult).then_inc(sXA[slot], 8)
                    v.wait_ge(sXA[slot], marks["xa_t1m"][(u, dc % 4)])
                    DV(v.tensor_tensor(
                        xnA[:, 2048 * dc + 512 * tb:2048 * dc + 512 * tb + 512],
                        t1[:], bbc[:], ALU.add))
                marks["xn_tb"].append(dvn[0])
            for gp in (range(96) if PHASES != "A0" else []):
                if gp >= 2 and (gp - 2) in vci_of:
                    vci2 = vci_of[gp - 2]
                    v.wait_ge(sem["sVA"], vci2 + 1)
                    v.tensor_copy(kvst[contrib_idx[gp - 2] % 2][:], psVtA[0][:]
                                  ).then_inc(sem["sVB"], 1)
                if gp % 2 == 1:
                    c = gp // 2 + 2
                    if _is_cast(c):
                        ws = c % 3
                        xs = cast_pos[c] % 2
                        v.wait_ge(sem["sQm"], 2 * (c - 2))
                        v.wait_ge(sXW[xs], stage_xw[c])
                        v.tensor_copy(wbuf[ws][:], xbuf[xs][:]
                                      ).then_inc(sWC[ws], 1)
                oc, pr, kind = gp_info[gp]
                v.wait_ge(sem["sQm"], gp + 1)
                if gp >= 1:
                    v.wait_ge(sem["sKV"], gp)
                if kind == "V":
                    ci = contrib_idx[gp]
                    slot = ci % 2
                    if ci >= 2:
                        rel = kv_release[ci]
                        if rel[0] == "K":
                            v.wait_ge(sCIK[rel[1]], rel[2])
                        else:
                            v.wait_ge(sCIV[0], rel[1])
                    v.tensor_copy(kvst[slot][:], psQ[gp % 2][:]).then_inc(sem["sKV"], 1)
                else:
                    ri = rope_idx[gp]
                    if ri >= 1:
                        v.wait_ge(sem["sRm"], ri)
                    v.tensor_copy(krawB[:], psQ[gp % 2][:]).then_inc(sem["sKR"], 1)
                    DV(v.tensor_tensor(s1b[:], psQ[gp % 2][:],
                                       cosT[:, 1024 * pr:1024 * pr + 1024], ALU.mult))
                    v.wait_ge(sem["sRm"], ri + 1)
                    v.tensor_tensor(s2b[:], psMix[:],
                                    sinT[:, 1024 * pr:1024 * pr + 1024], ALU.mult
                                    ).then_inc(sem["sRf"], 1)
                    v.wait_ge(sem["sDV"], dvn[0])
                    v.wait_ge(sem["sRf"], ri + 1)
                    if kind == "K":
                        ci = contrib_idx[gp]
                        slot = ci % 2
                        if ci >= 2:
                            rel = kv_release[ci]
                            if rel[0] == "K":
                                v.wait_ge(sCIK[rel[1]], rel[2])
                            else:
                                v.wait_ge(sCIV[0], rel[1])
                        dst = kvst[slot][:]
                    else:
                        dst = bufB[:, 2048 * oc + 1024 * pr:2048 * oc + 1024 * pr + 1024]
                    v.tensor_tensor(dst, s1b[:], s2b[:], ALU.add).then_inc(sem["sKV"], 1)

        @block.tensor
        def _(t):
            t.wait_ge(sem["sCN"], NCN)

            def stats_tb(tb):
                if tb >= 1:
                    t.wait_ge(sem["sP2F"], tb)
                for dc in range(NDC):
                    idx = tb * NDC + dc
                    t.wait_ge(sem["sCB"], idx + 1)
                    t.wait_ge(sem["sSQ"], idx + 1)
                    t.matmul(psAB[0:1, 0:512], onC[:], xcb[idx % 2][:],
                             start=(dc == 0), stop=(dc == 15),
                             skip_group_check=True).then_inc(sem["sXC"], 1)
                    t.matmul(psAB[0:1, 512:1024], onC[:], xsq[idx % 2][:],
                             start=(dc == 0), stop=(dc == 15),
                             skip_group_check=True).then_inc(sem["sXS"], 1)

            def bcast_tb(tb):
                t.wait_ge(sem["sSC"], tb + 1)
                if tb >= 1:
                    t.wait_ge(sem["sBCe"], tb)
                t.matmul(psMix[:, 0:512], onR[:], abR[0:1, 512 * tb:512 * tb + 512],
                         start=True, stop=True)
                t.matmul(psMix[:, 512:1024], onR[:], bR[0:1, 512 * tb:512 * tb + 512],
                         start=True, stop=True).then_inc(sem["sBC"], 2)

            for tb in range(NTB):
                stats_tb(tb)
            for tb in range(NTB):
                bcast_tb(tb)
            if PHASES == "A0":
                return
            pend_rope = []
            pend_v = []

            def emit_vtr(vgp):
                vci = vci_of[vgp]
                t.wait_ge(sem["sKV"], vgp + 1)
                if vci >= 1:
                    t.wait_ge(sem["sVB"], vci)
                kslot = contrib_idx[vgp] % 2
                for i8 in range(8):
                    tr = t.transpose(psVtA[0][:, 128 * i8:128 * i8 + 128],
                                     kvst[kslot][:, 128 * i8:128 * i8 + 128], idC[:])
                    if i8 == 7:
                        tr.then_inc(sem["sVA"], 1)

            def emit_rot(rgp):
                ri = rope_idx[rgp]
                t.wait_ge(sem["sKR"], ri + 1)
                if ri >= 1:
                    t.wait_ge(sem["sRf"], ri)
                t.matmul(psMix[:, 0:512], rtC[:], krawB[:, 0:512], start=True, stop=True)
                t.matmul(psMix[:, 512:1024], rtC[:], krawB[:, 512:1024],
                         start=True, stop=True).then_inc(sem["sRm"], 1)

            for oci, oc in enumerate(chunk_order):
                slot = oci % 3
                _tag, _val = wq_wait[oci]
                t.wait_ge(sWL[slot] if _tag == "L" else sWC[slot], _val)
                for pr in range(2):
                    gp = oci * 2 + pr
                    if oci == 0:
                        t.wait_ge(sem["sDV"], marks["xn_tb"][2 * pr + 1])
                    if gp >= 2:
                        t.wait_ge(sem["sKV"], gp - 1)
                    for tin in range(2):
                        tcol = 1024 * pr + 512 * tin
                        for dc in range(NDC):
                            mm = t.matmul(psQ[gp % 2][:, 512 * tin:512 * tin + 512],
                                          wbuf[slot][:, 128 * dc:128 * dc + 128],
                                          xnA[:, 2048 * dc + tcol:2048 * dc + tcol + 512],
                                          start=(dc == 0), stop=(dc == 15))
                            if tin == 1 and dc == NDC - 1:
                                mm.then_inc(sem["sQm"], 1)
                    if pend_rope:
                        emit_rot(pend_rope.pop(0))
                    if pend_v:
                        emit_vtr(pend_v.pop(0))
                    if gp in rope_idx:
                        pend_rope.append(gp)
                    if gp in vci_of:
                        pend_v.append(gp)
            while pend_rope:
                emit_rot(pend_rope.pop(0))
            while pend_v:
                emit_vtr(pend_v.pop(0))

        tapsA = (1 if "xn" in TAPS else 0)
        if PHASES != "A0":
            tapsA += (1 if "q" in TAPS else 0)

        @block.sync
        def _(sy):
            if "xn" in TAPS:
                sy.wait_ge(sem["sDV"], marks["xn_tb"][3])
                sy.dma_start(out=taps["xn"][:, :], in_=xnA[:]).then_inc(sem["sOD"], 16)
            if "q" in TAPS and PHASES != "A0":
                sy.wait_ge(sem["sKV"], 96)
                sy.dma_start(out=taps["q"][:, :], in_=bufB[:]).then_inc(sem["sOD"], 16)
            bar(sy, 1)
        for engname in ["gpsimd", "scalar", "vector", "tensor"]:
            getattr(block, engname)(lambda e: bar(e, 1))

        actx.close()

        if PHASES in ("A", "A0"):
            @block.sync
            def _(sy):
                sy.wait_ge(sem["sOD"], 16 * tapsA)
            return nc

        bctx = contextlib.ExitStack()
        EB = bctx.enter_context
        kbuf = [EB(nc.sbuf_tensor(f"kbuf{i}", [128, 8192], BF16)) for i in range(2)]
        vone = [EB(nc.sbuf_tensor(f"vone{i}", [128, 64 * 129], BF16)) for i in range(2)]
        ptb = [EB(nc.sbuf_tensor(f"ptb{i}", [128, 512], BF16)) for i in range(8)]
        onb = EB(nc.sbuf_tensor("onb", [128, 512], BF16))
        orawA = EB(nc.sbuf_tensor("orawA", [128, 258], F32))
        orawB = EB(nc.sbuf_tensor("orawB", [128, 258], F32))
        rlb = EB(nc.sbuf_tensor("rlb", [128, 4], F32))
        attst = [EB(nc.sbuf_tensor(f"attst{i}", [128, 2048], BF16)) for i in range(2)]
        psS = [EB(nc.psum_tensor(f"psS{i}", [128, 512], F32)) for i in range(5)]
        psOA = EB(nc.psum_tensor("psOA", [128, 258], F32))
        psOB = EB(nc.psum_tensor("psOB", [128, 258], F32))
        psT = EB(nc.psum_tensor("psT", [128, 512], BF16))

        @block.sync
        def _(sy):
            sy.wait_ge(sem["sKV"], 96)
            sy.wait_ge(sem["sRm"], 64)
            sy.wait_ge(sem["sSQT"], 4)

            def k_load(h):
                slot = h % 2
                sy.wait_ge(sem["sCC"], 2 * (h // 4) + 1)
                if h >= 2:
                    sy.wait_ge(sem["sS"], 256 * (h - 1))
                base = 2048 * (h // 4)
                for r in range(4):
                    sy.dma_start(out=kbuf[slot][:, 2048 * r:2048 * r + 2048],
                                 in_=cc_outK[base + 512 * r + 128 * (h % 4):
                                             base + 512 * r + 128 * (h % 4) + 128, :]
                                 ).then_inc(sKL[slot], 16)

            def v_load(ht):
                slot = ht % 2
                sy.wait_ge(sem["sCC"], 2 * (ht // 4) + 2)
                sy.wait_ge(sem["sVM"], slot + 1)
                if ht >= 2:
                    sy.wait_ge(sem["sPV"], 256 * (ht - 1))
                g2 = ht // 4
                dstv = vone[slot][:].rearrange("p (kt c) -> p kt c", kt=64, c=129)
                sy.dma_start(out=dstv[:, 0:32, 0:128],
                             in_=cc_outV[8192 * g2:8192 * g2 + 4096,
                                         128 * (ht % 4):128 * (ht % 4) + 128]
                             .rearrange("(kt p) d -> p kt d", p=128)
                             ).then_inc(sVL[slot], 16)

            k_load(0)
            k_load(1)
            v_load(0)
            for h in range(H):
                if h + 1 < H:
                    v_load(h + 1)
                if h + 2 < H:
                    k_load(h + 2)
                sy.wait_ge(sem["sPTf"], 4 * (h + 1))
                sy.dma_start(out=attd[128 * h:128 * h + 128, :],
                             in_=attst[h % 2][:]).then_inc(sASs[h % 2], 16)

        @block.scalar
        def _(sc):
            NK = H * NQB * 64
            for k in range(NK):
                sc.wait_ge(sem["sS"], k + 1)
                if k >= 8:
                    sc.wait_ge(sem["sPV"], k - 7)
                sc.activation(out=ptb[k % 8][:], in_=psS[k % 5][:],
                              func=AF.Exp, scale=SCALE).then_inc(sem["sE"], 1)

        @block.tensor
        def _(t):
            NK = H * NQB * 64
            LAG = 8
            kdesc = [(h, qb, lk) for h in range(H) for qb in range(NQB)
                     for lk in range(64)]
            ep_of_k = [h * NQB + qb for (h, qb, lk) in kdesc]
            for ki in range(NK + LAG):
                if ki >= LAG:
                    k = ki - LAG
                    e_h, e_qb, e_lk = kdesc[k]
                    ep = e_h * NQB + e_qb
                    if e_lk == 0 and e_qb == 0:
                        t.wait_ge(sVL[e_h % 2], 16 * (e_h // 2 + 1))
                        t.wait_ge(sVG[e_h % 2], 16 * (e_h // 2 + 1))
                    if e_lk == 0 and ep >= 1:
                        t.wait_ge(sem["sOn"], ep)
                    t.wait_ge(sem["sE"], k + 1)
                    for qt in range(4):
                        po = psOA if qt < 2 else psOB
                        col = 129 * (qt % 2)
                        mm = t.matmul(po[:, col:col + 129],
                                      ptb[k % 8][:, 128 * qt:128 * qt + 128],
                                      vone[e_h % 2][:, 129 * e_lk:129 * e_lk + 129],
                                      start=(e_lk == 0 and qt in (0, 2)),
                                      stop=(e_lk == 63), skip_group_check=True)
                        if qt == 3:
                            mm.then_inc(sem["sPV"], 1)
                    do_tr = None
                    if e_lk == 4 and k >= 64:
                        do_tr = ep_of_k[k] - 1
                    if do_tr is not None:
                        t.wait_ge(sem["sOnb"], do_tr + 1)
                        if do_tr >= 1:
                            t.wait_ge(sem["sPTf"], do_tr)
                        for qt in range(4):
                            tr = t.transpose(psT[:, 128 * qt:128 * qt + 128],
                                             onb[:, 128 * qt:128 * qt + 128], idC[:])
                            if qt == 3:
                                tr.then_inc(sem["sPT"], 1)
                if ki < NK:
                    h, qb, lk = kdesc[ki]
                    if qb == 0 and lk == 0:
                        t.wait_ge(sKL[h % 2], 64 * (h // 2 + 1))
                    if ki >= 5:
                        t.wait_ge(sem["sE"], ki - 4)
                    mm = t.matmul(psS[ki % 5][:],
                                  kbuf[h % 2][:, 128 * lk:128 * lk + 128],
                                  bufB[:, 2048 * h + 512 * qb:2048 * h + 512 * qb + 512],
                                  start=True, stop=True).then_inc(sem["sS"], 1)
            t.wait_ge(sem["sOnb"], NEP)
            t.wait_ge(sem["sPTf"], NEP - 1)
            for qt in range(4):
                tr = t.transpose(psT[:, 128 * qt:128 * qt + 128],
                                 onb[:, 128 * qt:128 * qt + 128], idC[:])
                if qt == 3:
                    tr.then_inc(sem["sPT"], 1)

        @block.vector
        def _(v):
            DV = mkDV(v)
            v.wait_ge(sem["sKV"], 96)
            v.memset(vone[0][:], 1.0).then_inc(sem["sVM"], 1)
            v.memset(vone[1][:], 1.0).then_inc(sem["sVM"], 1)

            for ep in range(NEP):
                h, qb = ep // NQB, ep % NQB
                v.wait_ge(sem["sPV"], 64 * (ep + 1))
                if qb == 0 and h >= 2:
                    v.wait_ge(sASs[h % 2], 16 * (h // 2))
                DV(v.tensor_copy(orawA[:], psOA[:]))
                v.tensor_copy(orawB[:], psOB[:]).then_inc(sem["sOn"], 1)
                v.wait_ge(sem["sDV"], dvn[0])
                v.wait_ge(sem["sOn"], ep + 1)
                DV(v.reciprocal(rlb[:, 0:2],
                                orawA[:].rearrange("p (a c) -> p a c", a=2, c=129)[:, :, 128]))
                DV(v.reciprocal(rlb[:, 2:4],
                                orawB[:].rearrange("p (a c) -> p a c", a=2, c=129)[:, :, 128]))
                v.wait_ge(sem["sDV"], dvn[0])
                for qt in range(4):
                    po = orawA if qt < 2 else orawB
                    col = 129 * (qt % 2)
                    mu = v.tensor_scalar(onb[:, 128 * qt:128 * qt + 128],
                                         po[:, col:col + 128], rlb[:, qt:qt + 1],
                                         None, ALU.mult)
                    if qt == 3:
                        mu.then_inc(sem["sOnb"], 1)
                v.wait_ge(sem["sPT"], ep + 1)
                v.tensor_copy(attst[h % 2][:, 512 * qb:512 * qb + 512],
                              psT[:]).then_inc(sem["sPTf"], 1)

        @block.sync
        def _(sy):
            if "att" in TAPS:
                sy.wait_ge(sASs[0], 128)
                sy.wait_ge(sASs[1], 128)
                sy.dma_start(out=taps["att"][:, :], in_=attd[:, :]).then_inc(sem["sOD"], 16)
            bar(sy, 2)

        @block.gpsimd
        def _(gq):
            for ht in range(H):
                slot = ht % 2
                gq.wait_ge(sem["sCC"], 2 * (ht // 4) + 2)
                gq.wait_ge(sem["sVM"], slot + 1)
                if ht >= 2:
                    gq.wait_ge(sem["sPV"], 256 * (ht - 1))
                g2 = ht // 4
                dstv = vone[slot][:].rearrange("p (kt c) -> p kt c", kt=64, c=129)
                gq.dma_start(out=dstv[:, 32:64, 0:128],
                             in_=cc_outV[8192 * g2 + 4096:8192 * g2 + 8192,
                                         128 * (ht % 4):128 * (ht % 4) + 128]
                             .rearrange("(kt p) d -> p kt d", p=128)
                             ).then_inc(sVG[slot], 16)

        @block.gpsimd
        def _(e):
            e.sem_inc(sem["sPH"], 1)
        for engname in ["scalar", "vector", "tensor"]:
            getattr(block, engname)(lambda e: bar(e, 2))

        bctx.close()

        if PHASES == "AB":
            tapsAB = tapsA + (1 if "att" in TAPS else 0)
            @block.sync
            def _(sy):
                sy.wait_ge(sem["sOD"], 16 * tapsAB)
            return nc

        cctx = contextlib.ExitStack()
        EC = cctx.enter_context
        bufC = EC(nc.sbuf_tensor("bufC", [128, NDC * T], BF16))
        xsqC = [EC(nc.sbuf_tensor(f"xsqC{i}", [128, 512], BF16)) for i in range(2)]
        ostg = [EC(nc.sbuf_tensor(f"ostg{i}", [128, 2048], F32)) for i in range(2)]
        xt1C = [EC(nc.sbuf_tensor("xt1C0", [128, 512], BF16))]
        abc2 = EC(nc.sbuf_tensor("abc2", [128, 512], BF16))
        bbc2 = EC(nc.sbuf_tensor("bbc2", [128, 512], BF16))
        lgb = EC(nc.sbuf_tensor("lgb", [128, 32], F32))
        mxb = EC(nc.sbuf_tensor("mxb", [128, 4], F32))
        mskb = EC(nc.sbuf_tensor("mskb", [128, 32], BF16))
        mskT = EC(nc.sbuf_tensor("mskT", [8, 512], BF16))
        m2b = EC(nc.sbuf_tensor("m2b", [128, 512], BF16))
        hmb = EC(nc.sbuf_tensor("hmb", [128, T], BF16))

        psY = [EC(nc.psum_tensor(f"psY{i}", [128, 512], F32)) for i in range(2)]
        psA2 = EC(nc.psum_tensor("psA2", [128, 512], F32))
        psB2s = EC(nc.psum_tensor("psB2s", [128, 512], F32))
        psMix2 = EC(nc.psum_tensor("psMix2", [128, 1024], F32))
        psMT = EC(nc.psum_tensor("psMT", [8, 512], BF16))
        psEx = EC(nc.psum_tensor("psEx", [128, 512], F32))
        psR = psA2
        psHx = psB2s
        psB2 = psY

        @block.sync
        def _(sy):
            sy.wait_ge(sASs[0], 128)
            sy.wait_ge(sASs[1], 128)
            sy.wait_ge(sem["sPTf"], 64)
            sy.wait_ge(sem["sPT"], 64)
            sy.wait_ge(sem["sE"], 4096)
            for oc in range(16):
                if oc >= 2:
                    sy.wait_ge(sALs[oc % 2], 16 * (oc // 2))
                sy.dma_start(out=bufC[:, 2048 * oc:2048 * oc + 2048],
                             in_=attd[128 * oc:128 * oc + 128, :]).then_inc(sALs[oc % 2], 16)
            for mt in range(16):
                slot = mt % 2
                if mt < 2:
                    sy.wait_ge(sXA[slot], marks["xa_end"][slot])
                else:
                    sy.wait_ge(sem["sYf"], 4 * (mt - 1))
                sy.dma_start(out=xbuf[slot][:], in_=xh[:, mt, :]
                             ).then_inc(sXA[slot], 16)
            for mt in range(16):
                sy.wait_ge(sem["sBf"], 4 * (mt + 1))
                sy.dma_start(out=outT[128 * mt:128 * mt + 128, :],
                             in_=ostg[mt % 2][:]).then_inc(sODs[mt % 2], 16)
            if "x1" in TAPS:
                sy.wait_ge(sem["sYf"], 64)
                sy.dma_start(out=taps["x1"][:, :], in_=bufB[:]).then_inc(sem["sOD"], 16)
            if "xn2" in TAPS:
                sy.wait_ge(sem["sXN2"], 64)
                sy.dma_start(out=taps["xn2"][:, :], in_=bufC[:]).then_inc(sem["sOD"], 16)
            if "hm" in TAPS:
                sy.wait_ge(sem["sHr"], 4)
                sy.dma_start(out=taps["hm"][:, :], in_=hmb[:]).then_inc(sem["sOD"], 16)
            sy.wait_ge(sODs[0], 128)
            sy.wait_ge(sODs[1], 128)
            if n_tap:
                sy.wait_ge(sem["sOD"], 16 * n_tap)

        @block.gpsimd
        def _(gq):
            for i, src in enumerate([wo5[:, mt, :, :] for mt in range(16)]
                                    + [wb5[:, mt, :, :] for mt in range(16)]):
                oci = 48 + i
                slot = oci % 3
                if i < 3:
                    gq.wait_ge(sem["sQm"], 2 * (oci - 2))
                elif i - 3 < 16:
                    gq.wait_ge(sem["sYm"], 4 * (i - 3 + 1))
                else:
                    gq.wait_ge(sem["sBm"], 4 * (i - 3 - 16 + 1))
                gq.dma_start(out=wbuf[slot][:],
                             in_=src.rearrange("p a b -> p (a b)")).then_inc(sWL[slot], 16)

        @block.tensor
        def _(t):
            for mt in range(16):
                oci = 48 + mt
                slot = oci % 3
                _tag, _val = wq_wait[oci]
                t.wait_ge(sWL[slot] if _tag == "L" else sWC[slot], _val)
                for qb in range(NQB):
                    idx = mt * 4 + qb
                    if idx >= 2:
                        t.wait_ge(sem["sYf"], idx - 1)
                    for oc in range(16):
                        if mt == 0 and qb == 0:
                            t.wait_ge(sALs[oc % 2], 16 * (oc // 2 + 1))
                        mm = t.matmul(psY[idx % 2][:],
                                      wbuf[slot][:, 128 * oc:128 * oc + 128],
                                      bufC[:, 2048 * oc + 512 * qb:2048 * oc + 512 * qb + 512],
                                      start=(oc == 0), stop=(oc == 15))
                        if oc == 15:
                            mm.then_inc(sem["sYm"], 1)
            t.wait_ge(sem["sYf"], 64)

            def stats2_tb(tb):
                if tb >= 1:
                    t.wait_ge(sem["sP2F"], 4 + tb)
                for dc in range(NDC):
                    idx = 64 + tb * NDC + dc
                    t.matmul(psA2[0:1, :], onC[:],
                             bufB[:, 2048 * dc + 512 * tb:2048 * dc + 512 * tb + 512],
                             start=(dc == 0), stop=(dc == 15))
                    t.wait_ge(sem["sSQ"], idx + 1)
                    t.matmul(psB2s[0:1, :], onC[:], xsqC[idx % 2][:],
                             start=(dc == 0), stop=(dc == 15)).then_inc(sem["sXS"], 1)

            def bcast2_tb(tb):
                t.wait_ge(sem["sSC"], 4 + tb + 1)
                if tb >= 1:
                    t.wait_ge(sem["sBCe"], 4 + tb)
                t.matmul(psMix2[:, 0:512], onR[:],
                         abR[0:1, 512 * tb:512 * tb + 512], start=True, stop=True)
                t.matmul(psMix2[:, 512:1024], onR[:],
                         bR[0:1, 512 * tb:512 * tb + 512],
                         start=True, stop=True).then_inc(sem["sBC"], 2)

            for tb in range(NTB):
                stats2_tb(tb)
            for tb in range(NTB):
                bcast2_tb(tb)
            for qb in range(NQB):
                t.wait_ge(sem["sXN2"], 16 * (qb + 1))
                if qb == 0:
                    t.wait_ge(sem["sSC"], 8)
                if qb >= 1:
                    t.wait_ge(sem["sLG"], qb)
                for qt in range(4):
                    for dc in range(NDC):
                        mm = t.matmul(psR[:, 8 * qt:8 * qt + 8],
                                      bufC[:, 2048 * dc + 512 * qb + 128 * qt:
                                            2048 * dc + 512 * qb + 128 * qt + 128],
                                      rwb[:, 8 * dc:8 * dc + 8],
                                      start=(dc == 0), stop=(dc == 15))
                        if qt == 3 and dc == NDC - 1:
                            mm.then_inc(sem["sRt"], 1)
                t.wait_ge(sem["sMsk"], qb + 1)
                if qb >= 1:
                    t.wait_ge(sem["sMTf"], qb)
                for qt in range(4):
                    tr = t.transpose(psMT[:, 128 * qt:128 * qt + 128],
                                     mskb[:, 8 * qt:8 * qt + 8], idC[:])
                    if qt == 3:
                        tr.then_inc(sem["sMT"], 1)
                t.wait_ge(sem["sMTf"], qb + 1)
                if qb >= 1:
                    t.wait_ge(sem["sM2"], qb)
                t.matmul(psEx[:], e2C[:], mskT[:], start=True, stop=True
                         ).then_inc(sem["sEm"], 1)
                if qb >= 1:
                    t.wait_ge(sem["sHr"], qb)
                for dc in range(NDC):
                    mm = t.matmul(psHx[:], wab[:, 128 * dc:128 * dc + 128],
                                  bufC[:, 2048 * dc + 512 * qb:2048 * dc + 512 * qb + 512],
                                  start=(dc == 0), stop=(dc == 15))
                    if dc == NDC - 1:
                        mm.then_inc(sem["sHm"], 1)
            for mt in range(16):
                oci = 64 + mt
                slot = oci % 3
                _tag, _val = wq_wait[oci]
                t.wait_ge(sWL[slot] if _tag == "L" else sWC[slot], _val)
                for qb in range(NQB):
                    idx = mt * 4 + qb
                    if idx < 2:
                        t.wait_ge(sem["sYf"], 64)
                    if idx >= 2:
                        t.wait_ge(sem["sBf"], idx - 1)
                    if mt == 0:
                        t.wait_ge(sem["sHr"], qb + 1)
                    for dc in range(NDC):
                        t.matmul(psB2[idx % 2][:],
                                 wbuf[slot][:, 128 * dc:128 * dc + 128],
                                 bufC[:, 2048 * dc + 512 * qb:2048 * dc + 512 * qb + 512],
                                 start=(dc == 0), stop=False)
                    t.matmul(psB2[idx % 2][:], wbwb[:, 128 * mt:128 * mt + 128],
                             hmb[:, 512 * qb:512 * qb + 512],
                             start=False, stop=True).then_inc(sem["sBm"], 1)

        @block.scalar
        def _(sc):
            sc.wait_ge(sem["sYf"], 64)
            for tb in range(NTB):
                for dc in range(NDC):
                    idx = 64 + tb * NDC + dc
                    sc.wait_ge(sem["sXS"], idx - 1)
                    sc.activation(out=xsqC[idx % 2][:],
                                  in_=bufB[:, 2048 * dc + 512 * tb:2048 * dc + 512 * tb + 512],
                                  func=AF.Square).then_inc(sem["sSQ"], 1)
                sc.wait_ge(sem["sVAR"], 4 + tb + 1)
                if tb >= 1:
                    sc.wait_ge(sem["sSC"], 4 + tb)
                sc.activation(out=stq_[:], in_=stv_[:], func=AF.Sqrt,
                              bias=epsC[:]).then_inc(sem["sSQT"], 1)

        @block.vector
        def _(v):
            DV = mkDV(v)
            for mt in range(16):
                for qb in range(NQB):
                    idx = mt * 4 + qb
                    v.wait_ge(sem["sYm"], idx + 1)
                    slot = mt % 2
                    v.wait_ge(sXA[slot], marks["xa_dma"][("C", mt)])
                    v.tensor_tensor(bufB[:, 2048 * mt + 512 * qb:2048 * mt + 512 * qb + 512],
                                    psY[idx % 2][:],
                                    xbuf[slot][:, 512 * qb:512 * qb + 512], ALU.add
                                    ).then_inc(sem["sYf"], 1)
            for tb in range(NTB):
                v.wait_ge(sem["sXS"], 64 + 16 * (tb + 1))
                v.wait_ge(sem["sSC"], 4 + tb)
                DV(v.tensor_scalar(stm_[:], psA2[0:1, :], 1.0 / 2048.0, None, ALU.mult))
                v.wait_ge(sem["sDV"], dvn[0])
                v.tensor_scalar(stq_[:], psB2s[0:1, :], 1.0 / 2048.0, None,
                                ALU.mult).then_inc(sem["sP2F"], 1)
                if tb >= 1:
                    v.wait_ge(sem["sSQT"], 4 + tb)
                DV(v.tensor_tensor(stv_[:], stm_[:], stm_[:], ALU.mult))
                v.wait_ge(sem["sDV"], dvn[0])
                v.wait_ge(sem["sP2F"], 4 + tb + 1)
                v.tensor_tensor(stv_[:], stq_[:], stv_[:], ALU.subtract
                                ).then_inc(sem["sVAR"], 1)
                v.wait_ge(sem["sSQT"], 4 + tb + 1)
                DV(v.reciprocal(stv_[:], stq_[:]))
                v.wait_ge(sem["sDV"], dvn[0])
                DV(v.tensor_copy(abR[0:1, 512 * tb:512 * tb + 512], stv_[:]))
                v.wait_ge(sem["sDV"], dvn[0])
                v.scalar_tensor_tensor(bR[0:1, 512 * tb:512 * tb + 512], stm_[:], -1.0,
                                       abR[0:1, 512 * tb:512 * tb + 512],
                                       ALU.mult, ALU.mult).then_inc(sem["sSC"], 1)
            v.wait_ge(sem["sYm"], 64)
            for tb in range(NTB):
                v.wait_ge(sem["sBC"], 8 + 2 * (tb + 1))
                v.wait_ge(sem["sDV"], dvn[0])
                v.wait_ge(sem["sXN2"], 16 * tb)
                DV(v.tensor_copy(abc2[:], psMix2[:, 0:512]))
                v.tensor_copy(bbc2[:], psMix2[:, 512:1024]).then_inc(sem["sBCe"], 1)
                v.wait_ge(sem["sBCe"], 4 + tb + 1)
                for dc in range(NDC):
                    cnt = tb * NDC + dc
                    t1 = xt1C[0]
                    if cnt >= 1:
                        v.wait_ge(sem["sXN2"], cnt)
                    DV(v.tensor_tensor(t1[:],
                                       bufB[:, 2048 * dc + 512 * tb:2048 * dc + 512 * tb + 512],
                                       abc2[:], ALU.mult))
                    v.wait_ge(sem["sDV"], dvn[0])
                    v.tensor_tensor(bufC[:, 2048 * dc + 512 * tb:2048 * dc + 512 * tb + 512],
                                    t1[:], bbc2[:], ALU.add
                                    ).then_inc(sem["sXN2"], 1)
            for qb in range(NQB):
                v.wait_ge(sem["sRt"], qb + 1)
                v.tensor_copy(lgb[:], psR[:, 0:32]).then_inc(sem["sLG"], 1)
                v.wait_ge(sem["sLG"], qb + 1)
                for qt in range(4):
                    DV(v.reduce_max(mxb[:, qt:qt + 1], lgb[:, 8 * qt:8 * qt + 8], axis=AX.X))
                v.wait_ge(sem["sDV"], dvn[0])
                for qt in range(4):
                    mk = v.tensor_scalar(mskb[:, 8 * qt:8 * qt + 8], lgb[:, 8 * qt:8 * qt + 8],
                                         mxb[:, qt:qt + 1], None, ALU.is_ge)
                    if qt == 3:
                        mk.then_inc(sem["sMsk"], 1)
                v.wait_ge(sem["sMT"], qb + 1)
                v.tensor_copy(mskT[:], psMT[:]).then_inc(sem["sMTf"], 1)
                v.wait_ge(sem["sEm"], qb + 1)
                v.tensor_copy(m2b[:], psEx[:]).then_inc(sem["sM2"], 1)
                v.wait_ge(sem["sHm"], qb + 1)
                v.wait_ge(sem["sM2"], qb + 1)
                v.tensor_tensor(hmb[:, 512 * qb:512 * qb + 512], psHx[:], m2b[:],
                                ALU.mult).then_inc(sem["sHr"], 1)
            for mt in range(16):
                for qb in range(NQB):
                    idx = mt * 4 + qb
                    v.wait_ge(sem["sBm"], idx + 1)
                    if qb == 0 and mt >= 2:
                        v.wait_ge(sODs[mt % 2], 16 * (mt // 2))
                    v.scalar_tensor_tensor(
                        ostg[mt % 2][:, 512 * qb:512 * qb + 512],
                        psB2[idx % 2][:], bbb[:, mt:mt + 1],
                        bufB[:, 2048 * mt + 512 * qb:2048 * mt + 512 * qb + 512],
                        ALU.add, ALU.add).then_inc(sem["sBf"], 1)

        cctx.close()
    return nc


def _host_layout(inputs):
    x = inputs["L0_attn_x"]
    w_qkv = inputs["L0_attn_w_qkv"]
    w_out = inputs["L0_attn_w_out"]
    W_base = inputs["L0_ffn_W_base"]
    b_base = inputs["L0_ffn_b_base"]
    router_W = inputs["L0_ffn_router_W"]
    A = inputs["L0_ffn_A"]
    Bw = inputs["L0_ffn_Bw"]

    wq5 = np.ascontiguousarray(
        np.asarray(w_qkv, np.float32).reshape(48, 128, 16, 128).transpose(3, 0, 2, 1))
    wo5 = np.ascontiguousarray(
        np.asarray(w_out, np.float32).reshape(16, 128, 16, 128).transpose(3, 0, 2, 1))
    wb5 = np.ascontiguousarray(
        np.asarray(W_base, np.float32).reshape(16, 128, 16, 128).transpose(3, 0, 2, 1))
    wa5 = np.ascontiguousarray(
        np.asarray(A, np.float32).reshape(128, 16, 128).transpose(2, 1, 0))
    wbwp = np.ascontiguousarray(
        np.transpose(np.asarray(Bw, np.float32), (0, 2, 1)).reshape(128, 2048))
    rw5 = np.ascontiguousarray(
        np.asarray(router_W, np.float32).T.reshape(16, 128, 8).transpose(1, 0, 2))
    bb = np.ascontiguousarray(np.asarray(b_base, np.float32).reshape(16, 128).T)

    inv_freq = (1.0 / np.float32(10000.0) **
                (np.arange(0, 128, 2, dtype=np.float32) / np.float32(128.0))).astype(np.float32)
    maps = []
    for c in range(8):
        b = c // 4
        j = c % 4
        ts0 = T * j
        xc = np.asarray(x[b, ts0:ts0 + T, :], np.float32)
        xhc = np.ascontiguousarray(xc.T.reshape(16, 128, T).transpose(1, 0, 2))
        tloc = np.arange(ts0, ts0 + T, dtype=np.float32)
        ang = np.concatenate([np.outer(tloc, inv_freq)] * 2, axis=1).astype(np.float32)
        cosTc = np.ascontiguousarray(np.cos(ang).T.astype(np.float32))
        sinTc = np.ascontiguousarray(np.sin(ang).T.astype(np.float32))
        maps.append({"xh": xhc, "wq5": wq5, "wo5": wo5, "wb5": wb5, "wa5": wa5,
                     "wbw": wbwp, "rw5": rw5, "bbi": bb, "cosi": cosTc, "sini": sinTc})
    return maps


def kernel(**inputs):
    maps = _host_layout(inputs)
    key = (frozenset(TAPS), PHASES)
    if key not in _NC_CACHE:
        _NC_CACHE[key] = build_program()
    nc = _NC_CACHE[key]
    res = run_bass_kernel_spmd(nc, maps, core_ids=list(range(8)))
    out = np.empty((B, S, D), np.float32)
    for c in range(8):
        b = c // 4
        j = c % 4
        out[b, T * j:T * j + T, :] = res.results[c]["outT"].T
    if TAPS:
        kernel.last_results = res.results
    return out


# version 22
# speedup vs baseline: 1.0683x; 1.0105x over previous
import math
import contextlib
import numpy as np
import ml_dtypes
import concourse.bass as bass
import concourse.mybir as mybir
from concourse.bass_utils import run_bass_kernel_spmd

F32 = mybir.dt.float32
BF16 = mybir.dt.bfloat16
AF = mybir.ActivationFunctionType
ALU = mybir.AluOpType
AX = mybir.AxisListType

B, S, D = 2, 8192, 2048
H, DH = 16, 128
T = 2048
GROUPS = [[0, 1, 2, 3], [4, 5, 6, 7]]
SCALE = 1.0 / math.sqrt(128.0)
EPS = 1e-5
NTB, NDC, NQB, NPAIR = 4, 16, 4, 32
NG = H * NQB * NPAIR
NEP = H * NQB
ALPHA_SCALING = 2.0

TAPS = set()
PHASES = "ABC"

_NC_CACHE = {}


def _consts():
    Rm = np.zeros((128, 128), np.float32)
    for d in range(64):
        Rm[d, d + 64] = -1.0
    for d in range(64, 128):
        Rm[d, d - 64] = 1.0
    RT = np.ascontiguousarray(Rm.T).astype(ml_dtypes.bfloat16)
    ident = np.eye(128, dtype=ml_dtypes.bfloat16)
    ones = np.ones((128, 1), dtype=ml_dtypes.bfloat16)
    onesrow = np.ones((1, 128), dtype=ml_dtypes.bfloat16)
    E2 = np.zeros((8, 128), dtype=ml_dtypes.bfloat16)
    for e in range(8):
        E2[e, 16 * e:16 * e + 16] = ALPHA_SCALING
    return RT, ident, ones, onesrow, E2


def build_program():
    nc = bass.Bass("TRN2", target_bir_lowering=False, debug=False)

    xh = nc.dram_tensor("xh", [128, NDC, T], F32, kind="ExternalInput").ap()
    wq5 = nc.dram_tensor("wq5", [128, 48, NDC, 128], F32, kind="ExternalInput").ap()
    wo5 = nc.dram_tensor("wo5", [128, 16, 16, 128], F32, kind="ExternalInput").ap()
    wb5 = nc.dram_tensor("wb5", [128, 16, NDC, 128], F32, kind="ExternalInput").ap()
    wa5 = nc.dram_tensor("wa5", [128, NDC, 128], F32, kind="ExternalInput").ap()
    wbw = nc.dram_tensor("wbw", [128, 2048], F32, kind="ExternalInput").ap()
    rw5 = nc.dram_tensor("rw5", [128, NDC, 8], F32, kind="ExternalInput").ap()
    bbi = nc.dram_tensor("bbi", [128, 16], F32, kind="ExternalInput").ap()
    cosi = nc.dram_tensor("cosi", [128, T], F32, kind="ExternalInput").ap()
    sini = nc.dram_tensor("sini", [128, T], F32, kind="ExternalInput").ap()
    outT = nc.dram_tensor("outT", [D, T], F32, kind="ExternalOutput").ap()

    cc_inK = nc.dram_tensor("cc_inK", [D, T], BF16, kind="Internal").ap()
    cc_outK = nc.dram_tensor("cc_outK", [4 * D, T], BF16, kind="Internal").ap()
    cc_inV = nc.dram_tensor("cc_inV", [4 * T, 512], BF16, kind="Internal").ap()
    cc_outV = nc.dram_tensor("cc_outV", [16 * T, 512], BF16, kind="Internal").ap()
    attd = nc.dram_tensor("attd", [D, T], BF16, kind="Internal").ap()

    RTc, IDc, ONESc, ONESRc, E2c = _consts()

    taps = {}
    for name, shape, dt in [("xn", [128, NDC * T], BF16), ("q", [128, H * T], BF16),
                            ("cc", [2 * D, T], BF16), ("att", [D, T], BF16),
                            ("x1", [128, 16 * T], BF16), ("xn2", [128, NDC * T], BF16),
                            ("hm", [128, T], BF16)]:
        if name in TAPS:
            taps[name] = nc.dram_tensor("t_" + name, shape, dt, kind="ExternalOutput").ap()
    n_tap = len(taps)
    n_tap_ab = sum(1 for nm in ("xn", "q", "cc", "att") if nm in TAPS)

    g = contextlib.ExitStack()
    EG = g.enter_context

    bufB = EG(nc.sbuf_tensor("bufB", [128, H * T], BF16))
    rtC = EG(nc.sbuf_tensor("rtC", [128, 128], BF16))
    idC = EG(nc.sbuf_tensor("idC", [128, 128], BF16))
    onC = EG(nc.sbuf_tensor("onC", [128, 1], BF16))
    onR = EG(nc.sbuf_tensor("onR", [1, 128], BF16))
    e2C = EG(nc.sbuf_tensor("e2C", [8, 128], BF16))
    wab = EG(nc.sbuf_tensor("wab", [128, 2048], BF16))
    wbwb = EG(nc.sbuf_tensor("wbwb", [128, 2048], BF16))
    rwb = EG(nc.sbuf_tensor("rwb", [128, NDC * 8], BF16))
    bbb = EG(nc.sbuf_tensor("bbb", [128, 16], F32))
    epsC = EG(nc.sbuf_tensor("epsC", [1, 1], F32))
    wbuf = [EG(nc.sbuf_tensor(f"wbuf{i}", [128, 2048], BF16)) for i in range(3)]
    xbuf = [EG(nc.sbuf_tensor(f"xbuf{i}", [128, 2048], F32)) for i in range(2)]
    abR = EG(nc.sbuf_tensor("abR", [1, T], BF16))
    bR = EG(nc.sbuf_tensor("bR", [1, T], BF16))
    stm_ = EG(nc.sbuf_tensor("stm_", [1, 512], F32))
    stq_ = EG(nc.sbuf_tensor("stq_", [1, 512], F32))
    stv_ = EG(nc.sbuf_tensor("stv_", [1, 512], F32))

    SN = ["sCN", "sDV", "sSQ", "sCB", "sXC", "sXS", "sST", "sVAR", "sSQT",
          "sSC", "sBC", "sXN", "sQm", "sQf", "sKR", "sRm", "sRf", "sKV", "sCI",
          "sCC", "sPH", "sVT", "sVTf", "sVR", "sS", "sE", "sPV", "sOn", "sOf",
          "sPT", "sPTf", "sAS", "sAL", "sYm", "sYf", "sX1", "sC1", "sLG",
          "sMsk", "sMT", "sMTf", "sEm", "sM2", "sHm", "sHr", "sBm", "sBf",
          "sOS", "sOD", "sXN2", "sRt", "sOnb", "sBCe", "sVA", "sVM", "sVB", "sP2F"]
    sem = {nm: EG(nc.semaphore(nm)) for nm in SN}
    sXA = [EG(nc.semaphore(f"sXA{i}")) for i in range(4)]
    sWL = [EG(nc.semaphore(f"sWL{i}")) for i in range(3)]
    sWF = [EG(nc.semaphore(f"sWF{i}")) for i in range(2)]
    sKL = [EG(nc.semaphore(f"sKL{i}")) for i in range(2)]
    sKF = [EG(nc.semaphore(f"sKF{i}")) for i in range(2)]
    sVL = [EG(nc.semaphore(f"sVL{i}")) for i in range(2)]
    sVG = [EG(nc.semaphore(f"sVG{i}")) for i in range(2)]
    sVF = [EG(nc.semaphore(f"sVF{i}")) for i in range(2)]
    sCIK = [EG(nc.semaphore(f"sCIK{i}")) for i in range(2)]
    sCIV = [EG(nc.semaphore(f"sCIV{i}")) for i in range(2)]
    sXW = [EG(nc.semaphore(f"sXW{i}")) for i in range(2)]
    sWC = [EG(nc.semaphore(f"sWC{i}")) for i in range(3)]
    sASs = [EG(nc.semaphore(f"sASs{i}")) for i in range(2)]
    sODs = [EG(nc.semaphore(f"sODs{i}")) for i in range(2)]
    sALs = [EG(nc.semaphore(f"sALs{i}")) for i in range(2)]

    block = EG(nc.Block())

    chunk_order = []
    for cg in range(4):
        chunk_order += [16 + 4 * cg + i for i in range(4)]
        chunk_order += [32 + 4 * cg + i for i in range(4)]
    chunk_order += list(range(16))
    gp_info, rope_gps, contrib_gps = [], [], []
    for oci, oc in enumerate(chunk_order):
        kind = "K" if 16 <= oc < 32 else ("V" if oc >= 32 else "Q")
        for pr in range(2):
            gp_info.append((oc, pr, kind))
            gp = oci * 2 + pr
            if kind in ("K", "Q"):
                rope_gps.append(gp)
            if kind in ("K", "V"):
                contrib_gps.append(gp)
    rope_idx = {gp: i for i, gp in enumerate(rope_gps)}
    contrib_idx = {gp: i for i, gp in enumerate(contrib_gps)}
    _kd = [0, 0]
    _vc = 0
    vci_of = {}
    _rel = {}
    for _ci, _gp in enumerate(contrib_gps):
        if gp_info[_gp][2] == "K":
            _kd[_ci % 2] += 1
            _rel[_ci] = ("K", _ci % 2, 16 * _kd[_ci % 2])
        else:
            vci_of[_gp] = _vc
            _rel[_ci] = ("VD", 16 * (_vc + 1))
            _vc += 1
    kv_release = {ci: _rel[ci - 2] for ci in range(2, 64)}
    pairs = [(h, qb, lp) for h in range(H) for qb in range(NQB) for lp in range(NPAIR)]

    def bar(eng, which):
        eng.sem_inc(sem["sPH"], 1)
        eng.wait_ge(sem["sPH"], 5 * which)

    with g:
        actx = contextlib.ExitStack()
        EA = actx.enter_context
        xnA = EA(nc.sbuf_tensor("xnA", [128, NDC * T], BF16))
        xcb = [EA(nc.sbuf_tensor(f"xcb{i}", [128, 512], BF16)) for i in range(2)]
        xsq = [EA(nc.sbuf_tensor(f"xsq{i}", [128, 512], BF16)) for i in range(2)]
        krawB = EA(nc.sbuf_tensor("krawB", [128, 1024], BF16))
        s1b = EA(nc.sbuf_tensor("s1b", [128, 1024], BF16))
        s2b = EA(nc.sbuf_tensor("s2b", [128, 1024], BF16))
        xt1 = [EA(nc.sbuf_tensor(f"xt1_{i}", [128, 512], BF16)) for i in range(2)]
        kvst = [EA(nc.sbuf_tensor(f"kvst{i}", [128, 1024], BF16)) for i in range(2)]
        abc = EA(nc.sbuf_tensor("abc", [128, 512], BF16))
        bbc = EA(nc.sbuf_tensor("bbc", [128, 512], BF16))
        cosT = EA(nc.sbuf_tensor("cosT", [128, T], BF16))
        sinT = EA(nc.sbuf_tensor("sinT", [128, T], BF16))
        psMix = EA(nc.psum_tensor("psMix", [128, 1024], F32))
        psQ = [EA(nc.psum_tensor(f"psQ{i}", [128, 1024], F32)) for i in range(2)]
        _sctx = contextlib.ExitStack()
        psAB = _sctx.enter_context(nc.psum_tensor("psAB", [1, 1024], F32))
        _sctx.close()
        psVtA = [EA(nc.psum_tensor("psVtA0", [128, 1024], BF16))]

        NCN = 12 * 16
        marks = {"xn_tb": []}
        xa = [0, 0]
        xa_dma, xa_t1m = {}, {}
        for u in range(32):
            sl = u % 2
            xa[sl] += 16
            xa_dma[u] = xa[sl]
            if u >= 16:
                for j in range(4):
                    xa[sl] += 8
                    xa_t1m[(u, j)] = xa[sl]
        xa_end = list(xa)
        for mt in range(16):
            sl = mt % 2
            xa[sl] += 16
            xa_dma[("C", mt)] = xa[sl]
        marks["xa_dma"] = xa_dma
        marks["xa_t1m"] = xa_t1m
        marks["xa_end"] = xa_end
        dvn = [0]
        N0W = 14

        def _is_cast(_o):
            return N0W <= _o < 48 and _o % 2 == 1

        _wl = [0, 0, 0]
        _wc = [0, 0, 0]
        wq_wait = {}
        for _o in range(80):
            if _is_cast(_o):
                _wc[_o % 3] += 1
                wq_wait[_o] = ("C", _wc[_o % 3])
            else:
                _wl[_o % 3] += 16
                wq_wait[_o] = ("L", _wl[_o % 3])
        cast_chunks = [c for c in range(48) if _is_cast(c)]
        cast_pos = {c: i for i, c in enumerate(cast_chunks)}
        stage_xw = {}

        def mkDV(v):
            def DV(inst):
                inst.then_inc(sem["sDV"], 1)
                dvn[0] += 1
                return inst
            return DV

        @block.gpsimd
        def _(gq):
            for dst, src in [(cosT, cosi[:, :]), (sinT, sini[:, :]),
                             (rtC, nc.inline_tensor(RTc, "c_rt").ap()),
                             (idC, nc.inline_tensor(IDc, "c_id").ap()),
                             (onC, nc.inline_tensor(ONESc, "c_on").ap()),
                             (onR, nc.inline_tensor(ONESRc, "c_or").ap()),
                             (e2C, nc.inline_tensor(E2c, "c_e2").ap()),
                             (wab, wa5.rearrange("p dc er -> p (dc er)")),
                             (wbwb, wbw[:, :]),
                             (rwb, rw5.rearrange("p dc e -> p (dc e)")),
                             (bbb, bbi[:, :]),
                             (epsC, nc.inline_tensor(np.full((1, 1), EPS, np.float32), "c_eps").ap())]:
                gq.dma_start(out=dst[:], in_=src).then_inc(sem["sCN"], 16)
            if PHASES == "A0":
                return
            agk_after = {8: 0}
            agv_after = {14: 0}
            for oci in range(48):
                if _is_cast(oci):
                    continue
                oc = chunk_order[oci]
                slot = oci % 3
                if oci >= 3:
                    gq.wait_ge(sem["sQm"], 2 * (oci - 2))
                gq.dma_start(out=wbuf[slot][:],
                             in_=wq5[:, oc, :, :].rearrange("p dc o -> p (dc o)")
                             ).then_inc(sWL[slot], 16)
                if oci in agk_after:
                    cg = agk_after[oci]
                    gq.wait_ge(sCIK[0], 64 * (cg + 1))
                    gq.wait_ge(sCIK[1], 64 * (cg + 1))
                    gq.collective_compute(
                        "AllGather", ALU.bypass, replica_groups=GROUPS,
                        ins=[cc_inK[512 * cg:512 * cg + 512, :].opt()],
                        outs=[cc_outK[2048 * cg:2048 * cg + 2048, :].opt()]
                    ).then_inc(sem["sCC"], 1)
                if oci in agv_after:
                    cg = agv_after[oci]
                    gq.wait_ge(sCIV[0], 16 * 8 * (cg + 1))
                    gq.collective_compute(
                        "AllGather", ALU.bypass, replica_groups=GROUPS,
                        ins=[cc_inV[2048 * cg:2048 * cg + 2048, :].opt()],
                        outs=[cc_outV[8192 * cg:8192 * cg + 8192, :].opt()]
                    ).then_inc(sem["sCC"], 1)
            for cg in range(1, 4):
                gq.wait_ge(sCIK[0], 64 * (cg + 1))
                gq.wait_ge(sCIK[1], 64 * (cg + 1))
                gq.collective_compute(
                    "AllGather", ALU.bypass, replica_groups=GROUPS,
                    ins=[cc_inK[512 * cg:512 * cg + 512, :].opt()],
                    outs=[cc_outK[2048 * cg:2048 * cg + 2048, :].opt()]
                ).then_inc(sem["sCC"], 1)
                gq.wait_ge(sCIV[0], 16 * 8 * (cg + 1))
                gq.collective_compute(
                    "AllGather", ALU.bypass, replica_groups=GROUPS,
                    ins=[cc_inV[2048 * cg:2048 * cg + 2048, :].opt()],
                    outs=[cc_outV[8192 * cg:8192 * cg + 8192, :].opt()]
                ).then_inc(sem["sCC"], 1)


        @block.sync
        def _(sy):
            for p in range(2):
                for tb in range(NTB):
                    for qd in range(4):
                        u = p * 16 + tb * 4 + qd
                        if u % 2 == 1:
                            continue
                        if 2 <= u < 18:
                            sy.wait_ge(sem["sXC"], 4 * (u - 1))
                            sy.wait_ge(sem["sXS"], 4 * (u - 1))
                        elif u >= 18:
                            sy.wait_ge(sXA[0], marks["xa_t1m"][(u - 2, 3)])
                        sy.dma_start(out=xbuf[0][:].rearrange("p (a b) -> p a b", a=4, b=512),
                                     in_=xh[:, 4 * qd:4 * qd + 4, 512 * tb:512 * tb + 512]
                                     ).then_inc(sXA[0], 16)

            if PHASES == "A0":
                return
            sxw = [0, 0]
            for j in range(48):
                c = j + 2
                if _is_cast(c):
                    ip = cast_pos[c]
                    xs = ip % 2
                    if ip < 2:
                        sy.wait_ge(sXA[xs], marks["xa_t1m"][(30 + xs, 3)])
                    else:
                        pc = cast_chunks[ip - 2]
                        sy.wait_ge(sWC[pc % 3], wq_wait[pc][1])
                    sy.dma_start(out=xbuf[xs][:],
                                 in_=wq5[:, chunk_order[c], :, :]
                                 .rearrange("p dc o -> p (dc o)")
                                 ).then_inc(sXW[xs], 16)
                    sxw[xs] += 16
                    stage_xw[c] = sxw[xs]
                if j > 31:
                    continue
                for ci in (2 * j, 2 * j + 1):
                    gp = contrib_gps[ci]
                    oc, pr, kind = gp_info[gp]
                    slot = ci % 2
                    if kind == "K":
                        sy.wait_ge(sem["sKV"], ci + 1)
                        hh = oc - 16
                        row = 512 * (hh // 4) + 128 * (hh % 4)
                        sy.dma_start(out=cc_inK[row:row + 128,
                                                1024 * pr:1024 * pr + 1024],
                                     in_=kvst[slot][:]).then_inc(sCIK[slot], 16)
                    else:
                        vci = vci_of[gp]
                        sy.wait_ge(sem["sVB"], vci + 1)
                        hh = oc - 32
                        dst = cc_inV[2048 * (hh // 4) + 1024 * pr:
                                     2048 * (hh // 4) + 1024 * pr + 1024,
                                     128 * (hh % 4):128 * (hh % 4) + 128]
                        sy.dma_start(out=dst.rearrange("(kt p) d -> p kt d", p=128),
                                     in_=kvst[slot][:]
                                     .rearrange("p (kt d) -> p kt d", d=128)
                                     ).then_inc(sCIV[0], 16)

        @block.scalar
        def _(sc):
            def xload(u):
                tb2, qd = (u % 16) // 4, (u % 16) % 4
                if 2 <= u < 18:
                    sc.wait_ge(sem["sXC"], 4 * (u - 1))
                    sc.wait_ge(sem["sXS"], 4 * (u - 1))
                elif u >= 18:
                    sc.wait_ge(sXA[1], marks["xa_t1m"][(u - 2, 3)])
                sc.dma_start(out=xbuf[1][:].rearrange("p (a b) -> p a b", a=4, b=512),
                             in_=xh[:, 4 * qd:4 * qd + 4, 512 * tb2:512 * tb2 + 512]
                             ).then_inc(sXA[1], 16)

            xload(1)
            for tb in range(NTB):
                for dc in range(NDC):
                    idx = tb * NDC + dc
                    u = tb * 4 + dc // 4
                    slot = u % 2
                    sc.wait_ge(sXA[slot], marks["xa_dma"][u])
                    if idx >= 2:
                        sc.wait_ge(sem["sXS"], idx - 1)
                    sc.activation(out=xsq[idx % 2][:],
                                  in_=xbuf[slot][:, 512 * (dc % 4):512 * (dc % 4) + 512],
                                  func=AF.Square).then_inc(sem["sSQ"], 1)
                    if idx % 8 == 0 and idx > 0 and idx // 4 + 1 < 16:
                        xload(idx // 4 + 1)
                sc.wait_ge(sem["sVAR"], tb + 1)
                if tb >= 1:
                    sc.wait_ge(sem["sSC"], tb)
                sc.activation(out=stq_[:], in_=stv_[:], func=AF.Sqrt,
                              bias=epsC[:]).then_inc(sem["sSQT"], 1)
            for u in range(17, 32, 2):
                xload(u)

        @block.vector
        def _(v):
            DV = mkDV(v)
            for tb in range(NTB):
                for dc in range(NDC):
                    idx = tb * NDC + dc
                    u = tb * 4 + dc // 4
                    slot = u % 2
                    v.wait_ge(sXA[slot], marks["xa_dma"][u])
                    if idx >= 2:
                        v.wait_ge(sem["sXC"], idx - 1)
                    v.tensor_copy(xcb[idx % 2][:],
                                  xbuf[slot][:, 512 * (dc % 4):512 * (dc % 4) + 512]
                                  ).then_inc(sem["sCB"], 1)
                v.wait_ge(sem["sXS"], 16 * (tb + 1))
                if tb >= 1:
                    v.wait_ge(sem["sSC"], tb)
                DV(v.tensor_scalar(stm_[:], psAB[0:1, 0:512], 1.0 / 2048.0, None, ALU.mult))
                v.wait_ge(sem["sDV"], dvn[0])
                v.tensor_scalar(stq_[:], psAB[0:1, 512:1024], 1.0 / 2048.0, None,
                                ALU.mult).then_inc(sem["sP2F"], 1)
                if tb >= 1:
                    v.wait_ge(sem["sSQT"], tb)
                DV(v.tensor_tensor(stv_[:], stm_[:], stm_[:], ALU.mult))
                v.wait_ge(sem["sDV"], dvn[0])
                v.wait_ge(sem["sP2F"], tb + 1)
                v.tensor_tensor(stv_[:], stq_[:], stv_[:], ALU.subtract
                                ).then_inc(sem["sVAR"], 1)
                v.wait_ge(sem["sSQT"], tb + 1)
                DV(v.reciprocal(stv_[:], stq_[:]))
                v.wait_ge(sem["sDV"], dvn[0])
                DV(v.tensor_copy(abR[0:1, 512 * tb:512 * tb + 512], stv_[:]))
                v.wait_ge(sem["sDV"], dvn[0])
                v.scalar_tensor_tensor(bR[0:1, 512 * tb:512 * tb + 512], stm_[:], -1.0,
                                       abR[0:1, 512 * tb:512 * tb + 512],
                                       ALU.mult, ALU.mult).then_inc(sem["sSC"], 1)
            for tb in range(NTB):
                v.wait_ge(sem["sBC"], 2 * (tb + 1))
                v.wait_ge(sem["sDV"], dvn[0])
                DV(v.tensor_copy(abc[:], psMix[:, 0:512]))
                v.tensor_copy(bbc[:], psMix[:, 512:1024]).then_inc(sem["sBCe"], 1)
                v.wait_ge(sem["sDV"], dvn[0])
                v.wait_ge(sem["sBCe"], tb + 1)
                for dc in range(NDC):
                    idx = 64 + tb * NDC + dc
                    u = 16 + tb * 4 + dc // 4
                    slot = u % 2
                    v.wait_ge(sXA[slot], marks["xa_dma"][u])
                    t1 = xt1[dc % 2]
                    if dc >= 2 or tb >= 1:
                        v.wait_ge(sem["sDV"], dvn[0] - 1)
                    v.tensor_tensor(t1[:],
                                    xbuf[slot][:, 512 * (dc % 4):512 * (dc % 4) + 512],
                                    abc[:], ALU.mult).then_inc(sXA[slot], 8)
                    v.wait_ge(sXA[slot], marks["xa_t1m"][(u, dc % 4)])
                    DV(v.tensor_tensor(
                        xnA[:, 2048 * dc + 512 * tb:2048 * dc + 512 * tb + 512],
                        t1[:], bbc[:], ALU.add))
                marks["xn_tb"].append(dvn[0])
            for gp in (range(96) if PHASES != "A0" else []):
                if gp >= 2 and (gp - 2) in vci_of:
                    vci2 = vci_of[gp - 2]
                    v.wait_ge(sem["sVA"], vci2 + 1)
                    v.tensor_copy(kvst[contrib_idx[gp - 2] % 2][:], psVtA[0][:]
                                  ).then_inc(sem["sVB"], 1)
                if gp % 2 == 1:
                    c = gp // 2 + 2
                    if _is_cast(c):
                        ws = c % 3
                        xs = cast_pos[c] % 2
                        v.wait_ge(sem["sQm"], 2 * (c - 2))
                        v.wait_ge(sXW[xs], stage_xw[c])
                        v.tensor_copy(wbuf[ws][:], xbuf[xs][:]
                                      ).then_inc(sWC[ws], 1)
                oc, pr, kind = gp_info[gp]
                v.wait_ge(sem["sQm"], gp + 1)
                if gp >= 1:
                    v.wait_ge(sem["sKV"], gp)
                if kind == "V":
                    ci = contrib_idx[gp]
                    slot = ci % 2
                    if ci >= 2:
                        rel = kv_release[ci]
                        if rel[0] == "K":
                            v.wait_ge(sCIK[rel[1]], rel[2])
                        else:
                            v.wait_ge(sCIV[0], rel[1])
                    v.tensor_copy(kvst[slot][:], psQ[gp % 2][:]).then_inc(sem["sKV"], 1)
                else:
                    ri = rope_idx[gp]
                    if ri >= 1:
                        v.wait_ge(sem["sRm"], ri)
                    v.tensor_copy(krawB[:], psQ[gp % 2][:]).then_inc(sem["sKR"], 1)
                    DV(v.tensor_tensor(s1b[:], psQ[gp % 2][:],
                                       cosT[:, 1024 * pr:1024 * pr + 1024], ALU.mult))
                    v.wait_ge(sem["sRm"], ri + 1)
                    v.tensor_tensor(s2b[:], psMix[:],
                                    sinT[:, 1024 * pr:1024 * pr + 1024], ALU.mult
                                    ).then_inc(sem["sRf"], 1)
                    v.wait_ge(sem["sDV"], dvn[0])
                    v.wait_ge(sem["sRf"], ri + 1)
                    if kind == "K":
                        ci = contrib_idx[gp]
                        slot = ci % 2
                        if ci >= 2:
                            rel = kv_release[ci]
                            if rel[0] == "K":
                                v.wait_ge(sCIK[rel[1]], rel[2])
                            else:
                                v.wait_ge(sCIV[0], rel[1])
                        dst = kvst[slot][:]
                    else:
                        dst = bufB[:, 2048 * oc + 1024 * pr:2048 * oc + 1024 * pr + 1024]
                    v.tensor_tensor(dst, s1b[:], s2b[:], ALU.add).then_inc(sem["sKV"], 1)

        @block.tensor
        def _(t):
            t.wait_ge(sem["sCN"], NCN)

            def stats_tb(tb):
                if tb >= 1:
                    t.wait_ge(sem["sP2F"], tb)
                for dc in range(NDC):
                    idx = tb * NDC + dc
                    t.wait_ge(sem["sCB"], idx + 1)
                    t.wait_ge(sem["sSQ"], idx + 1)
                    t.matmul(psAB[0:1, 0:512], onC[:], xcb[idx % 2][:],
                             start=(dc == 0), stop=(dc == 15),
                             skip_group_check=True).then_inc(sem["sXC"], 1)
                    t.matmul(psAB[0:1, 512:1024], onC[:], xsq[idx % 2][:],
                             start=(dc == 0), stop=(dc == 15),
                             skip_group_check=True).then_inc(sem["sXS"], 1)

            def bcast_tb(tb):
                t.wait_ge(sem["sSC"], tb + 1)
                if tb >= 1:
                    t.wait_ge(sem["sBCe"], tb)
                t.matmul(psMix[:, 0:512], onR[:], abR[0:1, 512 * tb:512 * tb + 512],
                         start=True, stop=True)
                t.matmul(psMix[:, 512:1024], onR[:], bR[0:1, 512 * tb:512 * tb + 512],
                         start=True, stop=True).then_inc(sem["sBC"], 2)

            for tb in range(NTB):
                stats_tb(tb)
            for tb in range(NTB):
                bcast_tb(tb)
            if PHASES == "A0":
                return
            pend_rope = []
            pend_v = []

            def emit_vtr(vgp):
                vci = vci_of[vgp]
                t.wait_ge(sem["sKV"], vgp + 1)
                if vci >= 1:
                    t.wait_ge(sem["sVB"], vci)
                kslot = contrib_idx[vgp] % 2
                for i8 in range(8):
                    tr = t.transpose(psVtA[0][:, 128 * i8:128 * i8 + 128],
                                     kvst[kslot][:, 128 * i8:128 * i8 + 128], idC[:])
                    if i8 == 7:
                        tr.then_inc(sem["sVA"], 1)

            def emit_rot(rgp):
                ri = rope_idx[rgp]
                t.wait_ge(sem["sKR"], ri + 1)
                if ri >= 1:
                    t.wait_ge(sem["sRf"], ri)
                t.matmul(psMix[:, 0:512], rtC[:], krawB[:, 0:512], start=True, stop=True)
                t.matmul(psMix[:, 512:1024], rtC[:], krawB[:, 512:1024],
                         start=True, stop=True).then_inc(sem["sRm"], 1)

            for oci, oc in enumerate(chunk_order):
                slot = oci % 3
                _tag, _val = wq_wait[oci]
                t.wait_ge(sWL[slot] if _tag == "L" else sWC[slot], _val)
                for pr in range(2):
                    gp = oci * 2 + pr
                    if oci == 0:
                        t.wait_ge(sem["sDV"], marks["xn_tb"][2 * pr + 1])
                    if gp >= 2:
                        t.wait_ge(sem["sKV"], gp - 1)
                    for tin in range(2):
                        tcol = 1024 * pr + 512 * tin
                        for dc in range(NDC):
                            mm = t.matmul(psQ[gp % 2][:, 512 * tin:512 * tin + 512],
                                          wbuf[slot][:, 128 * dc:128 * dc + 128],
                                          xnA[:, 2048 * dc + tcol:2048 * dc + tcol + 512],
                                          start=(dc == 0), stop=(dc == 15))
                            if tin == 1 and dc == NDC - 1:
                                mm.then_inc(sem["sQm"], 1)
                    if pend_rope:
                        emit_rot(pend_rope.pop(0))
                    if pend_v:
                        emit_vtr(pend_v.pop(0))
                    if gp in rope_idx:
                        pend_rope.append(gp)
                    if gp in vci_of:
                        pend_v.append(gp)
            while pend_rope:
                emit_rot(pend_rope.pop(0))
            while pend_v:
                emit_vtr(pend_v.pop(0))

        tapsA = (1 if "xn" in TAPS else 0)
        if PHASES != "A0":
            tapsA += (1 if "q" in TAPS else 0)

        @block.sync
        def _(sy):
            if "xn" in TAPS:
                sy.wait_ge(sem["sDV"], marks["xn_tb"][3])
                sy.dma_start(out=taps["xn"][:, :], in_=xnA[:]).then_inc(sem["sOD"], 16)
            if "q" in TAPS and PHASES != "A0":
                sy.wait_ge(sem["sKV"], 96)
                sy.dma_start(out=taps["q"][:, :], in_=bufB[:]).then_inc(sem["sOD"], 16)
            bar(sy, 1)
        for engname in ["gpsimd", "scalar", "vector", "tensor"]:
            getattr(block, engname)(lambda e: bar(e, 1))

        actx.close()

        if PHASES in ("A", "A0"):
            @block.sync
            def _(sy):
                sy.wait_ge(sem["sOD"], 16 * tapsA)
            return nc

        bctx = contextlib.ExitStack()
        EB = bctx.enter_context
        kbuf = [EB(nc.sbuf_tensor(f"kbuf{i}", [128, 8192], BF16)) for i in range(2)]
        vone = [EB(nc.sbuf_tensor(f"vone{i}", [128, 64 * 129], BF16)) for i in range(2)]
        ptb = [EB(nc.sbuf_tensor(f"ptb{i}", [128, 512], BF16)) for i in range(8)]
        onb = EB(nc.sbuf_tensor("onb", [128, 512], BF16))
        orawA = EB(nc.sbuf_tensor("orawA", [128, 258], F32))
        orawB = EB(nc.sbuf_tensor("orawB", [128, 258], F32))
        rlb = EB(nc.sbuf_tensor("rlb", [128, 4], F32))
        attst = [EB(nc.sbuf_tensor(f"attst{i}", [128, 2048], BF16)) for i in range(2)]
        psS = [EB(nc.psum_tensor(f"psS{i}", [128, 512], F32)) for i in range(5)]
        psOA = EB(nc.psum_tensor("psOA", [128, 258], F32))
        psOB = EB(nc.psum_tensor("psOB", [128, 258], F32))
        psT = EB(nc.psum_tensor("psT", [128, 512], BF16))

        @block.sync
        def _(sy):
            sy.wait_ge(sem["sKV"], 96)
            sy.wait_ge(sem["sRm"], 64)
            sy.wait_ge(sem["sSQT"], 4)

            def k_load(h):
                slot = h % 2
                sy.wait_ge(sem["sCC"], 2 * (h // 4) + 1)
                if h >= 2:
                    sy.wait_ge(sem["sS"], 256 * (h - 1))
                base = 2048 * (h // 4)
                for r in range(4):
                    sy.dma_start(out=kbuf[slot][:, 2048 * r:2048 * r + 2048],
                                 in_=cc_outK[base + 512 * r + 128 * (h % 4):
                                             base + 512 * r + 128 * (h % 4) + 128, :]
                                 ).then_inc(sKL[slot], 16)

            def v_load(ht):
                slot = ht % 2
                sy.wait_ge(sem["sCC"], 2 * (ht // 4) + 2)
                sy.wait_ge(sem["sVM"], slot + 1)
                if ht >= 2:
                    sy.wait_ge(sem["sPV"], 256 * (ht - 1))
                g2 = ht // 4
                dstv = vone[slot][:].rearrange("p (kt c) -> p kt c", kt=64, c=129)
                sy.dma_start(out=dstv[:, 0:32, 0:128],
                             in_=cc_outV[8192 * g2:8192 * g2 + 4096,
                                         128 * (ht % 4):128 * (ht % 4) + 128]
                             .rearrange("(kt p) d -> p kt d", p=128)
                             ).then_inc(sVL[slot], 16)

            k_load(0)
            v_load(0)
            k_load(1)
            for h in range(H):
                if h + 1 < H:
                    v_load(h + 1)
                if h + 2 < H:
                    k_load(h + 2)
                sy.wait_ge(sem["sPTf"], 4 * (h + 1))
                sy.dma_start(out=attd[128 * h:128 * h + 128, :],
                             in_=attst[h % 2][:]).then_inc(sASs[h % 2], 16)

        @block.scalar
        def _(sc):
            NK = H * NQB * 64
            for k in range(NK):
                sc.wait_ge(sem["sS"], k + 1)
                if k >= 8:
                    sc.wait_ge(sem["sPV"], k - 7)
                sc.activation(out=ptb[k % 8][:], in_=psS[k % 5][:],
                              func=AF.Exp, scale=SCALE).then_inc(sem["sE"], 1)

        @block.tensor
        def _(t):
            NK = H * NQB * 64
            LAG = 8
            kdesc = [(h, qb, lk) for h in range(H) for qb in range(NQB)
                     for lk in range(64)]
            ep_of_k = [h * NQB + qb for (h, qb, lk) in kdesc]
            for ki in range(NK + LAG):
                if ki >= LAG:
                    k = ki - LAG
                    e_h, e_qb, e_lk = kdesc[k]
                    ep = e_h * NQB + e_qb
                    if e_lk == 0 and e_qb == 0:
                        t.wait_ge(sVL[e_h % 2], 16 * (e_h // 2 + 1))
                        t.wait_ge(sVG[e_h % 2], 16 * (e_h // 2 + 1))
                    if e_lk == 0 and ep >= 1:
                        t.wait_ge(sem["sOn"], ep)
                    t.wait_ge(sem["sE"], k + 1)
                    for qt in range(4):
                        po = psOA if qt < 2 else psOB
                        col = 129 * (qt % 2)
                        mm = t.matmul(po[:, col:col + 129],
                                      ptb[k % 8][:, 128 * qt:128 * qt + 128],
                                      vone[e_h % 2][:, 129 * e_lk:129 * e_lk + 129],
                                      start=(e_lk == 0 and qt in (0, 2)),
                                      stop=(e_lk == 63), skip_group_check=True)
                        if qt == 3:
                            mm.then_inc(sem["sPV"], 1)
                    do_tr = None
                    if e_lk == 4 and k >= 64:
                        do_tr = ep_of_k[k] - 1
                    if do_tr is not None:
                        t.wait_ge(sem["sOnb"], do_tr + 1)
                        if do_tr >= 1:
                            t.wait_ge(sem["sPTf"], do_tr)
                        for qt in range(4):
                            tr = t.transpose(psT[:, 128 * qt:128 * qt + 128],
                                             onb[:, 128 * qt:128 * qt + 128], idC[:])
                            if qt == 3:
                                tr.then_inc(sem["sPT"], 1)
                if ki < NK:
                    h, qb, lk = kdesc[ki]
                    if qb == 0 and lk == 0:
                        t.wait_ge(sKL[h % 2], 64 * (h // 2 + 1))
                    if ki >= 5:
                        t.wait_ge(sem["sE"], ki - 4)
                    mm = t.matmul(psS[ki % 5][:],
                                  kbuf[h % 2][:, 128 * lk:128 * lk + 128],
                                  bufB[:, 2048 * h + 512 * qb:2048 * h + 512 * qb + 512],
                                  start=True, stop=True).then_inc(sem["sS"], 1)
            t.wait_ge(sem["sOnb"], NEP)
            t.wait_ge(sem["sPTf"], NEP - 1)
            for qt in range(4):
                tr = t.transpose(psT[:, 128 * qt:128 * qt + 128],
                                 onb[:, 128 * qt:128 * qt + 128], idC[:])
                if qt == 3:
                    tr.then_inc(sem["sPT"], 1)

        @block.vector
        def _(v):
            DV = mkDV(v)
            v.wait_ge(sem["sKV"], 96)
            v.memset(vone[0][:], 1.0).then_inc(sem["sVM"], 1)
            v.memset(vone[1][:], 1.0).then_inc(sem["sVM"], 1)

            for ep in range(NEP):
                h, qb = ep // NQB, ep % NQB
                v.wait_ge(sem["sPV"], 64 * (ep + 1))
                if qb == 0 and h >= 2:
                    v.wait_ge(sASs[h % 2], 16 * (h // 2))
                DV(v.tensor_copy(orawA[:], psOA[:]))
                v.tensor_copy(orawB[:], psOB[:]).then_inc(sem["sOn"], 1)
                v.wait_ge(sem["sDV"], dvn[0])
                v.wait_ge(sem["sOn"], ep + 1)
                DV(v.reciprocal(rlb[:, 0:2],
                                orawA[:].rearrange("p (a c) -> p a c", a=2, c=129)[:, :, 128]))
                DV(v.reciprocal(rlb[:, 2:4],
                                orawB[:].rearrange("p (a c) -> p a c", a=2, c=129)[:, :, 128]))
                v.wait_ge(sem["sDV"], dvn[0])
                for qt in range(4):
                    po = orawA if qt < 2 else orawB
                    col = 129 * (qt % 2)
                    mu = v.tensor_scalar(onb[:, 128 * qt:128 * qt + 128],
                                         po[:, col:col + 128], rlb[:, qt:qt + 1],
                                         None, ALU.mult)
                    if qt == 3:
                        mu.then_inc(sem["sOnb"], 1)
                v.wait_ge(sem["sPT"], ep + 1)
                v.tensor_copy(attst[h % 2][:, 512 * qb:512 * qb + 512],
                              psT[:]).then_inc(sem["sPTf"], 1)

        @block.sync
        def _(sy):
            if "att" in TAPS:
                sy.wait_ge(sASs[0], 128)
                sy.wait_ge(sASs[1], 128)
                sy.dma_start(out=taps["att"][:, :], in_=attd[:, :]).then_inc(sem["sOD"], 16)
            bar(sy, 2)

        @block.gpsimd
        def _(gq):
            for ht in range(H):
                slot = ht % 2
                gq.wait_ge(sem["sCC"], 2 * (ht // 4) + 2)
                gq.wait_ge(sem["sVM"], slot + 1)
                if ht >= 2:
                    gq.wait_ge(sem["sPV"], 256 * (ht - 1))
                g2 = ht // 4
                dstv = vone[slot][:].rearrange("p (kt c) -> p kt c", kt=64, c=129)
                gq.dma_start(out=dstv[:, 32:64, 0:128],
                             in_=cc_outV[8192 * g2 + 4096:8192 * g2 + 8192,
                                         128 * (ht % 4):128 * (ht % 4) + 128]
                             .rearrange("(kt p) d -> p kt d", p=128)
                             ).then_inc(sVG[slot], 16)

        @block.gpsimd
        def _(e):
            e.sem_inc(sem["sPH"], 1)
        for engname in ["scalar", "vector", "tensor"]:
            getattr(block, engname)(lambda e: bar(e, 2))

        bctx.close()

        if PHASES == "AB":
            tapsAB = tapsA + (1 if "att" in TAPS else 0)
            @block.sync
            def _(sy):
                sy.wait_ge(sem["sOD"], 16 * tapsAB)
            return nc

        cctx = contextlib.ExitStack()
        EC = cctx.enter_context
        bufC = EC(nc.sbuf_tensor("bufC", [128, NDC * T], BF16))
        xsqC = [EC(nc.sbuf_tensor(f"xsqC{i}", [128, 512], BF16)) for i in range(2)]
        ostg = [EC(nc.sbuf_tensor(f"ostg{i}", [128, 2048], F32)) for i in range(2)]
        xt1C = [EC(nc.sbuf_tensor("xt1C0", [128, 512], BF16))]
        abc2 = EC(nc.sbuf_tensor("abc2", [128, 512], BF16))
        bbc2 = EC(nc.sbuf_tensor("bbc2", [128, 512], BF16))
        lgb = EC(nc.sbuf_tensor("lgb", [128, 32], F32))
        mxb = EC(nc.sbuf_tensor("mxb", [128, 4], F32))
        mskb = EC(nc.sbuf_tensor("mskb", [128, 32], BF16))
        mskT = EC(nc.sbuf_tensor("mskT", [8, 512], BF16))
        m2b = EC(nc.sbuf_tensor("m2b", [128, 512], BF16))
        hmb = EC(nc.sbuf_tensor("hmb", [128, T], BF16))

        psY = [EC(nc.psum_tensor(f"psY{i}", [128, 512], F32)) for i in range(2)]
        psA2 = EC(nc.psum_tensor("psA2", [128, 512], F32))
        psB2s = EC(nc.psum_tensor("psB2s", [128, 512], F32))
        psMix2 = EC(nc.psum_tensor("psMix2", [128, 1024], F32))
        psMT = EC(nc.psum_tensor("psMT", [8, 512], BF16))
        psEx = EC(nc.psum_tensor("psEx", [128, 512], F32))
        psR = psA2
        psHx = psB2s
        psB2 = psY

        @block.sync
        def _(sy):
            sy.wait_ge(sASs[0], 128)
            sy.wait_ge(sASs[1], 128)
            sy.wait_ge(sem["sPTf"], 64)
            sy.wait_ge(sem["sPT"], 64)
            sy.wait_ge(sem["sE"], 4096)
            for oc in range(16):
                if oc >= 2:
                    sy.wait_ge(sALs[oc % 2], 16 * (oc // 2))
                sy.dma_start(out=bufC[:, 2048 * oc:2048 * oc + 2048],
                             in_=attd[128 * oc:128 * oc + 128, :]).then_inc(sALs[oc % 2], 16)
            for mt in range(16):
                slot = mt % 2
                if mt < 2:
                    sy.wait_ge(sXA[slot], marks["xa_end"][slot])
                else:
                    sy.wait_ge(sem["sYf"], 4 * (mt - 1))
                sy.dma_start(out=xbuf[slot][:], in_=xh[:, mt, :]
                             ).then_inc(sXA[slot], 16)
            for mt in range(16):
                sy.wait_ge(sem["sBf"], 4 * (mt + 1))
                sy.dma_start(out=outT[128 * mt:128 * mt + 128, :],
                             in_=ostg[mt % 2][:]).then_inc(sODs[mt % 2], 16)
            if "x1" in TAPS:
                sy.wait_ge(sem["sYf"], 64)
                sy.dma_start(out=taps["x1"][:, :], in_=bufB[:]).then_inc(sem["sOD"], 16)
            if "xn2" in TAPS:
                sy.wait_ge(sem["sXN2"], 64)
                sy.dma_start(out=taps["xn2"][:, :], in_=bufC[:]).then_inc(sem["sOD"], 16)
            if "hm" in TAPS:
                sy.wait_ge(sem["sHr"], 4)
                sy.dma_start(out=taps["hm"][:, :], in_=hmb[:]).then_inc(sem["sOD"], 16)
            sy.wait_ge(sODs[0], 128)
            sy.wait_ge(sODs[1], 128)
            if n_tap:
                sy.wait_ge(sem["sOD"], 16 * n_tap)

        @block.gpsimd
        def _(gq):
            for i, src in enumerate([wo5[:, mt, :, :] for mt in range(16)]
                                    + [wb5[:, mt, :, :] for mt in range(16)]):
                oci = 48 + i
                slot = oci % 3
                if i < 3:
                    gq.wait_ge(sem["sQm"], 2 * (oci - 2))
                elif i - 3 < 16:
                    gq.wait_ge(sem["sYm"], 4 * (i - 3 + 1))
                else:
                    gq.wait_ge(sem["sBm"], 4 * (i - 3 - 16 + 1))
                gq.dma_start(out=wbuf[slot][:],
                             in_=src.rearrange("p a b -> p (a b)")).then_inc(sWL[slot], 16)

        @block.tensor
        def _(t):
            for mt in range(16):
                oci = 48 + mt
                slot = oci % 3
                _tag, _val = wq_wait[oci]
                t.wait_ge(sWL[slot] if _tag == "L" else sWC[slot], _val)
                for qb in range(NQB):
                    idx = mt * 4 + qb
                    if idx >= 2:
                        t.wait_ge(sem["sYf"], idx - 1)
                    for oc in range(16):
                        if mt == 0 and qb == 0:
                            t.wait_ge(sALs[oc % 2], 16 * (oc // 2 + 1))
                        mm = t.matmul(psY[idx % 2][:],
                                      wbuf[slot][:, 128 * oc:128 * oc + 128],
                                      bufC[:, 2048 * oc + 512 * qb:2048 * oc + 512 * qb + 512],
                                      start=(oc == 0), stop=(oc == 15))
                        if oc == 15:
                            mm.then_inc(sem["sYm"], 1)
            t.wait_ge(sem["sYf"], 64)

            def stats2_tb(tb):
                if tb >= 1:
                    t.wait_ge(sem["sP2F"], 4 + tb)
                for dc in range(NDC):
                    idx = 64 + tb * NDC + dc
                    t.matmul(psA2[0:1, :], onC[:],
                             bufB[:, 2048 * dc + 512 * tb:2048 * dc + 512 * tb + 512],
                             start=(dc == 0), stop=(dc == 15))
                    t.wait_ge(sem["sSQ"], idx + 1)
                    t.matmul(psB2s[0:1, :], onC[:], xsqC[idx % 2][:],
                             start=(dc == 0), stop=(dc == 15)).then_inc(sem["sXS"], 1)

            def bcast2_tb(tb):
                t.wait_ge(sem["sSC"], 4 + tb + 1)
                if tb >= 1:
                    t.wait_ge(sem["sBCe"], 4 + tb)
                t.matmul(psMix2[:, 0:512], onR[:],
                         abR[0:1, 512 * tb:512 * tb + 512], start=True, stop=True)
                t.matmul(psMix2[:, 512:1024], onR[:],
                         bR[0:1, 512 * tb:512 * tb + 512],
                         start=True, stop=True).then_inc(sem["sBC"], 2)

            for tb in range(NTB):
                stats2_tb(tb)
            for tb in range(NTB):
                bcast2_tb(tb)
            for qb in range(NQB):
                t.wait_ge(sem["sXN2"], 16 * (qb + 1))
                if qb == 0:
                    t.wait_ge(sem["sSC"], 8)
                if qb >= 1:
                    t.wait_ge(sem["sLG"], qb)
                for qt in range(4):
                    for dc in range(NDC):
                        mm = t.matmul(psR[:, 8 * qt:8 * qt + 8],
                                      bufC[:, 2048 * dc + 512 * qb + 128 * qt:
                                            2048 * dc + 512 * qb + 128 * qt + 128],
                                      rwb[:, 8 * dc:8 * dc + 8],
                                      start=(dc == 0), stop=(dc == 15))
                        if qt == 3 and dc == NDC - 1:
                            mm.then_inc(sem["sRt"], 1)
                t.wait_ge(sem["sMsk"], qb + 1)
                if qb >= 1:
                    t.wait_ge(sem["sMTf"], qb)
                for qt in range(4):
                    tr = t.transpose(psMT[:, 128 * qt:128 * qt + 128],
                                     mskb[:, 8 * qt:8 * qt + 8], idC[:])
                    if qt == 3:
                        tr.then_inc(sem["sMT"], 1)
                t.wait_ge(sem["sMTf"], qb + 1)
                if qb >= 1:
                    t.wait_ge(sem["sM2"], qb)
                t.matmul(psEx[:], e2C[:], mskT[:], start=True, stop=True
                         ).then_inc(sem["sEm"], 1)
                if qb >= 1:
                    t.wait_ge(sem["sHr"], qb)
                for dc in range(NDC):
                    mm = t.matmul(psHx[:], wab[:, 128 * dc:128 * dc + 128],
                                  bufC[:, 2048 * dc + 512 * qb:2048 * dc + 512 * qb + 512],
                                  start=(dc == 0), stop=(dc == 15))
                    if dc == NDC - 1:
                        mm.then_inc(sem["sHm"], 1)
            for mt in range(16):
                oci = 64 + mt
                slot = oci % 3
                _tag, _val = wq_wait[oci]
                t.wait_ge(sWL[slot] if _tag == "L" else sWC[slot], _val)
                for qb in range(NQB):
                    idx = mt * 4 + qb
                    if idx < 2:
                        t.wait_ge(sem["sYf"], 64)
                    if idx >= 2:
                        t.wait_ge(sem["sBf"], idx - 1)
                    if mt == 0:
                        t.wait_ge(sem["sHr"], qb + 1)
                    for dc in range(NDC):
                        t.matmul(psB2[idx % 2][:],
                                 wbuf[slot][:, 128 * dc:128 * dc + 128],
                                 bufC[:, 2048 * dc + 512 * qb:2048 * dc + 512 * qb + 512],
                                 start=(dc == 0), stop=False)
                    t.matmul(psB2[idx % 2][:], wbwb[:, 128 * mt:128 * mt + 128],
                             hmb[:, 512 * qb:512 * qb + 512],
                             start=False, stop=True).then_inc(sem["sBm"], 1)

        @block.scalar
        def _(sc):
            sc.wait_ge(sem["sYf"], 64)
            for tb in range(NTB):
                for dc in range(NDC):
                    idx = 64 + tb * NDC + dc
                    sc.wait_ge(sem["sXS"], idx - 1)
                    sc.activation(out=xsqC[idx % 2][:],
                                  in_=bufB[:, 2048 * dc + 512 * tb:2048 * dc + 512 * tb + 512],
                                  func=AF.Square).then_inc(sem["sSQ"], 1)
                sc.wait_ge(sem["sVAR"], 4 + tb + 1)
                if tb >= 1:
                    sc.wait_ge(sem["sSC"], 4 + tb)
                sc.activation(out=stq_[:], in_=stv_[:], func=AF.Sqrt,
                              bias=epsC[:]).then_inc(sem["sSQT"], 1)

        @block.vector
        def _(v):
            DV = mkDV(v)
            for mt in range(16):
                for qb in range(NQB):
                    idx = mt * 4 + qb
                    v.wait_ge(sem["sYm"], idx + 1)
                    slot = mt % 2
                    v.wait_ge(sXA[slot], marks["xa_dma"][("C", mt)])
                    v.tensor_tensor(bufB[:, 2048 * mt + 512 * qb:2048 * mt + 512 * qb + 512],
                                    psY[idx % 2][:],
                                    xbuf[slot][:, 512 * qb:512 * qb + 512], ALU.add
                                    ).then_inc(sem["sYf"], 1)
            for tb in range(NTB):
                v.wait_ge(sem["sXS"], 64 + 16 * (tb + 1))
                v.wait_ge(sem["sSC"], 4 + tb)
                DV(v.tensor_scalar(stm_[:], psA2[0:1, :], 1.0 / 2048.0, None, ALU.mult))
                v.wait_ge(sem["sDV"], dvn[0])
                v.tensor_scalar(stq_[:], psB2s[0:1, :], 1.0 / 2048.0, None,
                                ALU.mult).then_inc(sem["sP2F"], 1)
                if tb >= 1:
                    v.wait_ge(sem["sSQT"], 4 + tb)
                DV(v.tensor_tensor(stv_[:], stm_[:], stm_[:], ALU.mult))
                v.wait_ge(sem["sDV"], dvn[0])
                v.wait_ge(sem["sP2F"], 4 + tb + 1)
                v.tensor_tensor(stv_[:], stq_[:], stv_[:], ALU.subtract
                                ).then_inc(sem["sVAR"], 1)
                v.wait_ge(sem["sSQT"], 4 + tb + 1)
                DV(v.reciprocal(stv_[:], stq_[:]))
                v.wait_ge(sem["sDV"], dvn[0])
                DV(v.tensor_copy(abR[0:1, 512 * tb:512 * tb + 512], stv_[:]))
                v.wait_ge(sem["sDV"], dvn[0])
                v.scalar_tensor_tensor(bR[0:1, 512 * tb:512 * tb + 512], stm_[:], -1.0,
                                       abR[0:1, 512 * tb:512 * tb + 512],
                                       ALU.mult, ALU.mult).then_inc(sem["sSC"], 1)
            v.wait_ge(sem["sYm"], 64)
            for tb in range(NTB):
                v.wait_ge(sem["sBC"], 8 + 2 * (tb + 1))
                v.wait_ge(sem["sDV"], dvn[0])
                v.wait_ge(sem["sXN2"], 16 * tb)
                DV(v.tensor_copy(abc2[:], psMix2[:, 0:512]))
                v.tensor_copy(bbc2[:], psMix2[:, 512:1024]).then_inc(sem["sBCe"], 1)
                v.wait_ge(sem["sBCe"], 4 + tb + 1)
                for dc in range(NDC):
                    cnt = tb * NDC + dc
                    t1 = xt1C[0]
                    if cnt >= 1:
                        v.wait_ge(sem["sXN2"], cnt)
                    DV(v.tensor_tensor(t1[:],
                                       bufB[:, 2048 * dc + 512 * tb:2048 * dc + 512 * tb + 512],
                                       abc2[:], ALU.mult))
                    v.wait_ge(sem["sDV"], dvn[0])
                    v.tensor_tensor(bufC[:, 2048 * dc + 512 * tb:2048 * dc + 512 * tb + 512],
                                    t1[:], bbc2[:], ALU.add
                                    ).then_inc(sem["sXN2"], 1)
            for qb in range(NQB):
                v.wait_ge(sem["sRt"], qb + 1)
                v.tensor_copy(lgb[:], psR[:, 0:32]).then_inc(sem["sLG"], 1)
                v.wait_ge(sem["sLG"], qb + 1)
                for qt in range(4):
                    DV(v.reduce_max(mxb[:, qt:qt + 1], lgb[:, 8 * qt:8 * qt + 8], axis=AX.X))
                v.wait_ge(sem["sDV"], dvn[0])
                for qt in range(4):
                    mk = v.tensor_scalar(mskb[:, 8 * qt:8 * qt + 8], lgb[:, 8 * qt:8 * qt + 8],
                                         mxb[:, qt:qt + 1], None, ALU.is_ge)
                    if qt == 3:
                        mk.then_inc(sem["sMsk"], 1)
                v.wait_ge(sem["sMT"], qb + 1)
                v.tensor_copy(mskT[:], psMT[:]).then_inc(sem["sMTf"], 1)
                v.wait_ge(sem["sEm"], qb + 1)
                v.tensor_copy(m2b[:], psEx[:]).then_inc(sem["sM2"], 1)
                v.wait_ge(sem["sHm"], qb + 1)
                v.wait_ge(sem["sM2"], qb + 1)
                v.tensor_tensor(hmb[:, 512 * qb:512 * qb + 512], psHx[:], m2b[:],
                                ALU.mult).then_inc(sem["sHr"], 1)
            for mt in range(16):
                for qb in range(NQB):
                    idx = mt * 4 + qb
                    v.wait_ge(sem["sBm"], idx + 1)
                    if qb == 0 and mt >= 2:
                        v.wait_ge(sODs[mt % 2], 16 * (mt // 2))
                    v.scalar_tensor_tensor(
                        ostg[mt % 2][:, 512 * qb:512 * qb + 512],
                        psB2[idx % 2][:], bbb[:, mt:mt + 1],
                        bufB[:, 2048 * mt + 512 * qb:2048 * mt + 512 * qb + 512],
                        ALU.add, ALU.add).then_inc(sem["sBf"], 1)

        cctx.close()
    return nc


def _host_layout(inputs):
    x = inputs["L0_attn_x"]
    w_qkv = inputs["L0_attn_w_qkv"]
    w_out = inputs["L0_attn_w_out"]
    W_base = inputs["L0_ffn_W_base"]
    b_base = inputs["L0_ffn_b_base"]
    router_W = inputs["L0_ffn_router_W"]
    A = inputs["L0_ffn_A"]
    Bw = inputs["L0_ffn_Bw"]

    wq5 = np.ascontiguousarray(
        np.asarray(w_qkv, np.float32).reshape(48, 128, 16, 128).transpose(3, 0, 2, 1))
    wo5 = np.ascontiguousarray(
        np.asarray(w_out, np.float32).reshape(16, 128, 16, 128).transpose(3, 0, 2, 1))
    wb5 = np.ascontiguousarray(
        np.asarray(W_base, np.float32).reshape(16, 128, 16, 128).transpose(3, 0, 2, 1))
    wa5 = np.ascontiguousarray(
        np.asarray(A, np.float32).reshape(128, 16, 128).transpose(2, 1, 0))
    wbwp = np.ascontiguousarray(
        np.transpose(np.asarray(Bw, np.float32), (0, 2, 1)).reshape(128, 2048))
    rw5 = np.ascontiguousarray(
        np.asarray(router_W, np.float32).T.reshape(16, 128, 8).transpose(1, 0, 2))
    bb = np.ascontiguousarray(np.asarray(b_base, np.float32).reshape(16, 128).T)

    inv_freq = (1.0 / np.float32(10000.0) **
                (np.arange(0, 128, 2, dtype=np.float32) / np.float32(128.0))).astype(np.float32)
    maps = []
    for c in range(8):
        b = c // 4
        j = c % 4
        ts0 = T * j
        xc = np.asarray(x[b, ts0:ts0 + T, :], np.float32)
        xhc = np.ascontiguousarray(xc.T.reshape(16, 128, T).transpose(1, 0, 2))
        tloc = np.arange(ts0, ts0 + T, dtype=np.float32)
        ang = np.concatenate([np.outer(tloc, inv_freq)] * 2, axis=1).astype(np.float32)
        cosTc = np.ascontiguousarray(np.cos(ang).T.astype(np.float32))
        sinTc = np.ascontiguousarray(np.sin(ang).T.astype(np.float32))
        maps.append({"xh": xhc, "wq5": wq5, "wo5": wo5, "wb5": wb5, "wa5": wa5,
                     "wbw": wbwp, "rw5": rw5, "bbi": bb, "cosi": cosTc, "sini": sinTc})
    return maps


def kernel(**inputs):
    maps = _host_layout(inputs)
    key = (frozenset(TAPS), PHASES)
    if key not in _NC_CACHE:
        _NC_CACHE[key] = build_program()
    nc = _NC_CACHE[key]
    res = run_bass_kernel_spmd(nc, maps, core_ids=list(range(8)))
    out = np.empty((B, S, D), np.float32)
    for c in range(8):
        b = c // 4
        j = c % 4
        out[b, T * j:T * j + T, :] = res.results[c]["outT"].T
    if TAPS:
        kernel.last_results = res.results
    return out
